# version 01
# baseline (speedup 1.0000x reference)
import math
import numpy as np
import ml_dtypes
import concourse.bass as bass
import concourse.mybir as mybir
from concourse.bass_utils import run_bass_kernel_spmd
from contextlib import ExitStack

F32 = mybir.dt.float32
BF16 = mybir.dt.bfloat16
AF = mybir.ActivationFunctionType
ALU = mybir.AluOpType
AX = mybir.AxisListType
BF = ml_dtypes.bfloat16

B, T, D = 4, 4096, 1024
TH = T // 2
NCORES = 8


QUEUES = ("tensor", "vector", "scalar", "gpsimd", "sync")


class _Op:
    __slots__ = ("queue", "fn", "kind", "deps", "done", "need_inc", "pos",
                 "tag", "wkey")

    def __init__(self, queue, fn, kind, tag=""):
        self.queue = queue
        self.fn = fn
        self.kind = kind
        self.deps = set()
        self.done = None
        self.need_inc = kind in ("dma", "cc")
        self.pos = None
        self.tag = tag


class Em:
    def __init__(self, nc):
        self.nc = nc
        self.es = ExitStack()
        self._cms = {}
        self._guard = set()
        self._guard_pending = False
        self._guarded_new = set()
        self.qops = {q: [] for q in QUEUES}
        self.all_ops = []
        self.last_writer = {}
        self.readers = {}
        self.sbuf_bytes = 0
        self._names = {}

    def _uname(self, name):
        n = self._names.get(name, 0)
        self._names[name] = n + 1
        return name if n == 0 else f"{name}_{n}"

    def sbuf(self, name, shape, dtype):
        cm = self.nc.sbuf_tensor(self._uname("s_" + name), list(shape), dtype)
        h = cm.__enter__()
        self._cms[self._key(h)] = cm
        self.sbuf_bytes += int(np.prod(shape[1:])) * (2 if dtype == BF16 else 4)
        return h

    def free(self, *handles):
        for h in handles:
            k = self._key(h)
            cm = self._cms.pop(k)
            guard = set(self.readers.get(k, ()))
            w = self.last_writer.get(k)
            if w is not None:
                guard.add(w)
            self._guard |= guard
            cm.__exit__(None, None, None)
        self._guarded_new = set(self._cms.keys())

    def psum(self, name, shape=(128, 512), dtype=F32):
        return self.es.enter_context(
            self.nc.psum_tensor(self._uname("p_" + name), list(shape), dtype))

    def sem(self, name):
        return self.es.enter_context(self.nc.semaphore(name))

    def op(self, queue, fn, reads=(), writes=(), tag=""):
        o = _Op(queue, fn, "compute", tag)
        self._record(o, reads, writes)
        return o

    def dma(self, queue, out, in_, reads=(), writes=(), tag="dma", **kw):
        fn = lambda e: e.dma_start(out=out, in_=in_, **kw)
        o = _Op(queue, fn, "dma", tag)
        o.wkey = self._key(writes[0]) if writes else ("dmaout", tag)
        self._record(o, reads, writes)
        return o

    def cc(self, kind, op_, groups, ins, outs, reads=(), writes=(), tag="cc"):
        def fn(e):
            return e.collective_compute(kind, op_, replica_groups=groups,
                                        ins=ins, outs=outs)
        o = _Op("gpsimd", fn, "cc", tag)
        o.wkey = self._key(writes[0]) if writes else ("ccout", tag)
        self._record(o, reads, writes)
        return o

    def _key(self, r):
        return r if isinstance(r, (str, tuple)) else r.name

    def _record(self, o, reads, writes):
        o.pos = len(self.all_ops)
        if self._guard:
            for r in writes:
                k = self._key(r)
                if k in self._cms and k not in self._guarded_new:
                    self._guarded_new.add(k)
                    o.deps |= self._guard
        for r in reads:
            k = self._key(r)
            w = self.last_writer.get(k)
            if w is not None:
                o.deps.add(w)
            self.readers.setdefault(k, []).append(o)
        for r in writes:
            k = self._key(r)
            w = self.last_writer.get(k)
            if w is not None:
                o.deps.add(w)
            for rd in self.readers.get(k, ()):
                if rd is not o:
                    o.deps.add(rd)
            self.readers[k] = []
            self.last_writer[k] = o
        self.all_ops.append(o)
        self.qops[o.queue].append(o)

    def finalize(self):
        nc = self.nc
        for o in self.all_ops:
            for d in o.deps:
                d.need_inc = True
        csem = {q: self.sem(f"c_{q}") for q in QUEUES}
        ccount = {q: 0 for q in QUEUES}
        dsem_map = {}
        dcnt_map = {}
        nsem = 0
        for o in self.all_ops:
            if o.kind == "dma" or o.kind == "cc":
                k = o.wkey
                if k not in dsem_map:
                    dsem_map[k] = self.sem(f"ds{nsem}")
                    dcnt_map[k] = 0
                    nsem += 1
                inc = 16 if o.kind == "dma" else 1
                dcnt_map[k] += inc
                o.done = (dsem_map[k], dcnt_map[k], inc)
        for q in QUEUES:
            for o in self.qops[q]:
                if o.kind == "compute" and o.need_inc:
                    ccount[q] += 1
                    o.done = (csem[q], ccount[q], 1)
        waits = {}
        for o in self.all_ops:
            w = {}
            for d in o.deps:
                sem, val, _ = d.done
                if w.get(id(sem), (None, 0))[1] < val:
                    w[id(sem)] = (sem, val)
            waits[id(o)] = list(w.values())
        final_outs = [o for o in self.all_ops if o.kind == "dma" and o.tag == "out"]
        with nc.Block() as block:
            for q in QUEUES:
                ops = self.qops[q]
                if not ops and q != "gpsimd":
                    continue
                sect = getattr(block, q)

                def make(ops=ops, q=q):
                    def _(eng):
                        mark = {}
                        for o in ops:
                            for sem, val in waits[id(o)]:
                                if mark.get(id(sem), 0) < val:
                                    eng.wait_ge(sem, val)
                                    mark[id(sem)] = val
                            instr = o.fn(eng)
                            if o.done is not None:
                                sem, _v, inc = o.done
                                instr.then_inc(sem, inc)
                        if q == "gpsimd":
                            for o in final_outs:
                                sem, val, _ = o.done
                                if mark.get(id(sem), 0) < val:
                                    eng.wait_ge(sem, val)
                                    mark[id(sem)] = val
                    return _

                sect(make())

    def close(self):
        self.es.close()



class Ctx:

    def __init__(self, em, ident_in, nbanks=8):
        self.em = em
        self.pp = [em.psum(f"pp{i}") for i in range(nbanks)]
        self.ident = em.sbuf("ident", (128, 128), BF16)
        self.ones_col = em.sbuf("ones_col", (1, 128), F32)
        em.op("gpsimd", lambda g: g.memset(self.ones_col[:, :], 1.0),
              writes=[self.ones_col], tag="memset_ones")
        self.ones_red = em.sbuf("ones_red", (128, 1), BF16)
        em.op("gpsimd", lambda g: g.memset(self.ones_red[:, :], 1.0),
              writes=[self.ones_red], tag="memset_onesr")
        em.dma("sync", self.ident[:, :], ident_in, writes=[self.ident],
               tag="ld_ident")


def ln_scratch(em, name="lns"):
    return {"sq": [em.sbuf(f"{name}_sq", (128, 512), BF16) for _ in range(2)],
            "xb": [em.sbuf(f"{name}_xb", (128, 512), BF16) for _ in range(2)],
            "rows": em.sbuf(f"{name}_rows", (1, 3 * 512), F32),
            "tmp": em.sbuf(f"{name}_tmp", (128, 512), F32)}


def ln_feature_major(em, cx, x_tiles, t_len, out_tiles, bank_a, bank_b,
                     eps=1e-5, gain_bias=None, name="ln", x_off=0, o_off=0,
                     scratch=None, x_offs=None):
    nd = len(x_tiles)
    Dm = nd * 128
    if x_offs is None:
        x_offs = [x_off] * nd
    if scratch is None:
        scratch = ln_scratch(em, name)
    sq = scratch["sq"]
    xb = scratch["xb"]
    rows = scratch["rows"]
    stats = rows[0:1, 0:512]
    stats2 = rows[0:1, 512:1024]
    musq = rows[0:1, 1024:1536]
    mu, rstd, mur = stats, stats2, musq
    tmp = scratch["tmp"]
    ps, ps2 = cx.pp[bank_a], cx.pp[bank_b]
    R = rows
    for c0 in range(0, t_len, 512):
        cl = min(512, t_len - c0)
        for i, xt in enumerate(x_tiles):
            xs, xe = x_offs[i] + c0, x_offs[i] + c0 + cl
            sqi, xbi = sq[i % 2], xb[i % 2]
            em.op("scalar", lambda s, xt=xt, xs=xs, xe=xe, cl=cl, xbi=xbi:
                  s.activation(out=xbi[:, :cl], in_=xt[:, xs:xe], func=AF.Copy),
                  reads=[xt], writes=[xbi], tag=f"{name}_cp")
            em.op("scalar", lambda s, xt=xt, xs=xs, xe=xe, cl=cl, sqi=sqi:
                  s.activation(out=sqi[:, :cl], in_=xt[:, xs:xe], func=AF.Square),
                  reads=[xt], writes=[sqi], tag=f"{name}_sq")
            em.op("tensor", lambda t, i=i, cl=cl, xbi=xbi: t.matmul(
                out=ps[0:1, :cl], lhsT=cx.ones_red[:, :], rhs=xbi[:, :cl],
                start=(i == 0), stop=(i == nd - 1)),
                reads=[xbi, cx.ones_red], writes=[ps], tag=f"{name}_mm1")
            em.op("tensor", lambda t, i=i, cl=cl, sqi=sqi: t.matmul(
                out=ps2[0:1, :cl], lhsT=cx.ones_red[:, :], rhs=sqi[:, :cl],
                start=(i == 0), stop=(i == nd - 1)),
                reads=[sqi, cx.ones_red], writes=[ps2], tag=f"{name}_mm2")
        em.op("vector", lambda v, cl=cl: v.tensor_copy(
            out=stats[:, :cl], in_=ps[0:1, :cl]),
            reads=[ps], writes=[R], tag=f"{name}_ev1")
        em.op("vector", lambda v, cl=cl: v.tensor_copy(
            out=stats2[:, :cl], in_=ps2[0:1, :cl]),
            reads=[ps2], writes=[R], tag=f"{name}_ev2")
        em.op("scalar", lambda s, cl=cl: s.activation(
            out=mu[:, :cl], in_=stats[:, :cl], func=AF.Copy, scale=1.0 / Dm),
            reads=[R], writes=[R], tag=f"{name}_mu")
        em.op("scalar", lambda s, cl=cl: s.activation(
            out=musq[:, :cl], in_=mu[:, :cl], func=AF.Square),
            reads=[R], writes=[R], tag=f"{name}_musq")
        em.op("vector", lambda v, cl=cl: v.tensor_scalar(
            out=stats2[:, :cl], in0=stats2[:, :cl], scalar1=1.0 / Dm,
            scalar2=eps, op0=ALU.mult, op1=ALU.add),
            reads=[R], writes=[R], tag=f"{name}_var")
        em.op("vector", lambda v, cl=cl: v.tensor_tensor(
            out=stats2[:, :cl], in0=stats2[:, :cl], in1=musq[:, :cl],
            op=ALU.subtract),
            reads=[R], writes=[R], tag=f"{name}_var2")
        em.op("scalar", lambda s, cl=cl: s.activation(
            out=stats2[:, :cl], in_=stats2[:, :cl], func=AF.Ln),
            reads=[R], writes=[R], tag=f"{name}_lnv")
        em.op("scalar", lambda s, cl=cl: s.activation(
            out=rstd[:, :cl], in_=stats2[:, :cl], func=AF.Exp, scale=-0.5),
            reads=[R], writes=[R], tag=f"{name}_rstd")
        em.op("vector", lambda v, cl=cl: v.tensor_tensor(
            out=mur[:, :cl], in0=mu[:, :cl], in1=rstd[:, :cl], op=ALU.mult),
            reads=[R], writes=[R], tag=f"{name}_mur")
        em.op("tensor", lambda t, cl=cl: t.matmul(
            out=ps[:, :cl], lhsT=cx.ones_col[:, :], rhs=rstd[:, :cl],
            start=True, stop=True),
            reads=[R, cx.ones_col], writes=[ps], tag=f"{name}_bc1")
        em.op("tensor", lambda t, cl=cl: t.matmul(
            out=ps2[:, :cl], lhsT=cx.ones_col[:, :], rhs=mur[:, :cl],
            start=True, stop=True),
            reads=[R, cx.ones_col], writes=[ps2], tag=f"{name}_bc2")
        for i, xt in enumerate(x_tiles):
            ot = out_tiles[i]
            xs, xe = x_offs[i] + c0, x_offs[i] + c0 + cl
            os_, oe = o_off + c0, o_off + c0 + cl
            em.op("vector", lambda v, xt=xt, xs=xs, xe=xe, cl=cl: v.tensor_tensor(
                out=tmp[:, :cl], in0=xt[:, xs:xe], in1=ps[:, :cl], op=ALU.mult),
                reads=[xt, ps], writes=[tmp], tag=f"{name}_n1")
            if gain_bias is None:
                em.op("vector", lambda v, ot=ot, os_=os_, oe=oe, cl=cl:
                      v.tensor_tensor(out=ot[:, os_:oe], in0=tmp[:, :cl],
                                      in1=ps2[:, :cl], op=ALU.subtract),
                      reads=[tmp, ps2], writes=[ot], tag=f"{name}_n2")
            else:
                g_col, b_col = gain_bias[i]
                em.op("vector", lambda v, cl=cl: v.tensor_tensor(
                    out=tmp[:, :cl], in0=tmp[:, :cl], in1=ps2[:, :cl],
                    op=ALU.subtract),
                    reads=[tmp, ps2], writes=[tmp], tag=f"{name}_n2")
                em.op("vector", lambda v, ot=ot, os_=os_, oe=oe, cl=cl,
                      g_col=g_col, b_col=b_col: v.tensor_scalar(
                          out=ot[:, os_:oe], in0=tmp[:, :cl],
                          scalar1=g_col[:, 0:1], scalar2=b_col[:, 0:1],
                          op0=ALU.mult, op1=ALU.add),
                      reads=[tmp, g_col, b_col, R], writes=[ot], tag=f"{name}_gb")



def build_rail(final_output):
    nc = bass.Bass("TRN2", target_bir_lowering=False)
    em = Em(nc)
    xin = nc.dram_tensor("xT", [D, TH], F32, kind="ExternalInput").ap()
    ident_in = nc.dram_tensor("ident", [128, 128], BF16, kind="ExternalInput").ap()
    bd1 = nc.dram_tensor("bd1", [128, 32 * 128], F32, kind="ExternalInput").ap()
    bd2 = nc.dram_tensor("bd2", [128, 128 * 128], F32, kind="ExternalInput").ap()
    wcT = nc.dram_tensor("wcT", [16384, D], F32, kind="ExternalInput").ap()
    xout = nc.dram_tensor("xoT", [D, TH], F32, kind="ExternalOutput").ap()
    if final_output:
        xout_tm = nc.dram_tensor("xo", [TH, D], F32, kind="ExternalOutput").ap()

    cx = Ctx(em, ident_in)
    xT = [em.sbuf(f"xT{i}", (128, TH), F32) for i in range(8)]
    xn = [em.sbuf(f"xn{i}", (128, TH), BF16) for i in range(8)]
    for i in range(8):
        em.dma("sync", xT[i][:, :], xin[128 * i:128 * (i + 1), :],
               writes=[xT[i]], tag="ld_x")
    ln_feature_major(em, cx, xT, TH, xn, 6, 7, name="ln")

    stage_f = em.sbuf("stage_f", (128, 4096), F32)
    bd1_s = em.sbuf("bd1s", (128, 32 * 128), BF16)
    bd2_s = em.sbuf("bd2s", (128, 128 * 128), BF16)
    em.dma("sync", stage_f[:, :], bd1, writes=[stage_f], tag="ld_bd1")
    em.op("vector", lambda v: v.tensor_copy(out=bd1_s[:, :], in_=stage_f[:, :]),
          reads=[stage_f], writes=[bd1_s], tag="cast_bd1")
    for q4 in range(4):
        em.dma("sync", stage_f[:, :], bd2[:, 4096 * q4:4096 * (q4 + 1)],
               writes=[stage_f], tag="ld_bd2")
        em.op("vector", lambda v, q4=q4: v.tensor_copy(
            out=bd2_s[:, 4096 * q4:4096 * (q4 + 1)], in_=stage_f[:, :]),
            reads=[stage_f], writes=[bd2_s], tag="cast_bd2")

    NWG = 16
    wg_s = [em.sbuf(f"wg{i}", (128, D), BF16) for i in range(NWG)]
    wg_f = [stage_f[:, 0:1024], stage_f[:, 1024:2048]]
    wg_fk = ["stage_f", "stage_f"]
    ytile = [em.sbuf("ytile", (128, 256), BF16) for _ in range(2)]
    ztile = [em.sbuf("ztile", (128, 256), BF16) for _ in range(2)]
    TC = 256
    for kg in range(8):
        for j in range(NWG):
            kt = NWG * kg + j
            wf = wg_f[j % 2]
            em.dma("sync", wf, wcT[128 * kt:128 * (kt + 1), :],
                   writes=["stage_f"], tag="ld_wc")
            em.op("scalar", lambda s, j=j, wf=wf: s.activation(
                out=wg_s[j][:, :], in_=wf, func=AF.Copy),
                reads=["stage_f"], writes=[wg_s[j]], tag="cast_wc")
        for c0 in range(0, TH, TC):
            for j in range(NWG):
                kt = NWG * kg + j
                hg, m = kt // 4, kt % 4
                yt = ytile[(kt // 4) % 2]
                zt = ztile[j % 2]
                if m == 0:
                    em.op("tensor", lambda t, hg=hg, c0=c0: t.matmul(
                        out=cx.pp[4][:, :TC], lhsT=bd1_s[:, hg * 128:(hg + 1) * 128],
                        rhs=xn[hg % 8][:, c0:c0 + TC], start=True, stop=True),
                        reads=[bd1_s, xn[hg % 8]], writes=[cx.pp[4]], tag="mm_y")
                    em.op("vector", lambda v, yt=yt: v.tensor_copy(
                        out=yt[:, :], in_=cx.pp[4][:, :TC]),
                        reads=[cx.pp[4]], writes=[yt], tag="ev_y")
                em.op("tensor", lambda t, kt=kt, yt=yt: t.matmul(
                    out=cx.pp[5][:, :TC],
                    lhsT=bd2_s[:, kt * 128:(kt + 1) * 128],
                    rhs=yt[:, :], start=True, stop=True),
                    reads=[bd2_s, yt], writes=[cx.pp[5]], tag="mm_z")
                em.op("scalar", lambda s, zt=zt: s.activation(
                    out=zt[:, :], in_=cx.pp[5][:, :TC], func=AF.Gelu),
                    reads=[cx.pp[5]], writes=[zt], tag="gelu")
                for o in range(8):
                    po = cx.pp[o // 2]
                    oc = (o % 2) * TC
                    em.op("tensor", lambda t, j=j, o=o, po=po, oc=oc, zt=zt: t.matmul(
                        out=po[:, oc:oc + TC], lhsT=wg_s[j][:, 128 * o:128 * (o + 1)],
                        rhs=zt[:, :], start=(j == 0), stop=(j == NWG - 1)),
                        reads=[wg_s[j], zt], writes=[po], tag="mm_o")
            scale = 0.5
            for o in range(8):
                po = cx.pp[o // 2]
                oc = (o % 2) * TC
                em.op("vector", lambda v, o=o, po=po, oc=oc, c0=c0: v.scalar_tensor_tensor(
                    out=xT[o][:, c0:c0 + TC], in0=po[:, oc:oc + TC], scalar=scale,
                    in1=xT[o][:, c0:c0 + TC], op0=ALU.mult, op1=ALU.add),
                    reads=[po, xT[o]], writes=[xT[o]], tag="resid")
    for i in range(8):
        em.dma("gpsimd", xout[128 * i:128 * (i + 1), :], xT[i][:, :],
               reads=[xT[i]], tag="out")
    if final_output:
        identf = em.sbuf("identf", (128, 128), F32)
        ttile = [em.sbuf("ttile", (128, 128), F32) for _ in range(2)]
        em.op("vector", lambda v: v.tensor_copy(out=identf[:, :],
                                                in_=cx.ident[:, :]),
              reads=[cx.ident], writes=[identf], tag="identf")
        for i in range(8):
            for c0 in range(0, TH, 128):
                tt = ttile[(c0 // 128) % 2]
                em.op("tensor", lambda t, i=i, c0=c0: t.transpose(
                    out=cx.pp[6][:, :128], in_=xT[i][:, c0:c0 + 128],
                    identity=identf[:, :]),
                    reads=[xT[i], identf], writes=[cx.pp[6]], tag="tr")
                em.op("vector", lambda v, tt=tt: v.tensor_copy(
                    out=tt[:, :], in_=cx.pp[6][:, :128]),
                    reads=[cx.pp[6]], writes=[tt], tag="tr_ev")
                em.dma("gpsimd", xout_tm[c0:c0 + 128, 128 * i:128 * (i + 1)],
                       tt[:, :], reads=[tt], tag="out")
    em.finalize()
    em.close()
    return nc



def build_alibi_th(th):
    t0 = th * TH
    hi_default = (th == 0)
    nc = bass.Bass("TRN2", target_bir_lowering=False)
    em = Em(nc)
    xin = nc.dram_tensor("xTf", [D, T], F32, kind="ExternalInput").ap()
    ident_in = nc.dram_tensor("ident", [128, 128], BF16, kind="ExternalInput").ap()
    qkvT = nc.dram_tensor("qkvT", [D, 3 * D], F32, kind="ExternalInput").ap()
    qkvb = nc.dram_tensor("qkvb", [128, 24], F32, kind="ExternalInput").ap()
    outwT = nc.dram_tensor("outwT", [D, D], F32, kind="ExternalInput").ap()
    outb = nc.dram_tensor("outb", [128, 8], F32, kind="ExternalInput").ap()
    kxl = nc.dram_tensor("kxl", [16, 5, T], BF16, kind="ExternalInput").ap()
    kxn = nc.dram_tensor("kxn", [16, 5, T], BF16, kind="ExternalInput").ap()
    qx = nc.dram_tensor("qx", [5, T], BF16, kind="ExternalInput").ap()
    fcor = nc.dram_tensor("fcor", [16, 128, 4 * 512], F32, kind="ExternalInput").ap()
    xout = nc.dram_tensor("xoT", [D, TH], F32, kind="ExternalOutput").ap()
    qhb = nc.dram_tensor("qhb", [16, 64, TH], BF16).ap()
    khb = nc.dram_tensor("khb", [16, 64, T], BF16).ap()
    vhb = nc.dram_tensor("vhb", [32, 128, 8, 130], BF16).ap()

    cx = Ctx(em, ident_in, nbanks=7)
    ptb = em.psum("ptb", (128, 1024), BF16)
    lns = ln_scratch(em)
    stage = em.sbuf("stage", (128, 3072), F32)
    qkvT_s = [em.sbuf(f"qkvT{k}", (128, 3 * D), BF16) for k in range(8)]
    outwT_s = [em.sbuf(f"outwT{k}", (128, D), BF16) for k in range(8)]
    for k in range(8):
        em.dma("sync", stage[:, :], qkvT[128 * k:128 * (k + 1), :],
               writes=[stage], tag="ld_wq")
        em.op("vector", lambda v, k=k: v.tensor_copy(out=qkvT_s[k][:, :],
                                                     in_=stage[:, :]),
              reads=[stage], writes=[qkvT_s[k]], tag="cast_wq")
    for k in range(8):
        em.dma("sync", stage[:, 0:D], outwT[128 * k:128 * (k + 1), :],
               writes=[stage], tag="ld_wo")
        em.op("vector", lambda v, k=k: v.tensor_copy(out=outwT_s[k][:, :],
                                                     in_=stage[:, 0:D]),
              reads=[stage], writes=[outwT_s[k]], tag="cast_wo")
    qb_s = em.sbuf("qb", (128, 24), F32)
    qbq_s = em.sbuf("qbq", (128, 8), F32)
    ob_s = em.sbuf("ob", (128, 8), F32)
    em.dma("sync", qb_s[:, :], qkvb, writes=[qb_s], tag="ld_qb")
    em.dma("sync", ob_s[:, :], outb, writes=[ob_s], tag="ld_ob")
    em.op("vector", lambda v: v.tensor_scalar(out=qbq_s[:, :], in0=qb_s[:, 0:8],
                                              scalar1=0.125, scalar2=0.0,
                                              op0=ALU.mult, op1=ALU.add),
          reads=[qb_s], writes=[qbq_s], tag="qb_scale")

    xc = [em.sbuf(f"xc{i}", (128, 512), F32) for i in range(8)]
    xn = [em.sbuf(f"xnc{i}", (128, 512), BF16) for i in range(8)]
    qkvc = [em.sbuf(f"qkvc{i}", (128, 512), BF16) for i in range(2)]
    vstg = [em.sbuf(f"vstg{i}", (128, 130), BF16) for i in range(2)]
    for i in range(2):
        em.op("gpsimd", lambda g, i=i: g.memset(vstg[i][:, 64:65], 1.0),
              writes=[("vset", i)], tag="ones_v")
        em.op("gpsimd", lambda g, i=i: g.memset(vstg[i][:, 129:130], 1.0),
              writes=[("vset", i)], tag="ones_v")
    for c in range(8):
        c0 = 512 * c
        own = (c0 >= t0) and (c0 < t0 + TH)
        for i in range(8):
            em.dma("sync", xc[i][:, :], xin[128 * i:128 * (i + 1), c0:c0 + 512],
                   writes=[xc[i]], tag="ld_xc")
        ln_feature_major(em, cx, xc, 512, xn, 5, 6, name=f"lnA{c}", scratch=lns)
        for ot in range(24):
            kind = ot // 8
            if kind == 0 and not own:
                continue
            ps = cx.pp[ot % 2]
            for k in range(8):
                em.op("tensor", lambda t, ot=ot, k=k, ps=ps: t.matmul(
                    out=ps[:, :], lhsT=qkvT_s[k][:, 128 * ot:128 * (ot + 1)],
                    rhs=xn[k][:, :], start=(k == 0), stop=(k == 7)),
                    reads=[qkvT_s[k], xn[k]], writes=[ps], tag="mm_qkv")
            qv = qkvc[ot % 2]
            if kind == 0:
                em.op("vector", lambda v, ot=ot, ps=ps, qv=qv: v.tensor_scalar(
                    out=qv[:, :], in0=ps[:, :], scalar1=0.125,
                    scalar2=qbq_s[:, ot:ot + 1], op0=ALU.mult, op1=ALU.add),
                    reads=[ps, qbq_s], writes=[qv], tag="ev_q")
                loc = c0 - t0
                em.dma("gpsimd", qhb[2 * ot, :, loc:loc + 512], qv[0:64, :],
                       reads=[qv], writes=["qhb"], tag="sp_q")
                em.dma("gpsimd", qhb[2 * ot + 1, :, loc:loc + 512], qv[64:128, :],
                       reads=[qv], writes=["qhb"], tag="sp_q")
            elif kind == 1:
                em.op("vector", lambda v, ot=ot, ps=ps, qv=qv: v.tensor_scalar(
                    out=qv[:, :], in0=ps[:, :], scalar1=qb_s[:, ot:ot + 1],
                    scalar2=0.0, op0=ALU.add, op1=ALU.add),
                    reads=[ps, qb_s], writes=[qv], tag="ev_k")
                hh = 2 * (ot - 8)
                em.dma("gpsimd", khb[hh, :, c0:c0 + 512], qv[0:64, :],
                       reads=[qv], writes=["khb"], tag="sp_k")
                em.dma("gpsimd", khb[hh + 1, :, c0:c0 + 512], qv[64:128, :],
                       reads=[qv], writes=["khb"], tag="sp_k")
            else:
                em.op("vector", lambda v, ot=ot, ps=ps, qv=qv: v.tensor_scalar(
                    out=qv[:, :], in0=ps[:, :], scalar1=qb_s[:, ot:ot + 1],
                    scalar2=0.0, op0=ALU.add, op1=ALU.add),
                    reads=[ps, qb_s], writes=[qv], tag="ev_v")
                hp = ot - 16
                for s4 in range(4):
                    jt = 4 * c + s4
                    pth = 512 * (s4 % 2)
                    em.op("tensor", lambda t, qv=qv, s4=s4, pth=pth: t.transpose(
                        out=ptb[:, pth:pth + 128],
                        in_=qv[:, 128 * s4:128 * (s4 + 1)],
                        identity=cx.ident[:, :]),
                        reads=[qv, cx.ident], writes=[ptb], tag="tr_v")
                    vs = vstg[s4 % 2]
                    em.op("vector", lambda v, pth=pth, vs=vs: v.tensor_copy(
                        out=vs[:, 0:64], in_=ptb[:, pth:pth + 64]),
                        reads=[ptb], writes=[vs], tag="ev_v1")
                    em.op("vector", lambda v, pth=pth, vs=vs: v.tensor_copy(
                        out=vs[:, 65:129], in_=ptb[:, pth + 64:pth + 128]),
                        reads=[ptb], writes=[vs], tag="ev_v2")
                    em.dma("gpsimd", vhb[jt, :, hp, :], vs[:, :],
                           reads=[vs, ("vset", 0), ("vset", 1)], writes=["vhb"],
                           tag="sp_v")

    qhat = [em.sbuf(f"qhat{i}", (69, 512), BF16) for i in range(4)]
    khat8 = [em.sbuf(f"khat8{i}", (69, 512), BF16) for i in range(2)]
    kneg8 = [em.sbuf(f"kneg8{i}", (69, 512), BF16) for i in range(2)]
    vt4 = [em.sbuf(f"vt4{i}", (128, 4 * 130), BF16) for i in range(2)]
    att = [em.sbuf(f"att{i}", (128, 512), BF16) for i in range(4)]
    fc_s = [em.sbuf(f"fc{i}", (128, 4 * 512), F32) for i in range(2)]
    aoT = [em.sbuf(f"aoT{i}", (128, TH), BF16) for i in range(8)]
    rrow = em.sbuf("rrow", (1, 512), F32)
    rb = em.sbuf("rb", (64, 512), F32)
    nb8 = em.sbuf("nb8", (128, 1), F32)
    em.op("gpsimd", lambda g: g.memset(nb8[:, :], -8.0), writes=[nb8],
          tag="set_nb8")
    PSO = [0, 1, 4, 5]
    for h in range(16):
        hp, hs = h // 2, h % 2
        fcs = fc_s[h % 2]
        em.dma("sync", fcs[:, :], fcor[h], writes=[fcs], tag="ld_fc")
        for qc in range(4):
            qh = qhat[qc]
            em.dma("sync", qh[0:64, :], qhb[h, :, 512 * qc:512 * (qc + 1)],
                   reads=["qhb"], writes=[qh], tag="ld_qh")
            em.dma("sync", qh[64:69, :], qx[:, t0 + 512 * qc:t0 + 512 * (qc + 1)],
                   writes=[qh], tag="ld_qx")
        for jb in range(8):
            ka = khat8[jb % 2]
            kn = kneg8[jb % 2]
            vt = vt4[jb % 2]
            jB = 512 * jb
            em.dma("sync", ka[0:64, :], khb[h, :, jB:jB + 512],
                   reads=["khb"], writes=[ka], tag="ld_kh")
            em.dma("sync", ka[64:69, :], kxl[h, :, jB:jB + 512],
                   writes=[ka], tag="ld_kx")
            em.dma("sync", kn[64:69, :], kxn[h, :, jB:jB + 512],
                   writes=[kn], tag="ld_kn")
            em.dma("sync", bass.AP(vt, 0, [[520, 128], [130, 4], [1, 130]]),
                   vhb[4 * jb:4 * jb + 4, :, hp, :].rearrange("j p v -> p j v"),
                   reads=["vhb"], writes=[vt], tag="ld_vt")
            for s4 in range(4):
                jt = 4 * jb + s4
                j0 = 128 * jt
                for qc in range(4):
                    q0 = t0 + 512 * qc
                    qh = qhat[qc]
                    crossing = (j0 + 128 > q0) and (j0 < q0 + 512)
                    if hi_default:
                        fixup = (j0 + 128 <= q0)
                    else:
                        fixup = (j0 >= q0 + 512)
                    ps = cx.pp[2 + (jt + qc) % 2]
                    em.op("tensor", lambda t, ka=ka, s4=s4, qh=qh, ps=ps,
                          fixup=fixup: t.matmul(
                        out=ps[:, :], lhsT=ka[:, 128 * s4:128 * (s4 + 1)],
                        rhs=qh[:, :], start=True, stop=not fixup),
                        reads=[ka, qh], writes=[ps], tag="mm_s")
                    if fixup:
                        em.op("tensor", lambda t, kn=kn, s4=s4, qh=qh, ps=ps:
                              t.matmul(out=ps[:, :],
                                       lhsT=kn[64:69, 128 * s4:128 * (s4 + 1)],
                                       rhs=qh[64:69, :], start=False, stop=True),
                              reads=[kn, qh], writes=[ps], tag="mm_s2")
                    at = att[(4 * s4 + qc) % 4]
                    if crossing:
                        ph = (j0 - q0) // 128
                        em.op("vector", lambda v, ps=ps, ph=ph, fcs=fcs:
                              v.tensor_tensor(out=ps[:, :], in0=ps[:, :],
                                              in1=fcs[:, 512 * ph:512 * (ph + 1)],
                                              op=ALU.add),
                              reads=[ps, fcs], writes=[ps], tag="fcor")
                    em.op("scalar", lambda s, ps=ps, at=at: s.activation(
                        out=at[:, :], in_=ps[:, :], func=AF.Exp,
                        bias=nb8[:, 0:1]),
                        reads=[ps, nb8], writes=[at], tag="exp")
                    pso = cx.pp[PSO[qc]]
                    em.op("tensor", lambda t, vt=vt, s4=s4, hs=hs, at=at, pso=pso,
                          jt=jt: t.matmul(
                              out=pso[0:65, :],
                              lhsT=vt[:, 130 * s4 + 65 * hs:130 * s4 + 65 * hs + 65],
                              rhs=at[:, :], start=(jt == 0), stop=(jt == 31)),
                          reads=[vt, at], writes=[pso], tag="mm_av")
        for qc in range(4):
            pso = cx.pp[PSO[qc]]
            em.op("vector", lambda v, pso=pso: v.tensor_copy(
                out=rrow[:, :], in_=pso[64:65, :]),
                reads=[pso], writes=[rrow], tag="sumrow")
            em.op("vector", lambda v: v.reciprocal(out=rrow[:, :], in_=rrow[:, :]),
                  reads=[rrow], writes=[rrow], tag="recip")
            pbc = cx.pp[6]
            em.op("tensor", lambda t, pbc=pbc: t.matmul(
                out=pbc[0:64, :], lhsT=cx.ones_col[:, 0:64], rhs=rrow[:, :],
                start=True, stop=True),
                reads=[rrow, cx.ones_col], writes=[pbc], tag="bc_r")
            em.op("vector", lambda v, pbc=pbc: v.tensor_copy(
                out=rb[:, :], in_=pbc[0:64, :]),
                reads=[pbc], writes=[rb], tag="ev_rb")
            ao = aoT[h // 2]
            ro = 64 * (h % 2)
            em.op("vector", lambda v, pso=pso, ao=ao, ro=ro, qc=qc: v.tensor_tensor(
                out=ao[ro:ro + 64, 512 * qc:512 * (qc + 1)], in0=pso[0:64, :],
                in1=rb[:, :], op=ALU.mult),
                reads=[pso, rb], writes=[ao], tag="ev_ao")

    xoc = [em.sbuf(f"xoc{i}", (128, 512), F32) for i in range(2)]
    tmpo = em.sbuf("tmpo", (128, 512), F32)
    for qc in range(4):
        for ot in range(8):
            ps = cx.pp[ot % 2]
            for k in range(8):
                em.op("tensor", lambda t, ot=ot, k=k, qc=qc, ps=ps: t.matmul(
                    out=ps[:, :], lhsT=outwT_s[k][:, 128 * ot:128 * (ot + 1)],
                    rhs=aoT[k][:, 512 * qc:512 * (qc + 1)],
                    start=(k == 0), stop=(k == 7)),
                    reads=[outwT_s[k], aoT[k]], writes=[ps], tag="mm_out")
            xo = xoc[ot % 2]
            em.dma("sync", xo[:, :],
                   xin[128 * ot:128 * (ot + 1), t0 + 512 * qc:t0 + 512 * (qc + 1)],
                   writes=[xo], tag="ld_xo")
            em.op("vector", lambda v, ps=ps, ot=ot: v.tensor_scalar(
                out=tmpo[:, :], in0=ps[:, :], scalar1=ob_s[:, ot:ot + 1],
                scalar2=0.0, op0=ALU.add, op1=ALU.add),
                reads=[ps, ob_s], writes=[tmpo], tag="ev_out")
            em.op("vector", lambda v, xo=xo: v.tensor_tensor(
                out=xo[:, :], in0=xo[:, :], in1=tmpo[:, :], op=ALU.add),
                reads=[xo, tmpo], writes=[xo], tag="resid")
            em.dma("gpsimd", xout[128 * ot:128 * (ot + 1),
                                  512 * qc:512 * (qc + 1)], xo[:, :],
                   reads=[xo], tag="out")
    em.finalize()
    em.close()
    return nc


def _pack_alibi(qkv_w, qkv_b, out_w, out_b, th):
    perm = np.zeros(3 * D, np.int64)
    for h in range(16):
        for j in range(64):
            perm[h * 64 + j] = h * 192 + j
            perm[D + h * 64 + j] = h * 192 + 64 + j
            perm[2 * D + h * 64 + j] = h * 192 + 128 + j
    qkvT = np.ascontiguousarray(qkv_w[perm].T)
    qb = np.ascontiguousarray(qkv_b[perm].reshape(24, 128).T)
    outwT = np.ascontiguousarray(out_w.T)
    ob = np.ascontiguousarray(out_b.reshape(8, 128).T)
    m = 0.5 ** np.arange(16, dtype=np.float64)
    j = np.arange(T, dtype=np.float64)
    sgn = -1.0 if th == 0 else 1.0
    kxlv = np.zeros((16, 5, T), np.float32)
    for h in range(16):
        kxlv[h, 0] = sgn * m[h] * (j % 128)
        kxlv[h, 1] = sgn * m[h] * 128 * (j // 128)
        kxlv[h, 2] = sgn * m[h]
        kxlv[h, 3] = sgn * 128 * m[h]
        kxlv[h, 4] = sgn * m[h]
    kxnv = (-2.0 * kxlv).astype(np.float32)
    q = np.arange(T)
    qxv = np.zeros((5, T), np.float32)
    qxv[0] = 1.0
    qxv[1] = 1.0
    qxv[2] = -(q % 128)
    qxv[3] = -((q % 512) // 128)
    qxv[4] = -(512 * (q // 512))
    fc = np.zeros((16, 4, 128, 512), np.float32)
    jj = np.arange(128)[:, None]
    qq = np.arange(512)[None, :]
    for h in range(16):
        for ph in range(4):
            dd = (128 * ph + jj) - qq
            if th == 0:
                fc[h, ph] = np.where(dd < 0, 2 * m[h] * dd, 0.0)
            else:
                fc[h, ph] = np.where(dd > 0, -2 * m[h] * dd, 0.0)
    fc2 = np.ascontiguousarray(fc.transpose(0, 2, 1, 3).reshape(16, 128, 2048))
    return (qkvT, qb, outwT, ob, kxlv.astype(BF), kxnv.astype(BF),
            qxv.astype(BF), fc2.astype(np.float32))


def _alibi_stage(X_T, qkv_w, qkv_b, out_w, out_b):
    X_T2 = np.empty_like(X_T)
    for th in (0, 1):
        qkvT, qb, outwT, ob, kxlv, kxnv, qxv, fc = _pack_alibi(
            qkv_w, qkv_b, out_w, out_b, th)
        nc = _get_prog("alibi", build_alibi_th, th)
        ims = [{"xTf": np.ascontiguousarray(X_T[b]), "ident": _IDENT,
                "qkvT": qkvT, "qkvb": qb, "outwT": outwT, "outb": ob,
                "kxl": kxlv, "kxn": kxnv, "qx": qxv, "fcor": fc}
               for b in range(B)]
        res = run_bass_kernel_spmd(nc, ims, core_ids=list(range(B)))
        for b in range(B):
            X_T2[b][:, th * TH:(th + 1) * TH] = np.asarray(res.results[b]["xoT"])
    return X_T2



def _qk_process(em, cx, name, ps_qkv, srow, rsq_scale, ctab, stab, rmat_s,
                out_ring, ring_cols, blk_g, rows2, epscol, sqb, kb, krot, t1):
    pp = cx.pp
    em.op("scalar", lambda s: s.activation(out=sqb[:, :], in_=ps_qkv[:, :],
                                           func=AF.Square),
          reads=[ps_qkv], writes=[sqb], tag=f"{name}_sq")
    em.op("scalar", lambda s: s.activation(out=kb[:, :], in_=ps_qkv[:, :],
                                           func=AF.Copy),
          reads=[ps_qkv], writes=[kb], tag=f"{name}_cp")
    for hs in range(2):
        em.op("tensor", lambda t, hs=hs: t.matmul(
            out=pp[6][0:1, :], lhsT=cx.ones_red[64 * hs:64 * hs + 64, :],
            rhs=sqb[64 * hs:64 * hs + 64, :], start=True, stop=True),
            reads=[sqb, cx.ones_red], writes=[pp[6]], tag=f"{name}_ssq")
        em.op("vector", lambda v, hs=hs: v.tensor_copy(
            out=rows2[0:1, 512 * hs:512 * (hs + 1)], in_=pp[6][0:1, :]),
            reads=[pp[6]], writes=[rows2], tag=f"{name}_evsq")
    em.op("scalar", lambda s: s.activation(out=rows2[0:1, :], in_=rows2[0:1, :],
                                           func=AF.Ln, bias=epscol[:, 0:1]),
          reads=[rows2, epscol], writes=[rows2], tag=f"{name}_ln")
    em.op("scalar", lambda s: s.activation(out=rows2[0:1, :], in_=rows2[0:1, :],
                                           func=AF.Exp, scale=-0.5),
          reads=[rows2], writes=[rows2], tag=f"{name}_exp")
    if rsq_scale != 1.0:
        em.op("vector", lambda v: v.tensor_scalar(
            out=rows2[0:1, :], in0=rows2[0:1, :], scalar1=float(rsq_scale),
            scalar2=0.0, op0=ALU.mult, op1=ALU.add),
            reads=[rows2], writes=[rows2], tag=f"{name}_sc")
    em.op("tensor", lambda t: t.matmul(
        out=pp[6][:, :], lhsT=srow[0:1, 0:128], rhs=rows2[0:1, 0:512],
        start=True, stop=False),
        reads=[srow, rows2], writes=[pp[6]], tag=f"{name}_bc1")
    em.op("tensor", lambda t: t.matmul(
        out=pp[6][:, :], lhsT=srow[0:1, 128:256], rhs=rows2[0:1, 512:1024],
        start=False, stop=True),
        reads=[srow, rows2], writes=[pp[6]], tag=f"{name}_bc2")
    em.op("vector", lambda v: v.tensor_tensor(out=kb[:, :], in0=kb[:, :],
                                              in1=pp[6][:, :], op=ALU.mult),
          reads=[kb, pp[6]], writes=[kb], tag=f"{name}_nrm")
    em.op("tensor", lambda t: t.matmul(out=pp[6][:, :], lhsT=rmat_s[:, :],
                                       rhs=kb[:, :], start=True, stop=True),
          reads=[rmat_s, kb], writes=[pp[6]], tag=f"{name}_rot")
    em.op("vector", lambda v: v.tensor_copy(out=krot[:, :], in_=pp[6][:, :]),
          reads=[pp[6]], writes=[krot], tag=f"{name}_evrot")
    em.op("vector", lambda v: v.tensor_tensor(out=t1[:, :], in0=kb[:, :],
                                              in1=ctab, op=ALU.mult),
          reads=[kb, ("tabs",)], writes=[t1], tag=f"{name}_c")
    em.op("vector", lambda v: v.tensor_tensor(out=krot[:, :], in0=krot[:, :],
                                              in1=stab, op=ALU.mult),
          reads=[krot, ("tabs",)], writes=[krot], tag=f"{name}_s")
    em.op("vector", lambda v: v.tensor_tensor(
        out=out_ring[:, ring_cols[0]:ring_cols[1]], in0=t1[:, :],
        in1=krot[:, :], op=ALU.add),
        reads=[t1, krot], writes=[out_ring], tag=f"{name}_add")


def build_ffn_attn(th, first):
    t0 = th * TH
    g0 = t0 // 512
    nc = bass.Bass("TRN2", target_bir_lowering=False)
    em = Em(nc)
    zin = nc.dram_tensor("zin", [D, T], F32, kind="ExternalInput").ap()
    ident_in = nc.dram_tensor("ident", [128, 128], BF16, kind="ExternalInput").ap()
    wqkv = nc.dram_tensor("wqkv", [D, 1536], F32, kind="ExternalInput").ap()
    wo = nc.dram_tensor("wo", [512, D], F32, kind="ExternalInput").ap()
    qsr = nc.dram_tensor("qsr", [1, 256], F32, kind="ExternalInput").ap()
    ksr = nc.dram_tensor("ksr", [1, 256], F32, kind="ExternalInput").ap()
    rmat = nc.dram_tensor("rmat", [128, 128], BF16, kind="ExternalInput").ap()
    rtab = nc.dram_tensor("rtab", [4, 128, T], BF16, kind="ExternalInput").ap()
    amask = nc.dram_tensor("amask", [128, 8 * 512], BF16, kind="ExternalInput").ap()
    uout = nc.dram_tensor("uout", [D, TH], F32, kind="ExternalOutput").ap()

    cx = Ctx(em, ident_in, nbanks=7)
    ptb = em.psum("ptb", (128, 1024), BF16)
    lns = ln_scratch(em)
    stage = em.sbuf("stage", (128, 1536), F32)
    wqkv_s = [em.sbuf(f"wq{k}", (128, 1536), BF16) for k in range(8)]
    wo_s = [em.sbuf(f"wos{k}", (128, D), BF16) for k in range(4)]
    for k in range(8):
        em.dma("sync", stage[:, :], wqkv[128 * k:128 * (k + 1), :],
               writes=[stage], tag="ld_wq")
        em.op("vector", lambda v, k=k: v.tensor_copy(out=wqkv_s[k][:, :],
                                                     in_=stage[:, :]),
              reads=[stage], writes=[wqkv_s[k]], tag="cast_wq")
    for k in range(4):
        em.dma("sync", stage[:, 0:D], wo[128 * k:128 * (k + 1), :],
               writes=[stage], tag="ld_wo")
        em.op("vector", lambda v, k=k: v.tensor_copy(out=wo_s[k][:, :],
                                                     in_=stage[:, 0:D]),
              reads=[stage], writes=[wo_s[k]], tag="cast_wo")
    rmat_s = em.sbuf("rmat", (128, 128), BF16)
    em.dma("sync", rmat_s[:, :], rmat, writes=[rmat_s], tag="ld_rmat")
    qsr_s = em.sbuf("qsr", (1, 256), F32)
    ksr_s = em.sbuf("ksr", (1, 256), F32)
    em.dma("sync", qsr_s[:, :], qsr, writes=[qsr_s], tag="ld_qsr")
    em.dma("sync", ksr_s[:, :], ksr, writes=[ksr_s], tag="ld_ksr")
    amask_s = em.sbuf("amask", (128, 8 * 512), BF16)
    em.dma("sync", amask_s[:, :], amask, writes=[amask_s], tag="ld_amask")
    epscol = em.sbuf("epscol", (1, 1), F32)
    em.op("gpsimd", lambda g: g.memset(epscol[:, :], 1e-24), writes=[epscol],
          tag="eps")

    kring = [em.sbuf(f"kring{j}", (128, 3 * 512), BF16) for j in range(4)]
    vring = [em.sbuf(f"vring{s}", (128, 3 * 520), BF16) for s in range(4)]
    for s in range(4):
        for sl in range(3):
            for hh in range(8):
                em.op("gpsimd", lambda g, s=s, sl=sl, hh=hh: g.memset(
                    vring[s][:, 520 * sl + 65 * hh + 64:520 * sl + 65 * hh + 65],
                    1.0), writes=[("vones",)], tag="ones_v")
    zc = [em.sbuf(f"zc{i}", (128, 513), F32) for i in range(8)]
    zb = [em.sbuf(f"zb{i}", (128, 513), BF16) for i in range(8)]
    z1 = [em.sbuf(f"z1_{i}", (128, TH), F32) for i in range(8)]
    qkvc = em.sbuf("qkvc", (128, 512), BF16)
    sqb = em.sbuf("sqb", (128, 512), BF16)
    kb = em.sbuf("kb", (128, 512), BF16)
    krot = em.sbuf("krot", (128, 512), BF16)
    t1b = em.sbuf("t1b", (128, 512), BF16)
    rows2 = em.sbuf("rows2", (1, 1024), F32)
    tabs = [em.sbuf(f"tab{i}", (128, 512), BF16) for i in range(4)]
    qhat = [em.sbuf(f"qhat{i}", (128, 512), BF16) for i in range(4)]
    att = [em.sbuf(f"att{i}", (128, 512), BF16) for i in range(2)]
    aot = [em.sbuf(f"aot{i}", (128, 512), BF16) for i in range(4)]
    rrow = em.sbuf("rrow", (1, 512), F32)
    rb = em.sbuf("rb", (64, 512), F32)
    tmpo = em.sbuf("tmpo", (128, 512), F32)

    blk_start = g0 - (1 if th == 1 else 0)
    for gb in range(blk_start, g0 + 4):
        own = gb >= g0
        bs = 512 * gb
        slot = gb % 3
        for i in range(8):
            if bs == 0:
                em.op("gpsimd", lambda g, i=i: g.memset(zc[i][:, 0:1], 0.0),
                      writes=[zc[i]], tag="zpad")
                em.dma("sync", zc[i][:, 1:513], zin[128 * i:128 * (i + 1),
                                                    bs:bs + 512],
                       writes=[zc[i]], tag="ld_z")
            else:
                em.dma("sync", zc[i][:, :], zin[128 * i:128 * (i + 1),
                                                bs - 1:bs + 512],
                       writes=[zc[i]], tag="ld_z")
        if first:
            ln_feature_major(em, cx, zc, 513, zc, 4, 5, name=f"lnS{gb}",
                             scratch=lns)
        for i in range(8):
            em.op("scalar", lambda s, i=i: s.activation(
                out=zb[i][:, :], in_=zc[i][:, :], func=AF.Copy),
                reads=[zc[i]], writes=[zb[i]], tag="cast_z")
        for ti in range(4):
            em.dma("sync", tabs[ti][:, :], rtab[ti, :, bs:bs + 512],
                   writes=[("tabs",)], tag="ld_tab")
        for ot in range(12):
            kind = 0 if ot < 4 else (1 if ot < 8 else 2)
            if kind == 0 and not own:
                continue
            ps = cx.pp[ot % 2]
            for k in range(8):
                em.op("tensor", lambda t, ot=ot, k=k, ps=ps: t.matmul(
                    out=ps[:, :], lhsT=wqkv_s[k][:, 128 * ot:128 * (ot + 1)],
                    rhs=(zb[k][:, 1:513] if k < 4 else zb[k][:, 0:512]),
                    start=(k == 0), stop=(k == 7)),
                    reads=[wqkv_s[k], zb[k]], writes=[ps], tag="mm_qkv")
            if kind == 2:
                em.op("scalar", lambda s, ps=ps: s.activation(
                    out=qkvc[:, :], in_=ps[:, :], func=AF.Copy),
                    reads=[ps], writes=[qkvc], tag="ev_vv")
                jj = ot - 8
                for s4 in range(4):
                    em.op("tensor", lambda t, s4=s4: t.transpose(
                        out=ptb[:, 512 * (s4 % 2):512 * (s4 % 2) + 128],
                        in_=qkvc[:, 128 * s4:128 * (s4 + 1)],
                        identity=cx.ident[:, :]),
                        reads=[qkvc, cx.ident], writes=[ptb], tag="tr_v")
                    for hs in range(2):
                        hh = 2 * jj + hs
                        em.op("vector", lambda v, s4=s4, hs=hs, hh=hh, slot=slot:
                              v.tensor_copy(
                                  out=vring[s4][:, 520 * slot + 65 * hh:
                                                520 * slot + 65 * hh + 64],
                                  in_=ptb[:, 512 * (s4 % 2) + 64 * hs:
                                          512 * (s4 % 2) + 64 * hs + 64]),
                              reads=[ptb], writes=[vring[s4]], tag="ev_vt")
            else:
                srow = qsr_s if kind == 0 else ksr_s
                scl = 8.0 if kind == 0 else 1.0
                ct = tabs[0] if kind == 0 else tabs[2]
                st = tabs[1] if kind == 0 else tabs[3]
                if kind == 0:
                    oring, rc = qhat[ot], (0, 512)
                else:
                    oring, rc = kring[ot - 4], (512 * slot, 512 * slot + 512)
                _qk_process(em, cx, f"qk{gb}_{ot}", ps, srow, scl,
                            ct[:, :], st[:, :], rmat_s, oring, rc, gb, rows2,
                            epscol, sqb, kb, krot, t1b)
        if not own:
            continue
        ob = gb - g0
        for h in range(8):
            pso = cx.pp[4 + (h % 2)]
            jts = range(4, 8) if gb == 0 else range(8)
            first_jt = 4 if gb == 0 else 0
            for jtl in jts:
                jgb = gb - 1 + jtl // 4
                jslot = jgb % 3
                s4 = jtl % 4
                ps = cx.pp[2 + (jtl % 2)]
                em.op("tensor", lambda t, h=h, jslot=jslot, s4=s4, ps=ps:
                      t.matmul(out=ps[:, :],
                               lhsT=kring[h // 2]
                               [64 * (h % 2):64 * (h % 2) + 64,
                                512 * jslot + 128 * s4:512 * jslot + 128 * (s4 + 1)],
                               rhs=qhat[h // 2][64 * (h % 2):64 * (h % 2) + 64, :],
                               start=True, stop=True),
                      reads=[kring[h // 2], qhat[h // 2]], writes=[ps], tag="mm_s")
                em.op("vector", lambda v, ps=ps, jtl=jtl: v.tensor_tensor(
                    out=ps[:, :], in0=ps[:, :],
                    in1=amask_s[:, 512 * jtl:512 * (jtl + 1)], op=ALU.add),
                    reads=[ps, amask_s], writes=[ps], tag="mask")
                at = att[jtl % 2]
                em.op("scalar", lambda s, ps=ps, at=at: s.activation(
                    out=at[:, :], in_=ps[:, :], func=AF.Exp),
                    reads=[ps], writes=[at], tag="exp")
                em.op("tensor", lambda t, at=at, h=h, jslot=jslot, s4=s4,
                      pso=pso, jtl=jtl, first_jt=first_jt: t.matmul(
                          out=pso[0:65, :],
                          lhsT=vring[s4][:, 520 * jslot + 65 * h:
                                         520 * jslot + 65 * h + 65],
                          rhs=at[:, :], start=(jtl == first_jt),
                          stop=(jtl == 7)),
                      reads=[vring[s4], at, ("vones",)], writes=[pso], tag="mm_av")
            em.op("vector", lambda v, pso=pso: v.tensor_copy(
                out=rrow[:, :], in_=pso[64:65, :]),
                reads=[pso], writes=[rrow], tag="sumrow")
            em.op("vector", lambda v: v.reciprocal(out=rrow[:, :],
                                                   in_=rrow[:, :]),
                  reads=[rrow], writes=[rrow], tag="recip")
            em.op("tensor", lambda t: t.matmul(
                out=cx.pp[6][0:64, :], lhsT=cx.ones_col[:, 0:64],
                rhs=rrow[:, :], start=True, stop=True),
                reads=[rrow, cx.ones_col], writes=[cx.pp[6]], tag="bc_r")
            em.op("vector", lambda v: v.tensor_copy(out=rb[:, :],
                                                    in_=cx.pp[6][0:64, :]),
                  reads=[cx.pp[6]], writes=[rb], tag="ev_rb")
            ao = aot[h // 2]
            ro = 64 * (h % 2)
            em.op("vector", lambda v, pso=pso, ao=ao, ro=ro: v.tensor_tensor(
                out=ao[ro:ro + 64, :], in0=pso[0:64, :], in1=rb[:, :],
                op=ALU.mult),
                reads=[pso, rb], writes=[ao], tag="ev_ao")
        for ot in range(8):
            ps = cx.pp[ot % 2]
            for k in range(4):
                em.op("tensor", lambda t, ot=ot, k=k, ps=ps: t.matmul(
                    out=ps[:, :], lhsT=wo_s[k][:, 128 * ot:128 * (ot + 1)],
                    rhs=aot[k][:, :], start=(k == 0), stop=(k == 3)),
                    reads=[wo_s[k], aot[k]], writes=[ps], tag="mm_wo")
            em.op("vector", lambda v, ps=ps, ot=ot, ob=ob: v.tensor_tensor(
                out=z1[ot][:, 512 * ob:512 * (ob + 1)], in0=zc[ot][:, 1:513],
                in1=ps[:, :], op=ALU.add),
                reads=[ps, zc[ot]], writes=[z1[ot]], tag="resid")
    for i in range(8):
        em.dma("gpsimd", uout[128 * i:128 * (i + 1), :], z1[i][:, :],
               reads=[z1[i]], tag="out")
    em.finalize()
    em.close()
    return nc


def build_ffn_ffn(th, last):
    t0 = th * TH
    nc = bass.Bass("TRN2", target_bir_lowering=False)
    em = Em(nc)
    uin = nc.dram_tensor("uin", [D, T], F32, kind="ExternalInput").ap()
    ident_in = nc.dram_tensor("ident", [128, 128], BF16, kind="ExternalInput").ap()
    w1p = nc.dram_tensor("w1p", [D, 5632], F32, kind="ExternalInput").ap()
    w2p = nc.dram_tensor("w2p", [2816, D], F32, kind="ExternalInput").ap()
    b1p = nc.dram_tensor("b1p", [128, 44], F32, kind="ExternalInput").ap()
    b2r = nc.dram_tensor("b2r", [1, D], F32, kind="ExternalInput").ap()
    lng = nc.dram_tensor("lng", [128, 8], F32, kind="ExternalInput").ap()
    lnb = nc.dram_tensor("lnb", [128, 8], F32, kind="ExternalInput").ap()
    if last:
        xst = nc.dram_tensor("xst", [D, TH], F32, kind="ExternalInput").ap()
    zout = nc.dram_tensor("zoT", [D, TH], F32, kind="ExternalOutput").ap()

    cx = Ctx(em, ident_in)
    lns = ln_scratch(em)
    stage = em.sbuf("stage", (128, 5632), F32)
    w1_s = [em.sbuf(f"w1s{k}", (128, 5632), BF16) for k in range(8)]
    for k in range(8):
        em.dma("sync", stage[:, :], w1p[128 * k:128 * (k + 1), :],
               writes=[stage], tag="ld_w1")
        em.op("vector", lambda v, k=k: v.tensor_copy(out=w1_s[k][:, :],
                                                     in_=stage[:, :]),
              reads=[stage], writes=[w1_s[k]], tag="cast_w1")
    b1_s = em.sbuf("b1s", (128, 44), F32)
    lng_s = em.sbuf("lngs", (128, 8), F32)
    lnb_s = em.sbuf("lnbs", (128, 8), F32)
    em.dma("sync", b1_s[:, :], b1p, writes=[b1_s], tag="ld_b1")
    em.dma("sync", lng_s[:, :], lng, writes=[lng_s], tag="ld_lng")
    em.dma("sync", lnb_s[:, :], lnb, writes=[lnb_s], tag="ld_lnb")
    w2_s = [em.sbuf(f"w2s{i}", (128, D), BF16) for i in range(2)]
    w2_f = [stage[:, 0:1024], stage[:, 1024:2048]]
    b2r_f = em.sbuf("b2rf", (1, D), F32)
    b2r_s = em.sbuf("b2rs", (1, D), BF16)
    ones512 = em.sbuf("ones512", (1, 512), BF16)
    em.dma("sync", b2r_f[:, :], b2r, writes=[b2r_f], tag="ld_b2")
    em.op("vector", lambda v: v.tensor_copy(out=b2r_s[:, :], in_=b2r_f[:, :]),
          reads=[b2r_f], writes=[b2r_s], tag="cast_b2")
    em.op("gpsimd", lambda g: g.memset(ones512[:, :], 1.0), writes=[ones512],
          tag="ones512")

    uc = [em.sbuf(f"uc{i}", (128, 513), F32) for i in range(8)]
    hb = [em.sbuf(f"hb{i}", (128, 513), BF16) for i in range(8)]
    ag = [em.sbuf(f"ag{i}", (128, 512), BF16) for i in range(22)]
    gg = em.sbuf("gg", (128, 512), BF16)
    xoc = [em.sbuf(f"xoc{i}", (128, 512), F32) for i in range(5)]
    gb_cols = [(lng_s[:, i:i + 1], lnb_s[:, i:i + 1]) for i in range(8)]
    for c in range(4):
        bs = t0 + 512 * c
        for i in range(8):
            if bs == 0:
                em.op("gpsimd", lambda g, i=i: g.memset(uc[i][:, 0:1], 0.0),
                      writes=[uc[i]], tag="zpad")
                em.dma("sync", uc[i][:, 1:513],
                       uin[128 * i:128 * (i + 1), bs:bs + 512],
                       writes=[uc[i]], tag="ld_u")
            else:
                em.dma("sync", uc[i][:, :],
                       uin[128 * i:128 * (i + 1), bs - 1:bs + 512],
                       writes=[uc[i]], tag="ld_u")
        ln_feature_major(em, cx, uc, 512, hb, 4, 5, gain_bias=gb_cols,
                         name=f"ln2_{c}", scratch=lns,
                         x_offs=[1 if i < 4 else 0 for i in range(8)])
        for ht in range(44):
            ps = cx.pp[ht % 2]
            for k in range(8):
                em.op("tensor", lambda t, ht=ht, k=k, ps=ps: t.matmul(
                    out=ps[:, :], lhsT=w1_s[k][:, 128 * ht:128 * (ht + 1)],
                    rhs=hb[k][:, 0:512], start=(k == 0), stop=(k == 7)),
                    reads=[w1_s[k], hb[k]], writes=[ps], tag="mm_w1")
            if ht < 22:
                em.op("vector", lambda v, ps=ps, ht=ht: v.tensor_scalar(
                    out=ag[ht][:, :], in0=ps[:, :], scalar1=b1_s[:, ht:ht + 1],
                    scalar2=0.0, op0=ALU.add, op1=ALU.add),
                    reads=[ps, b1_s], writes=[ag[ht]], tag="ev_a")
            else:
                em.op("scalar", lambda s, ps=ps, ht=ht: s.activation(
                    out=gg[:, :], in_=ps[:, :], func=AF.Gelu,
                    bias=b1_s[:, ht:ht + 1]),
                    reads=[ps, b1_s], writes=[gg], tag="gelu")
                em.op("vector", lambda v, ht=ht: v.tensor_tensor(
                    out=ag[ht - 22][:, :], in0=ag[ht - 22][:, :], in1=gg[:, :],
                    op=ALU.mult),
                    reads=[ag[ht - 22], gg], writes=[ag[ht - 22]], tag="hgm")
        for oh in range(2):
            for kk in range(22):
                wf = w2_f[kk % 2]
                em.dma("sync", wf, w2p[128 * kk:128 * (kk + 1), :],
                       writes=[stage], tag="ld_w2")
                em.op("scalar", lambda s, kk=kk, wf=wf: s.activation(
                    out=w2_s[kk % 2][:, :], in_=wf, func=AF.Copy),
                    reads=[stage], writes=[w2_s[kk % 2]], tag="cast_w2")
                for oi in range(4):
                    o = 4 * oh + oi
                    em.op("tensor", lambda t, kk=kk, o=o, oi=oi: t.matmul(
                        out=cx.pp[2 + oi][:, :],
                        lhsT=w2_s[kk % 2][:, 128 * o:128 * (o + 1)],
                        rhs=ag[kk][:, :], start=(kk == 0), stop=False),
                        reads=[w2_s[kk % 2], ag[kk]], writes=[cx.pp[2 + oi]],
                        tag="mm_w2")
            for oi in range(4):
                o = 4 * oh + oi
                em.op("tensor", lambda t, o=o, oi=oi: t.matmul(
                    out=cx.pp[2 + oi][:, :],
                    lhsT=b2r_s[0:1, 128 * o:128 * (o + 1)],
                    rhs=ones512[:, :], start=False, stop=True),
                    reads=[b2r_s, ones512], writes=[cx.pp[2 + oi]],
                    tag="mm_b2")
            for oi in range(4):
                o = 4 * oh + oi
                xo = xoc[oi]
                em.op("vector", lambda v, o=o, oi=oi, xo=xo: v.tensor_tensor(
                    out=xo[:, :], in0=cx.pp[2 + oi][:, :],
                    in1=uc[o][:, 1:513], op=ALU.add),
                    reads=[cx.pp[2 + oi], uc[o]], writes=[xo], tag="resid")
                if last:
                    xs2 = xoc[4]
                    em.dma("sync", xs2[:, :],
                           xst[128 * o:128 * (o + 1), 512 * c:512 * (c + 1)],
                           writes=[xs2], tag="ld_xst")
                    em.op("vector", lambda v, xo=xo, xs2=xs2: v.scalar_tensor_tensor(
                        out=xo[:, :], in0=xo[:, :], scalar=0.5, in1=xs2[:, :],
                        op0=ALU.mult, op1=ALU.add),
                        reads=[xo, xs2], writes=[xo], tag="final")
                em.dma("gpsimd", zout[128 * o:128 * (o + 1),
                                      512 * c:512 * (c + 1)], xo[:, :],
                       reads=[xo], tag="out")
    em.finalize()
    em.close()
    return nc


def _pack_ffn(l, wqkv, q_scale, k_scale, wo, ln_g, ln_b, w1, b1, w2, b2):
    wq = np.ascontiguousarray(wqkv[l])
    wos = np.ascontiguousarray(wo[l])
    qsr = np.zeros((1, 256), np.float32)
    qsr[0, 0:64] = q_scale[l]; qsr[0, 64:128] = 0
    qsr[0, 128 + 64:128 + 128] = q_scale[l]
    ksr = np.zeros((1, 256), np.float32)
    ksr[0, 0:64] = k_scale[l]; ksr[0, 128 + 64:256] = k_scale[l]
    R = np.zeros((128, 128), np.float32)
    for hh in range(2):
        b0 = 64 * hh
        for r in range(32):
            R[b0 + r + 32, b0 + r] = -1.0
            R[b0 + r, b0 + r + 32] = 1.0
    n = T
    half = np.arange(0, 64, 2, dtype=np.float64)
    inv_freq = 1.0 / 10000.0 ** (half / 64.0)
    t = np.arange(n, dtype=np.float64)
    freqs = t[:, None] * inv_freq[None, :]
    freqs = np.concatenate([freqs, freqs], axis=1)
    base = (half + 0.4 * 64.0) / (1.4 * 64.0)
    power = (t - n // 2) / 256.0
    scale = base[None, :] ** power[:, None]
    scale = np.concatenate([scale, scale], axis=1)
    cos, sin = np.cos(freqs), np.sin(freqs)
    cq = (cos * scale).T; sq = (sin * scale).T
    ck = (cos / scale).T; sk = (sin / scale).T
    rtab = np.stack([np.concatenate([a, a], 0) for a in (cq, sq, ck, sk)])
    am = np.zeros((8, 128, 512), np.float32)
    jl = np.arange(128)[:, None]
    qq = np.arange(512)[None, :]
    for jtl in range(8):
        dd = 512 - 128 * jtl - jl + qq
        am[jtl] = np.where((dd >= 0) & (dd <= 512), 0.0, -100.0)
    amp = np.ascontiguousarray(am.transpose(1, 0, 2).reshape(128, 8 * 512)).astype(BF)
    w1l, b1l, w2l, b2l = w1[l], b1[l], w2[l], b2[l]
    w1p = np.zeros((1024, 5632), np.float32)
    w1p[:, 0:2730] = w1l[:, 0:2730]
    w1p[:, 2816:2816 + 2730] = w1l[:, 2730:5460]
    b1pk = np.zeros((128, 44), np.float32)
    b1f = np.zeros(5632, np.float32)
    b1f[0:2730] = b1l[0:2730]
    b1f[2816:2816 + 2730] = b1l[2730:5460]
    b1pk[:, :] = b1f.reshape(44, 128).T
    w2p = np.zeros((2816, 1024), np.float32)
    w2p[0:2730, :] = w2l
    lngp = np.ascontiguousarray(ln_g[l].reshape(8, 128).T)
    lnbp = np.ascontiguousarray(ln_b[l].reshape(8, 128).T)
    return dict(wqkv=wq, wo=wos, qsr=qsr, ksr=ksr,
                rmat=R.astype(BF), rtab=rtab.astype(BF), amask=amp,
                w1p=w1p, w2p=w2p, b1p=b1pk, lng=lngp, lnb=lnbp,
                b2r=np.ascontiguousarray(b2l[None, :]))


def _ffn_stage(X_T, wqkv, q_scale, k_scale, wo, ln_g, ln_b, w1, b1, w2, b2,
               final_unused=False):
    Z = X_T
    X0 = X_T
    for l in range(2):
        pk = _pack_ffn(l, wqkv, q_scale, k_scale, wo, ln_g, ln_b, w1, b1, w2, b2)
        first = (l == 0)
        last = (l == 1)
        U = np.empty_like(Z)
        for th in (0, 1):
            nc = _get_prog("ffn_attn", build_ffn_attn, th, first)
            ims = [{"zin": np.ascontiguousarray(Z[b]), "ident": _IDENT,
                    "wqkv": pk["wqkv"], "wo": pk["wo"], "qsr": pk["qsr"],
                    "ksr": pk["ksr"], "rmat": pk["rmat"], "rtab": pk["rtab"],
                    "amask": pk["amask"]} for b in range(B)]
            res = run_bass_kernel_spmd(nc, ims, core_ids=list(range(B)))
            for b in range(B):
                U[b][:, th * TH:(th + 1) * TH] = np.asarray(res.results[b]["uout"])
        Z2 = np.empty_like(Z)
        for th in (0, 1):
            nc = _get_prog("ffn_ffn", build_ffn_ffn, th, last)
            ims = []
            for b in range(B):
                im = {"uin": np.ascontiguousarray(U[b]), "ident": _IDENT,
                      "w1p": pk["w1p"], "w2p": pk["w2p"], "b1p": pk["b1p"],
                      "lng": pk["lng"], "lnb": pk["lnb"], "b2r": pk["b2r"]}
                if last:
                    im["xst"] = np.ascontiguousarray(
                        X0[b][:, th * TH:(th + 1) * TH])
                ims.append(im)
            res = run_bass_kernel_spmd(nc, ims, core_ids=list(range(B)))
            for b in range(B):
                Z2[b][:, th * TH:(th + 1) * TH] = np.asarray(res.results[b]["zoT"])
        Z = Z2
    return Z



def build_mamba_ab(first_layer):
    nc = bass.Bass("TRN2", target_bir_lowering=False)
    em = Em(nc)
    ap = lambda n, s, d=F32, k="ExternalInput": nc.dram_tensor(n, s, d, kind=k).ap()
    hin = ap("hin", [D, T])
    ident_in = ap("ident", [128, 128], BF16)
    if not first_layer:
        rin = ap("rin", [D, T])
    wpA = ap("wpA", [D, 1024])
    wpB = ap("wpB", [D, 1024])
    wx = ap("wx", [1024, 96])
    wdt = ap("wdt", [64, 1024])
    nw = ap("nw", [128, 8])
    nb = ap("nb", [128, 8])
    wcp = ap("wcp", [128, 32])
    wcb = ap("wcb", [128, 8])
    dtb = ap("dtb", [128, 8])
    dsk = ap("dsk", [128, 8])
    alog = ap("alog", [128, 8 * 16])
    sel = ap("sel", [32, 32 * 128], BF16)
    rout = ap("rout", [D, T], F32, "ExternalOutput")
    yout = ap("yout", [1024, T], BF16, "ExternalOutput")
    xdb_in = nc.dram_tensor("xdb_i", [96, T], F32)
    xdb_out = nc.dram_tensor("xdb_o", [96, T], F32)
    xcs_sp = nc.dram_tensor("xcs_sp", [1024, T], BF16)
    z_sp = nc.dram_tensor("z_sp", [1024, T], BF16)

    cx = Ctx(em, ident_in)
    lns = ln_scratch(em)
    NCH = T // 512

    stage = em.sbuf("stage", (128, 2048), F32)
    wpA_s = [em.sbuf(f"wpA{k}", (128, 1024), BF16) for k in range(8)]
    wpB_s = [em.sbuf(f"wpB{k}", (128, 1024), BF16) for k in range(8)]
    for k in range(8):
        em.dma("sync", stage[:, 0:1024], wpA[128 * k:128 * (k + 1), :],
               writes=[stage], tag="ld_wpA")
        em.op("vector", lambda v, k=k: v.tensor_copy(out=wpA_s[k][:, :],
                                                     in_=stage[:, 0:1024]),
              reads=[stage], writes=[wpA_s[k]], tag="c_wpA")
        em.dma("sync", stage[:, 0:1024], wpB[128 * k:128 * (k + 1), :],
               writes=[stage], tag="ld_wpB")
        em.op("vector", lambda v, k=k: v.tensor_copy(out=wpB_s[k][:, :],
                                                     in_=stage[:, 0:1024]),
              reads=[stage], writes=[wpB_s[k]], tag="c_wpB")
    wx_s = [em.sbuf(f"wx{k}", (128, 96), BF16) for k in range(8)]
    for k in range(8):
        em.dma("sync", stage[:, 0:96], wx[128 * k:128 * (k + 1), :],
               writes=[stage], tag="ld_wx")
        em.op("scalar", lambda s, k=k: s.activation(
            out=wx_s[k][:, :], in_=stage[:, 0:96], func=AF.Copy, scale=0.5),
            reads=[stage], writes=[wx_s[k]], tag="c_wx")
    wdt_s = em.sbuf("wdt", (64, 1024), BF16)
    em.dma("sync", stage[0:64, 0:1024], wdt, writes=[stage], tag="ld_wdt")
    em.op("vector", lambda v: v.tensor_copy(out=wdt_s[:, :],
                                            in_=stage[0:64, 0:1024]),
          reads=[stage], writes=[wdt_s], tag="c_wdt")
    sel_s = em.sbuf("sel", (32, 32 * 128), BF16)
    em.dma("sync", sel_s[:, :], sel, writes=[sel_s], tag="ld_sel")
    nw_s = em.sbuf("nw", (128, 8), F32)
    nb_s = em.sbuf("nb", (128, 8), F32)
    wcp_s = em.sbuf("wcp", (128, 32), F32)
    wcb_s = em.sbuf("wcb", (128, 8), F32)
    dtb_s = em.sbuf("dtb", (128, 8), F32)
    dsk_s = em.sbuf("dsk", (128, 8), F32)
    alog_s = em.sbuf("alog", (128, 128), F32)
    for nm, dst, srcp in [("nw", nw_s, nw), ("nb", nb_s, nb), ("wcp", wcp_s, wcp),
                          ("wcb", wcb_s, wcb), ("dtb", dtb_s, dtb),
                          ("dsk", dsk_s, dsk), ("alog", alog_s, alog)]:
        em.dma("sync", dst[:, :], srcp, writes=[dst], tag=f"ld_{nm}")
    As_s = em.sbuf("As", (128, 128), F32)
    em.op("scalar", lambda s: s.activation(out=As_s[:, :], in_=alog_s[:, :],
                                           func=AF.Exp),
          reads=[alog_s], writes=[As_s], tag="mkA")
    em.op("vector", lambda v: v.tensor_scalar(out=As_s[:, :], in0=As_s[:, :],
                                              scalar1=-1.0, scalar2=0.0,
                                              op0=ALU.mult, op1=ALU.add),
          reads=[As_s], writes=[As_s], tag="negA")
    Dh_s = em.sbuf("Dh", (128, 8), F32)
    em.op("vector", lambda v: v.tensor_scalar(out=Dh_s[:, :], in0=dsk_s[:, :],
                                              scalar1=0.5, scalar2=0.0,
                                              op0=ALU.mult, op1=ALU.add),
          reads=[dsk_s], writes=[Dh_s], tag="Dh")

    build_mamba_ab._acc = em.sbuf("cacc", (128, 512), F32)
    build_mamba_ab._th = em.sbuf("thm", (128, 512), F32)
    gb_cols = [(nw_s[:, i:i + 1], nb_s[:, i:i + 1]) for i in range(8)]
    hc = [em.sbuf(f"hc{i}", (128, 512), F32) for i in range(8)]
    rc = [em.sbuf(f"rc{i}", (128, 512), F32) for i in range(8)]
    xnc = [em.sbuf(f"xnc{i}", (128, 512), BF16) for i in range(8)]
    xct = [em.sbuf(f"xct{i}", (128, 515), F32) for i in range(8)]
    zb = em.sbuf("zbm", (128, 512), BF16)
    xcb = em.sbuf("xcbm", (128, 512), BF16)
    xdb_s = em.sbuf("xdbs", (96, T), F32)

    for c in range(NCH):
        c0 = 512 * c
        for i in range(8):
            em.dma("sync", hc[i][:, :], hin[128 * i:128 * (i + 1), c0:c0 + 512],
                   writes=[hc[i]], tag="ld_h")
            if not first_layer:
                em.dma("sync", rc[i][:, :], rin[128 * i:128 * (i + 1),
                                                c0:c0 + 512],
                       writes=[rc[i]], tag="ld_r")
                em.op("vector", lambda v, i=i: v.tensor_tensor(
                    out=rc[i][:, :], in0=rc[i][:, :], in1=hc[i][:, :],
                    op=ALU.add),
                    reads=[rc[i], hc[i]], writes=[rc[i]], tag="hr")
        rsrc = hc if first_layer else rc
        if first_layer:
            ln_feature_major(em, cx, hc, 512, rc, 6, 7, name=f"lnS{c}",
                             scratch=lns)
            rsrc = rc
        for i in range(8):
            em.dma("gpsimd", rout[128 * i:128 * (i + 1), c0:c0 + 512],
                   rsrc[i][:, :], reads=[rsrc[i]], tag="out")
        ln_feature_major(em, cx, rsrc, 512, xnc, 6, 7, gain_bias=gb_cols,
                         name=f"lnM{c}", scratch=lns)
        for half, ws in ((0, wpA_s), (1, wpB_s)):
            for ot in range(8):
                ps = cx.pp[ot % 2]
                for k in range(8):
                    em.op("tensor", lambda t, ot=ot, k=k, ps=ps, ws=ws: t.matmul(
                        out=ps[:, :], lhsT=ws[k][:, 128 * ot:128 * (ot + 1)],
                        rhs=xnc[k][:, :], start=(k == 0), stop=(k == 7)),
                        reads=[ws[k], xnc[k]], writes=[ps], tag="mm_ip")
                if half == 1:
                    em.op("scalar", lambda s, ps=ps: s.activation(
                        out=zb[:, :], in_=ps[:, :], func=AF.Copy),
                        reads=[ps], writes=[zb], tag="ev_z")
                    em.dma("gpsimd", z_sp.ap()[128 * ot:128 * (ot + 1),
                                               c0:c0 + 512], zb[:, :],
                           reads=[zb], writes=["z_sp"], tag="sp_z")
                else:
                    xt = xct[ot]
                    if c == 0:
                        em.op("gpsimd", lambda g, xt=xt: g.memset(
                            xt[:, 0:3], 0.0), writes=[xt], tag="cpad")
                    em.op("vector", lambda v, ps=ps, xt=xt: v.tensor_copy(
                        out=xt[:, 3:515], in_=ps[:, :]),
                        reads=[ps], writes=[xt], tag="ev_xc")
                    acc = build_mamba_ab._acc
                    em.op("vector", lambda v, xt=xt, ot=ot: v.tensor_scalar(
                        out=acc[:, :], in0=xt[:, 0:512],
                        scalar1=wcp_s[:, 4 * ot:4 * ot + 1],
                        scalar2=wcb_s[:, ot:ot + 1],
                        op0=ALU.mult, op1=ALU.add),
                        reads=[xt, wcp_s, wcb_s], writes=[acc], tag="cv0")
                    for kk in range(1, 4):
                        em.op("vector", lambda v, xt=xt, ot=ot, kk=kk, acc=acc:
                              v.scalar_tensor_tensor(
                                  out=acc[:, :], in0=xt[:, kk:kk + 512],
                                  scalar=wcp_s[:, 4 * ot + kk:4 * ot + kk + 1],
                                  in1=acc[:, :], op0=ALU.mult, op1=ALU.add),
                              reads=[xt, wcp_s, acc], writes=[acc], tag="cvk")
                    th_ = build_mamba_ab._th
                    em.op("scalar", lambda s, acc=acc, th_=th_: s.activation(
                        out=th_[:, :], in_=acc[:, :], func=AF.Tanh, scale=0.5),
                        reads=[acc], writes=[th_], tag="tanh")
                    em.op("vector", lambda v, acc=acc, th_=th_: v.scalar_tensor_tensor(
                        out=xcb[:, :], in0=th_[:, :], scalar=1.0, in1=acc[:, :],
                        op0=ALU.add, op1=ALU.mult),
                        reads=[th_, acc], writes=[xcb], tag="silu2")
                    em.dma("gpsimd", xcs_sp.ap()[128 * ot:128 * (ot + 1),
                                                 c0:c0 + 512], xcb[:, :],
                           reads=[xcb], writes=["xcs_sp"], tag="sp_xc")
                    em.op("vector", lambda v, xt=xt: v.tensor_copy(
                        out=xt[:, 0:3], in_=xt[:, 512:515]),
                        reads=[xt], writes=[xt], tag="c3")
                    em.op("tensor", lambda t, ot=ot: t.matmul(
                        out=cx.pp[3][0:96, :], lhsT=wx_s[ot][:, :],
                        rhs=xcb[:, :], start=(ot == 0), stop=(ot == 7)),
                        reads=[wx_s[ot], xcb], writes=[cx.pp[3]], tag="mm_xp")
        em.op("vector", lambda v, c0=c0: v.tensor_copy(
            out=xdb_s[:, c0:c0 + 512], in_=cx.pp[3][0:96, :]),
            reads=[cx.pp[3]], writes=[xdb_s], tag="ev_xdb")
    em.dma("gpsimd", xdb_in.ap(), xdb_s[:, :], reads=[xdb_s],
           writes=["xdb_i"], tag="sp_xdb")
    em.cc("AllReduce", ALU.add, [[0, 1], [2, 3], [4, 5], [6, 7]],
          [xdb_in.ap().opt()], [xdb_out.ap().opt()],
          reads=["xdb_i"], writes=["xdb_o"], tag="cc_xdb")
    em.dma("sync", xdb_s[:, :], xdb_out.ap(), reads=["xdb_o"],
           writes=[xdb_s], tag="ld_xdb")
    xdbD = em.sbuf("xdbD", (64, T), BF16)
    em.op("scalar", lambda s: s.activation(out=xdbD[:, :], in_=xdb_s[0:64, :],
                                           func=AF.Copy),
          reads=[xdb_s], writes=[xdbD], tag="c_xdbD")
    xdbB_f = em.sbuf("xdbBf", (32, T), F32)
    em.dma("gpsimd", xdbB_f[:, :], xdb_s[64:96, :], reads=[xdb_s],
           writes=[xdbB_f], tag="mv_bc")
    xdbB = em.sbuf("xdbB", (32, T), BF16)
    em.op("scalar", lambda s: s.activation(out=xdbB[:, :], in_=xdbB_f[:, :],
                                           func=AF.Copy),
          reads=[xdbB_f], writes=[xdbB], tag="c_xdbB")

    dtt = em.sbuf("dtt", (128, 512), F32)
    dtx = em.sbuf("dtx", (128, 512), F32)
    xcf = em.sbuf("xcf", (128, 512), F32)
    zf = em.sbuf("zf", (128, 512), F32)
    sb1 = em.sbuf("sb1", (128, 512), F32)
    uu = [em.sbuf(f"uu{i}", (128, 512), F32) for i in range(2)]
    dA = [em.sbuf(f"dA{i}", (128, 512), F32) for i in range(2)]
    hh = [em.sbuf(f"hh{i}", (128, 512), F32) for i in range(2)]
    pb = [em.sbuf(f"pb{i}", (128, 512), BF16) for i in range(2)]
    carry = [em.sbuf(f"carry{i}", (128, 16), F32) for i in range(8)]
    yv = em.sbuf("yv", (128, 512), F32)
    yf = em.sbuf("yf", (128, 512), BF16)
    for i in range(8):
        em.op("gpsimd", lambda g, i=i: g.memset(carry[i][:, :], 0.0),
              writes=[carry[i]], tag="c0")
    for c in range(NCH):
        c0 = 512 * c
        for dtl in range(8):
            em.op("tensor", lambda t, dtl=dtl, c0=c0: t.matmul(
                out=cx.pp[2][:, :], lhsT=wdt_s[:, 128 * dtl:128 * (dtl + 1)],
                rhs=xdbD[:, c0:c0 + 512], start=True, stop=True),
                reads=[wdt_s, xdbD], writes=[cx.pp[2]], tag="mm_dt")
            em.op("scalar", lambda s, dtl=dtl: s.activation(
                out=sb1[:, :], in_=cx.pp[2][:, :], func=AF.Abs,
                bias=dtb_s[:, dtl:dtl + 1]),
                reads=[cx.pp[2], dtb_s], writes=[sb1], tag="sp_abs")
            em.op("scalar", lambda s: s.activation(
                out=sb1[:, :], in_=sb1[:, :], func=AF.Exp, scale=-1.0),
                reads=[sb1], writes=[sb1], tag="sp_exp")
            em.op("scalar", lambda s: s.activation(
                out=sb1[:, :], in_=sb1[:, :], func=AF.Ln, bias=1.0),
                reads=[sb1], writes=[sb1], tag="sp_ln")
            em.op("scalar", lambda s, dtl=dtl: s.activation(
                out=dtt[:, :], in_=cx.pp[2][:, :], func=AF.Relu,
                bias=dtb_s[:, dtl:dtl + 1]),
                reads=[cx.pp[2], dtb_s], writes=[dtt], tag="sp_relu")
            em.op("vector", lambda v: v.tensor_tensor(
                out=dtt[:, :], in0=dtt[:, :], in1=sb1[:, :], op=ALU.add),
                reads=[dtt, sb1], writes=[dtt], tag="sp_add")
            em.dma("sync", xcb[:, :], xcs_sp.ap()[128 * dtl:128 * (dtl + 1),
                                                  c0:c0 + 512],
                   reads=["xcs_sp"], writes=[xcb], tag="ld_xcs")
            em.op("scalar", lambda s: s.activation(out=xcf[:, :], in_=xcb[:, :],
                                                   func=AF.Copy),
                  reads=[xcb], writes=[xcf], tag="c_xcf")
            em.op("scalar", lambda s: s.activation(out=xcf[:, :], in_=xcf[:, :],
                                                   func=AF.Copy, scale=0.5),
                  reads=[xcf], writes=[xcf], tag="half_xc")
            em.op("vector", lambda v: v.tensor_tensor(
                out=dtx[:, :], in0=dtt[:, :], in1=xcf[:, :], op=ALU.mult),
                reads=[dtt, xcf], writes=[dtx], tag="dtx")
            for n in range(16):
                ps_b = cx.pp[3]
                ps_c = cx.pp[4]
                da_ = dA[n % 2]
                u_ = uu[n % 2]
                h_ = hh[n % 2]
                p_ = pb[n % 2]
                em.op("tensor", lambda t, n=n, c0=c0, ps_b=ps_b: t.matmul(
                    out=ps_b[:, :], lhsT=sel_s[:, 128 * n:128 * (n + 1)],
                    rhs=xdbB[:, c0:c0 + 512], start=True, stop=True),
                    reads=[sel_s, xdbB], writes=[ps_b], tag="bc_B")
                em.op("scalar", lambda s, n=n, dtl=dtl, da_=da_: s.activation(
                    out=da_[:, :], in_=dtt[:, :], func=AF.Exp,
                    scale=As_s[:, 16 * dtl + n:16 * dtl + n + 1]),
                    reads=[dtt, As_s], writes=[da_], tag="dA")
                em.op("vector", lambda v, u_=u_, ps_b=ps_b: v.tensor_tensor(
                    out=u_[:, :], in0=dtx[:, :], in1=ps_b[:, :], op=ALU.mult),
                    reads=[dtx, ps_b], writes=[u_], tag="u")
                em.op("vector", lambda v, da_=da_, u_=u_, h_=h_, dtl=dtl, n=n:
                      v.tensor_tensor_scan(
                          out=h_[:, :], data0=da_[:, :], data1=u_[:, :],
                          initial=carry[dtl][:, n:n + 1], op0=ALU.mult,
                          op1=ALU.add),
                      reads=[da_, u_, carry[dtl]], writes=[h_], tag="scan")
                em.op("vector", lambda v, h_=h_, dtl=dtl, n=n: v.tensor_copy(
                    out=carry[dtl][:, n:n + 1], in_=h_[:, 511:512]),
                    reads=[h_], writes=[carry[dtl]], tag="carry")
                em.op("tensor", lambda t, n=n, c0=c0, ps_c=ps_c: t.matmul(
                    out=ps_c[:, :], lhsT=sel_s[:, 2048 + 128 * n:2048 + 128 * (n + 1)],
                    rhs=xdbB[:, c0:c0 + 512], start=True, stop=True),
                    reads=[sel_s, xdbB], writes=[ps_c], tag="bc_C")
                em.op("vector", lambda v, h_=h_, p_=p_, ps_c=ps_c: v.tensor_tensor(
                    out=p_[:, :], in0=h_[:, :], in1=ps_c[:, :], op=ALU.mult),
                    reads=[h_, ps_c], writes=[p_], tag="p")
                em.op("tensor", lambda t, n=n, p_=p_: t.matmul(
                    out=cx.pp[5][:, :], lhsT=cx.ident[:, :], rhs=p_[:, :],
                    start=(n == 0), stop=(n == 15)),
                    reads=[cx.ident, p_], writes=[cx.pp[5]], tag="mm_y")
            em.dma("sync", zb[:, :], z_sp.ap()[128 * dtl:128 * (dtl + 1),
                                               c0:c0 + 512],
                   reads=["z_sp"], writes=[zb], tag="ld_z")
            em.op("scalar", lambda s: s.activation(out=zf[:, :], in_=zb[:, :],
                                                   func=AF.Copy),
                  reads=[zb], writes=[zf], tag="c_zf")
            em.op("scalar", lambda s: s.activation(out=sb1[:, :], in_=zf[:, :],
                                                   func=AF.Tanh, scale=0.5),
                  reads=[zf], writes=[sb1], tag="tanh_z")
            em.op("vector", lambda v: v.scalar_tensor_tensor(
                out=zf[:, :], in0=sb1[:, :], scalar=1.0, in1=zf[:, :],
                op0=ALU.add, op1=ALU.mult),
                reads=[sb1, zf], writes=[zf], tag="siluz")
            em.op("vector", lambda v, dtl=dtl: v.scalar_tensor_tensor(
                out=yv[:, :], in0=xcf[:, :], scalar=dsk_s[:, dtl:dtl + 1],
                in1=cx.pp[5][:, :], op0=ALU.mult, op1=ALU.add),
                reads=[xcf, dsk_s, cx.pp[5]], writes=[yv], tag="yv")
            em.op("vector", lambda v: v.tensor_tensor(
                out=yf[:, :], in0=yv[:, :], in1=zf[:, :], op=ALU.mult),
                reads=[yv, zf], writes=[yf], tag="yf")
            em.dma("gpsimd", yout[128 * dtl:128 * (dtl + 1), c0:c0 + 512],
                   yf[:, :], reads=[yf], tag="out")
    em.finalize()
    em.close()
    return nc


def build_mamba_c():
    nc = bass.Bass("TRN2", target_bir_lowering=False)
    em = Em(nc)
    yin = nc.dram_tensor("yin", [2048, TH], BF16, kind="ExternalInput").ap()
    ident_in = nc.dram_tensor("ident", [128, 128], BF16, kind="ExternalInput").ap()
    wot = nc.dram_tensor("wot", [2048, D], F32, kind="ExternalInput").ap()
    hout = nc.dram_tensor("hout", [D, TH], F32, kind="ExternalOutput").ap()
    cx = Ctx(em, ident_in)
    stage = em.sbuf("stage", (128, D), F32)
    wot_s = [em.sbuf(f"wot{k}", (128, D), BF16) for k in range(16)]
    for k in range(16):
        em.dma("sync", stage[:, :], wot[128 * k:128 * (k + 1), :],
               writes=[stage], tag="ld_wot")
        em.op("scalar", lambda s, k=k: s.activation(
            out=wot_s[k][:, :], in_=stage[:, :], func=AF.Copy, scale=0.5),
            reads=[stage], writes=[wot_s[k]], tag="c_wot")
    yc = [em.sbuf(f"yc{i}", (128, 512), BF16) for i in range(16)]
    ho = [em.sbuf(f"ho{i}", (128, 512), F32) for i in range(2)]
    for c in range(TH // 512):
        c0 = 512 * c
        for k in range(16):
            em.dma("sync", yc[k][:, :], yin[128 * k:128 * (k + 1), c0:c0 + 512],
                   writes=[yc[k]], tag="ld_y")
        for ot in range(8):
            ps = cx.pp[ot % 2]
            for k in range(16):
                em.op("tensor", lambda t, ot=ot, k=k, ps=ps: t.matmul(
                    out=ps[:, :], lhsT=wot_s[k][:, 128 * ot:128 * (ot + 1)],
                    rhs=yc[k][:, :], start=(k == 0), stop=(k == 15)),
                    reads=[wot_s[k], yc[k]], writes=[ps], tag="mm_op")
            hx = ho[ot % 2]
            em.op("vector", lambda v, ps=ps, hx=hx: v.tensor_copy(
                out=hx[:, :], in_=ps[:, :]),
                reads=[ps], writes=[hx], tag="ev_h")
            em.dma("gpsimd", hout[128 * ot:128 * (ot + 1), c0:c0 + 512],
                   hx[:, :], reads=[hx], tag="out")
    em.finalize()
    em.close()
    return nc


def build_mamba_f():
    nc = bass.Bass("TRN2", target_bir_lowering=False)
    em = Em(nc)
    hin = nc.dram_tensor("hin", [D, TH], F32, kind="ExternalInput").ap()
    rin = nc.dram_tensor("rin", [D, TH], F32, kind="ExternalInput").ap()
    xst = nc.dram_tensor("xst", [D, TH], F32, kind="ExternalInput").ap()
    ident_in = nc.dram_tensor("ident", [128, 128], BF16, kind="ExternalInput").ap()
    nfw = nc.dram_tensor("nfw", [128, 8], F32, kind="ExternalInput").ap()
    nfb = nc.dram_tensor("nfb", [128, 8], F32, kind="ExternalInput").ap()
    xout = nc.dram_tensor("xoT", [D, TH], F32, kind="ExternalOutput").ap()
    cx = Ctx(em, ident_in)
    lns = ln_scratch(em)
    nfw_s = em.sbuf("nfw", (128, 8), F32)
    nfb_s = em.sbuf("nfb", (128, 8), F32)
    em.dma("sync", nfw_s[:, :], nfw, writes=[nfw_s], tag="ld_nfw")
    em.dma("sync", nfb_s[:, :], nfb, writes=[nfb_s], tag="ld_nfb")
    gbc = [(nfw_s[:, i:i + 1], nfb_s[:, i:i + 1]) for i in range(8)]
    hc = [em.sbuf(f"hc{i}", (128, 512), F32) for i in range(8)]
    rcn = [em.sbuf(f"rc{i}", (128, 512), F32) for i in range(8)]
    lno = [em.sbuf(f"lno{i}", (128, 512), F32) for i in range(8)]
    xs = em.sbuf("xs", (128, 512), F32)
    for c in range(TH // 512):
        c0 = 512 * c
        for i in range(8):
            em.dma("sync", hc[i][:, :], hin[128 * i:128 * (i + 1), c0:c0 + 512],
                   writes=[hc[i]], tag="ld_h")
            em.dma("sync", rcn[i][:, :], rin[128 * i:128 * (i + 1), c0:c0 + 512],
                   writes=[rcn[i]], tag="ld_r")
            em.op("vector", lambda v, i=i: v.tensor_tensor(
                out=rcn[i][:, :], in0=rcn[i][:, :], in1=hc[i][:, :], op=ALU.add),
                reads=[rcn[i], hc[i]], writes=[rcn[i]], tag="hr")
        if c == 0:
            build_mamba_f._lns = ln_scratch(em)
        ln_feature_major(em, cx, rcn, 512, lno, 6, 7, gain_bias=gbc,
                         name=f"lnF{c}", scratch=build_mamba_f._lns)
        for i in range(8):
            em.dma("sync", xs[:, :], xst[128 * i:128 * (i + 1), c0:c0 + 512],
                   writes=[xs], tag="ld_xs")
            em.op("vector", lambda v, i=i: v.tensor_tensor(
                out=lno[i][:, :], in0=lno[i][:, :], in1=xs[:, :], op=ALU.add),
                reads=[lno[i], xs], writes=[lno[i]], tag="resid")
            em.dma("gpsimd", xout[128 * i:128 * (i + 1), c0:c0 + 512],
                   lno[i][:, :], reads=[lno[i]], tag="out")
    em.finalize()
    em.close()
    return nc


def _mamba_stage(X_T, norm_w, norm_b, in_proj_w, conv_w, conv_b, x_proj_w,
                 dt_proj_w, dt_proj_b, A_log, D_skip, out_proj_w,
                 normf_w, normf_b):
    NL = 4
    pairs = [(b, dh) for b in range(B) for dh in range(2)]
    H = X_T
    R = None
    for l in range(NL):
        nc_ab = _get_prog("m_ab", build_mamba_ab, l == 0)
        ims = []
        for (b, dh) in pairs:
            d0 = dh * 1024
            wpAl = np.ascontiguousarray(in_proj_w[l].T[:, d0:d0 + 1024])
            wpBl = np.ascontiguousarray(in_proj_w[l].T[:, 2048 + d0:2048 + d0 + 1024])
            wxl = np.ascontiguousarray(x_proj_w[l].T[d0:d0 + 1024, :])
            wdtl = np.ascontiguousarray(dt_proj_w[l].T[:, d0:d0 + 1024])
            pk8 = lambda a: np.ascontiguousarray(a.reshape(8, 128).T)
            sel = np.zeros((32, 32 * 128), np.float32)
            for j in range(32):
                sel[j, 128 * j:128 * (j + 1)] = 0.5 if j < 16 else 1.0
            wcpk = np.zeros((128, 32), np.float32)
            for dt_ in range(8):
                wcpk[:, 4 * dt_:4 * dt_ + 4] = conv_w[l][d0 + 128 * dt_:
                                                         d0 + 128 * (dt_ + 1), :]
            alogk = np.zeros((128, 128), np.float32)
            for dt_ in range(8):
                alogk[:, 16 * dt_:16 * (dt_ + 1)] = A_log[l][d0 + 128 * dt_:
                                                             d0 + 128 * (dt_ + 1), :]
            im = {"hin": np.ascontiguousarray(H[b]), "ident": _IDENT,
                  "wpA": wpAl, "wpB": wpBl, "wx": wxl, "wdt": wdtl,
                  "nw": pk8(norm_w[l]), "nb": pk8(norm_b[l]),
                  "wcp": wcpk, "wcb": pk8(conv_b[l][d0:d0 + 1024]),
                  "dtb": pk8(dt_proj_b[l][d0:d0 + 1024]),
                  "dsk": pk8(D_skip[l][d0:d0 + 1024]),
                  "alog": alogk, "sel": sel.astype(BF)}
            if l > 0:
                im["rin"] = np.ascontiguousarray(R[b])
            ims.append(im)
        res = run_bass_kernel_spmd(nc_ab, ims, core_ids=list(range(8)))
        Rn = np.empty((B, D, T), np.float32)
        YF = np.empty((B, 2048, T), dtype=BF)
        for ci, (b, dh) in enumerate(pairs):
            if dh == 0:
                Rn[b] = np.asarray(res.results[ci]["rout"])
            YF[b][1024 * dh:1024 * (dh + 1)] = np.asarray(res.results[ci]["yout"])
        R = Rn
        nc_c = _get_prog("m_c", build_mamba_c)
        wotl = np.ascontiguousarray(out_proj_w[l].T)
        ims = []
        for c in range(NCORES):
            b, th = c // 2, c % 2
            ims.append({"yin": np.ascontiguousarray(
                YF[b][:, th * TH:(th + 1) * TH]), "ident": _IDENT, "wot": wotl})
        res = run_bass_kernel_spmd(nc_c, ims, core_ids=list(range(NCORES)))
        Hn = np.empty((B, D, T), np.float32)
        for c in range(NCORES):
            b, th = c // 2, c % 2
            Hn[b][:, th * TH:(th + 1) * TH] = np.asarray(res.results[c]["hout"])
        H = Hn
    nc_f = _get_prog("m_f", build_mamba_f)
    ims = []
    for c in range(NCORES):
        b, th = c // 2, c % 2
        sl = slice(th * TH, (th + 1) * TH)
        ims.append({"hin": np.ascontiguousarray(H[b][:, sl]),
                    "rin": np.ascontiguousarray(R[b][:, sl]),
                    "xst": np.ascontiguousarray(X_T[b][:, sl]),
                    "ident": _IDENT,
                    "nfw": np.ascontiguousarray(normf_w.reshape(8, 128).T),
                    "nfb": np.ascontiguousarray(normf_b.reshape(8, 128).T)})
    res = run_bass_kernel_spmd(nc_f, ims, core_ids=list(range(NCORES)))
    X2 = np.empty_like(X_T)
    for c in range(NCORES):
        b, th = c // 2, c % 2
        X2[b][:, th * TH:(th + 1) * TH] = np.asarray(res.results[c]["xoT"])
    return X2



_IDENT = np.eye(128, dtype=BF)


def _pack_rail(rail_maker, rail_up_proj, w_cproj):
    bd1 = np.zeros((32, 128, 128), np.float32)
    for h in range(4):
        for g in range(8):
            for r in range(8):
                bd1[h * 8 + g, 16 * r:16 * r + 16, 16 * r:16 * r + 16] = \
                    rail_maker[h, 8 * g + r]
    bd1p = np.ascontiguousarray(bd1.transpose(1, 0, 2).reshape(128, 32 * 128))
    bd2p = np.zeros((128, 128 * 128), np.float32)
    for h in range(4):
        for g in range(8):
            hg = h * 8 + g
            for m in range(4):
                kt = hg * 4 + m
                for r2 in range(2):
                    r = 8 * g + 2 * m + r2
                    bd2p[32 * m + 16 * r2:32 * m + 16 * r2 + 16,
                         kt * 128 + 64 * r2:kt * 128 + 64 * r2 + 64] = \
                        rail_up_proj[h, r].T
    wcT = np.zeros((16384, 1024), np.float32)
    wc = np.ascontiguousarray(w_cproj)
    for h in range(4):
        for g in range(8):
            for m in range(4):
                kt = (h * 8 + g) * 4 + m
                src0 = h * 4096 + (8 * g + 2 * m) * 64
                wcT[128 * kt:128 * kt + 128, :] = wc[:, src0:src0 + 128].T
    return bd1p, bd2p, wcT


_prog_cache = {}


def _get_prog(name, builder, *args):
    key = (name,) + args
    if key not in _prog_cache:
        _prog_cache[key] = builder(*args)
    return _prog_cache[key]


def _rail_stage(X_T, rail_maker, rail_up_proj, w_cproj, final):
    bd1p, bd2p, wcT = _pack_rail(rail_maker, rail_up_proj, w_cproj)
    nc = _get_prog("rail", build_rail, bool(final))
    in_maps = []
    for c in range(NCORES):
        b, th = c // 2, c % 2
        in_maps.append({
            "xT": np.ascontiguousarray(X_T[b][:, th * TH:(th + 1) * TH]),
            "ident": _IDENT, "bd1": bd1p, "bd2": bd2p, "wcT": wcT,
        })
    res = run_bass_kernel_spmd(nc, in_maps, core_ids=list(range(NCORES)))
    X_T2 = np.empty_like(X_T)
    Xout = np.empty((B, T, D), np.float32) if final else None
    for c in range(NCORES):
        b, th = c // 2, c % 2
        X_T2[b][:, th * TH:(th + 1) * TH] = np.asarray(res.results[c]["xoT"])
        if final:
            Xout[b][th * TH:(th + 1) * TH, :] = np.asarray(res.results[c]["xo"])
    return X_T2, Xout


def kernel(**inputs):
    gi = lambda n: np.asarray(inputs[n], np.float32)
    x = gi("L0_attn_hidden_states")
    X_T = np.ascontiguousarray(x.transpose(0, 2, 1))
    ffn_args = (gi("L0_ffn_wqkv"), gi("L0_ffn_q_scale"), gi("L0_ffn_k_scale"),
                gi("L0_ffn_wo"), gi("L0_ffn_ln_g"), gi("L0_ffn_ln_b"),
                gi("L0_ffn_w1"), gi("L0_ffn_b1"), gi("L0_ffn_w2"),
                gi("L0_ffn_b2"))
    mamba_args = (gi("L0_attn_norm_w"), gi("L0_attn_norm_b"),
                  gi("L0_attn_in_proj_w"), gi("L0_attn_conv_w"),
                  gi("L0_attn_conv_b"), gi("L0_attn_x_proj_w"),
                  gi("L0_attn_dt_proj_w"), gi("L0_attn_dt_proj_b"),
                  gi("L0_attn_A_log"), gi("L0_attn_D_skip"),
                  gi("L0_attn_out_proj_w"), gi("L0_attn_normf_w"),
                  gi("L0_attn_normf_b"))
    rail_args = (gi("L1_ffn_rail_maker"), gi("L1_ffn_rail_up_proj"),
                 gi("L1_ffn_w_cproj"))
    alibi_args = (gi("L1_attn_qkv_w"), gi("L1_attn_qkv_b"),
                  gi("L1_attn_out_w"), gi("L1_attn_out_b"))
    X_T = _ffn_stage(X_T, *ffn_args)
    X_T = _mamba_stage(X_T, *mamba_args)
    X_T = _ffn_stage(X_T, *ffn_args)
    X_T, _ = _rail_stage(X_T, *rail_args, False)
    X_T = _alibi_stage(X_T, *alibi_args)
    X_T, Xout = _rail_stage(X_T, *rail_args, True)
    return Xout


# version 02
# speedup vs baseline: 1.2029x; 1.2029x over previous
import math
import numpy as np
import ml_dtypes
import concourse.bass as bass
import concourse.mybir as mybir
from concourse.bass_utils import run_bass_kernel_spmd
from contextlib import ExitStack

F32 = mybir.dt.float32
BF16 = mybir.dt.bfloat16
AF = mybir.ActivationFunctionType
ALU = mybir.AluOpType
AX = mybir.AxisListType
BF = ml_dtypes.bfloat16

B, T, D = 4, 4096, 1024
TH = T // 2
NCORES = 8


QUEUES = ("tensor", "vector", "scalar", "gpsimd", "sync")


class _Op:
    __slots__ = ("queue", "fn", "kind", "deps", "done", "need_inc", "pos",
                 "tag", "wkey")

    def __init__(self, queue, fn, kind, tag=""):
        self.queue = queue
        self.fn = fn
        self.kind = kind
        self.deps = set()
        self.done = None
        self.need_inc = kind in ("dma", "cc")
        self.pos = None
        self.tag = tag


class Em:
    def __init__(self, nc):
        self.nc = nc
        self.es = ExitStack()
        self._cms = {}
        self._alloc_idx = 0
        self._guard = set()
        self._guard_pending = False
        self._guarded_new = set()
        self.qops = {q: [] for q in QUEUES}
        self.all_ops = []
        self.last_writer = {}
        self.readers = {}
        self.sbuf_bytes = 0
        self._names = {}

    def _uname(self, name):
        n = self._names.get(name, 0)
        self._names[name] = n + 1
        return name if n == 0 else f"{name}_{n}"

    def sbuf(self, name, shape, dtype):
        cm = self.nc.sbuf_tensor(self._uname("s_" + name), list(shape), dtype)
        h = cm.__enter__()
        self._alloc_idx += 1
        self._cms[self._key(h)] = (cm, self._alloc_idx)
        self.sbuf_bytes += int(np.prod(shape[1:])) * (2 if dtype == BF16 else 4)
        return h

    def free(self, *handles):
        hs = sorted(handles, key=lambda h: self._cms[self._key(h)][1],
                    reverse=True)
        for h in hs:
            k = self._key(h)
            cm, _ = self._cms.pop(k)
            guard = set(self.readers.get(k, ()))
            w = self.last_writer.get(k)
            if w is not None:
                guard.add(w)
            self._guard |= guard
            cm.__exit__(None, None, None)
        self._guarded_new = set(self._cms.keys())

    def psum(self, name, shape=(128, 512), dtype=F32):
        return self.es.enter_context(
            self.nc.psum_tensor(self._uname("p_" + name), list(shape), dtype))

    def sem(self, name):
        return self.es.enter_context(self.nc.semaphore(name))

    def op(self, queue, fn, reads=(), writes=(), tag=""):
        o = _Op(queue, fn, "compute", tag)
        self._record(o, reads, writes)
        return o

    def dma(self, queue, out, in_, reads=(), writes=(), tag="dma", **kw):
        fn = lambda e: e.dma_start(out=out, in_=in_, **kw)
        o = _Op(queue, fn, "dma", tag)
        o.wkey = self._key(writes[0]) if writes else ("dmaout", tag)
        self._record(o, reads, writes)
        return o

    def cc(self, kind, op_, groups, ins, outs, reads=(), writes=(), tag="cc"):
        def fn(e):
            return e.collective_compute(kind, op_, replica_groups=groups,
                                        ins=ins, outs=outs)
        o = _Op("gpsimd", fn, "cc", tag)
        o.wkey = self._key(writes[0]) if writes else ("ccout", tag)
        self._record(o, reads, writes)
        return o

    def _key(self, r):
        return r if isinstance(r, (str, tuple)) else r.name

    def _record(self, o, reads, writes):
        o.pos = len(self.all_ops)
        if self._guard:
            for r in writes:
                k = self._key(r)
                if k in self._cms and k not in self._guarded_new:
                    self._guarded_new.add(k)
                    o.deps |= self._guard
        for r in reads:
            k = self._key(r)
            w = self.last_writer.get(k)
            if w is not None:
                o.deps.add(w)
            self.readers.setdefault(k, []).append(o)
        for r in writes:
            k = self._key(r)
            w = self.last_writer.get(k)
            if w is not None:
                o.deps.add(w)
            for rd in self.readers.get(k, ()):
                if rd is not o:
                    o.deps.add(rd)
            self.readers[k] = []
            self.last_writer[k] = o
        self.all_ops.append(o)
        self.qops[o.queue].append(o)

    def finalize(self):
        nc = self.nc
        for o in self.all_ops:
            for d in o.deps:
                d.need_inc = True
        csem = {q: self.sem(f"c_{q}") for q in QUEUES}
        ccount = {q: 0 for q in QUEUES}
        dsem_map = {}
        dcnt_map = {}
        nsem = 0
        for o in self.all_ops:
            if o.kind == "dma" or o.kind == "cc":
                k = o.wkey
                if k not in dsem_map:
                    dsem_map[k] = self.sem(f"ds{nsem}")
                    dcnt_map[k] = 0
                    nsem += 1
                inc = 16 if o.kind == "dma" else 1
                dcnt_map[k] += inc
                o.done = (dsem_map[k], dcnt_map[k], inc)
        for q in QUEUES:
            for o in self.qops[q]:
                if o.kind == "compute" and o.need_inc:
                    ccount[q] += 1
                    o.done = (csem[q], ccount[q], 1)
        waits = {}
        for o in self.all_ops:
            w = {}
            for d in o.deps:
                sem, val, _ = d.done
                if w.get(id(sem), (None, 0))[1] < val:
                    w[id(sem)] = (sem, val)
            waits[id(o)] = list(w.values())
        final_outs = [o for o in self.all_ops if o.kind == "dma" and o.tag == "out"]
        with nc.Block() as block:
            for q in QUEUES:
                ops = self.qops[q]
                if not ops and q != "gpsimd":
                    continue
                sect = getattr(block, q)

                def make(ops=ops, q=q):
                    def _(eng):
                        mark = {}
                        for o in ops:
                            for sem, val in waits[id(o)]:
                                if mark.get(id(sem), 0) < val:
                                    eng.wait_ge(sem, val)
                                    mark[id(sem)] = val
                            instr = o.fn(eng)
                            if o.done is not None:
                                sem, _v, inc = o.done
                                instr.then_inc(sem, inc)
                        if q == "gpsimd":
                            for o in final_outs:
                                sem, val, _ = o.done
                                if mark.get(id(sem), 0) < val:
                                    eng.wait_ge(sem, val)
                                    mark[id(sem)] = val
                    return _

                sect(make())

    def close(self):
        self.es.close()



class Ctx:

    def __init__(self, em, ident_in, nbanks=8):
        self.em = em
        self.pp = [em.psum(f"pp{i}") for i in range(nbanks)]
        self.ident = em.sbuf("ident", (128, 128), BF16)
        self.ones_col = em.sbuf("ones_col", (1, 128), F32)
        em.op("gpsimd", lambda g: g.memset(self.ones_col[:, :], 1.0),
              writes=[self.ones_col], tag="memset_ones")
        self.ones_red = em.sbuf("ones_red", (128, 1), BF16)
        em.op("gpsimd", lambda g: g.memset(self.ones_red[:, :], 1.0),
              writes=[self.ones_red], tag="memset_onesr")
        em.dma("sync", self.ident[:, :], ident_in, writes=[self.ident],
               tag="ld_ident")


def ln_scratch(em, name="lns"):
    return {"sq": [em.sbuf(f"{name}_sq", (128, 512), BF16) for _ in range(2)],
            "xb": [em.sbuf(f"{name}_xb", (128, 512), BF16) for _ in range(2)],
            "rows": em.sbuf(f"{name}_rows", (1, 3 * 512), F32),
            "tmp": em.sbuf(f"{name}_tmp", (128, 512), F32)}


def ln_feature_major(em, cx, x_tiles, t_len, out_tiles, bank_a, bank_b,
                     eps=1e-5, gain_bias=None, name="ln", x_off=0, o_off=0,
                     scratch=None, x_offs=None):
    nd = len(x_tiles)
    Dm = nd * 128
    if x_offs is None:
        x_offs = [x_off] * nd
    if scratch is None:
        scratch = ln_scratch(em, name)
    sq = scratch["sq"]
    xb = scratch["xb"]
    rows = scratch["rows"]
    stats = rows[0:1, 0:512]
    stats2 = rows[0:1, 512:1024]
    musq = rows[0:1, 1024:1536]
    mu, rstd, mur = stats, stats2, musq
    tmp = scratch["tmp"]
    ps, ps2 = cx.pp[bank_a], cx.pp[bank_b]
    R = rows
    for c0 in range(0, t_len, 512):
        cl = min(512, t_len - c0)
        for i, xt in enumerate(x_tiles):
            xs, xe = x_offs[i] + c0, x_offs[i] + c0 + cl
            sqi, xbi = sq[i % 2], xb[i % 2]
            em.op("scalar", lambda s, xt=xt, xs=xs, xe=xe, cl=cl, xbi=xbi:
                  s.activation(out=xbi[:, :cl], in_=xt[:, xs:xe], func=AF.Copy),
                  reads=[xt], writes=[xbi], tag=f"{name}_cp")
            em.op("scalar", lambda s, xt=xt, xs=xs, xe=xe, cl=cl, sqi=sqi:
                  s.activation(out=sqi[:, :cl], in_=xt[:, xs:xe], func=AF.Square),
                  reads=[xt], writes=[sqi], tag=f"{name}_sq")
            em.op("tensor", lambda t, i=i, cl=cl, xbi=xbi: t.matmul(
                out=ps[0:1, :cl], lhsT=cx.ones_red[:, :], rhs=xbi[:, :cl],
                start=(i == 0), stop=(i == nd - 1)),
                reads=[xbi, cx.ones_red], writes=[ps], tag=f"{name}_mm1")
            em.op("tensor", lambda t, i=i, cl=cl, sqi=sqi: t.matmul(
                out=ps2[0:1, :cl], lhsT=cx.ones_red[:, :], rhs=sqi[:, :cl],
                start=(i == 0), stop=(i == nd - 1)),
                reads=[sqi, cx.ones_red], writes=[ps2], tag=f"{name}_mm2")
        em.op("vector", lambda v, cl=cl: v.tensor_copy(
            out=stats[:, :cl], in_=ps[0:1, :cl]),
            reads=[ps], writes=[R], tag=f"{name}_ev1")
        em.op("vector", lambda v, cl=cl: v.tensor_copy(
            out=stats2[:, :cl], in_=ps2[0:1, :cl]),
            reads=[ps2], writes=[R], tag=f"{name}_ev2")
        em.op("scalar", lambda s, cl=cl: s.activation(
            out=mu[:, :cl], in_=stats[:, :cl], func=AF.Copy, scale=1.0 / Dm),
            reads=[R], writes=[R], tag=f"{name}_mu")
        em.op("scalar", lambda s, cl=cl: s.activation(
            out=musq[:, :cl], in_=mu[:, :cl], func=AF.Square),
            reads=[R], writes=[R], tag=f"{name}_musq")
        em.op("vector", lambda v, cl=cl: v.tensor_scalar(
            out=stats2[:, :cl], in0=stats2[:, :cl], scalar1=1.0 / Dm,
            scalar2=eps, op0=ALU.mult, op1=ALU.add),
            reads=[R], writes=[R], tag=f"{name}_var")
        em.op("vector", lambda v, cl=cl: v.tensor_tensor(
            out=stats2[:, :cl], in0=stats2[:, :cl], in1=musq[:, :cl],
            op=ALU.subtract),
            reads=[R], writes=[R], tag=f"{name}_var2")
        em.op("scalar", lambda s, cl=cl: s.activation(
            out=stats2[:, :cl], in_=stats2[:, :cl], func=AF.Ln),
            reads=[R], writes=[R], tag=f"{name}_lnv")
        em.op("scalar", lambda s, cl=cl: s.activation(
            out=rstd[:, :cl], in_=stats2[:, :cl], func=AF.Exp, scale=-0.5),
            reads=[R], writes=[R], tag=f"{name}_rstd")
        em.op("vector", lambda v, cl=cl: v.tensor_tensor(
            out=mur[:, :cl], in0=mu[:, :cl], in1=rstd[:, :cl], op=ALU.mult),
            reads=[R], writes=[R], tag=f"{name}_mur")
        em.op("tensor", lambda t, cl=cl: t.matmul(
            out=ps[:, :cl], lhsT=cx.ones_col[:, :], rhs=rstd[:, :cl],
            start=True, stop=True),
            reads=[R, cx.ones_col], writes=[ps], tag=f"{name}_bc1")
        em.op("tensor", lambda t, cl=cl: t.matmul(
            out=ps2[:, :cl], lhsT=cx.ones_col[:, :], rhs=mur[:, :cl],
            start=True, stop=True),
            reads=[R, cx.ones_col], writes=[ps2], tag=f"{name}_bc2")
        for i, xt in enumerate(x_tiles):
            ot = out_tiles[i]
            xs, xe = x_offs[i] + c0, x_offs[i] + c0 + cl
            os_, oe = o_off + c0, o_off + c0 + cl
            em.op("vector", lambda v, xt=xt, xs=xs, xe=xe, cl=cl: v.tensor_tensor(
                out=tmp[:, :cl], in0=xt[:, xs:xe], in1=ps[:, :cl], op=ALU.mult),
                reads=[xt, ps], writes=[tmp], tag=f"{name}_n1")
            if gain_bias is None:
                em.op("vector", lambda v, ot=ot, os_=os_, oe=oe, cl=cl:
                      v.tensor_tensor(out=ot[:, os_:oe], in0=tmp[:, :cl],
                                      in1=ps2[:, :cl], op=ALU.subtract),
                      reads=[tmp, ps2], writes=[ot], tag=f"{name}_n2")
            else:
                g_col, b_col = gain_bias[i]
                em.op("vector", lambda v, cl=cl: v.tensor_tensor(
                    out=tmp[:, :cl], in0=tmp[:, :cl], in1=ps2[:, :cl],
                    op=ALU.subtract),
                    reads=[tmp, ps2], writes=[tmp], tag=f"{name}_n2")
                em.op("vector", lambda v, ot=ot, os_=os_, oe=oe, cl=cl,
                      g_col=g_col, b_col=b_col: v.tensor_scalar(
                          out=ot[:, os_:oe], in0=tmp[:, :cl],
                          scalar1=g_col[:, 0:1], scalar2=b_col[:, 0:1],
                          op0=ALU.mult, op1=ALU.add),
                      reads=[tmp, g_col, b_col, R], writes=[ot], tag=f"{name}_gb")



def build_rail(final_output):
    nc = bass.Bass("TRN2", target_bir_lowering=False)
    em = Em(nc)
    xin = nc.dram_tensor("xT", [D, TH], F32, kind="ExternalInput").ap()
    ident_in = nc.dram_tensor("ident", [128, 128], BF16, kind="ExternalInput").ap()
    bd1 = nc.dram_tensor("bd1", [128, 32 * 128], F32, kind="ExternalInput").ap()
    bd2 = nc.dram_tensor("bd2", [128, 128 * 128], F32, kind="ExternalInput").ap()
    wcT = nc.dram_tensor("wcT", [16384, D], F32, kind="ExternalInput").ap()
    xout = nc.dram_tensor("xoT", [D, TH], F32, kind="ExternalOutput").ap()
    if final_output:
        xout_tm = nc.dram_tensor("xo", [TH, D], F32, kind="ExternalOutput").ap()

    cx = Ctx(em, ident_in)
    xT = [em.sbuf(f"xT{i}", (128, TH), F32) for i in range(8)]
    xn = [em.sbuf(f"xn{i}", (128, TH), BF16) for i in range(8)]
    for i in range(8):
        em.dma("sync", xT[i][:, :], xin[128 * i:128 * (i + 1), :],
               writes=[xT[i]], tag="ld_x")
    ln_feature_major(em, cx, xT, TH, xn, 6, 7, name="ln")

    stage_f = em.sbuf("stage_f", (128, 4096), F32)
    bd1_s = em.sbuf("bd1s", (128, 32 * 128), BF16)
    bd2_s = em.sbuf("bd2s", (128, 128 * 128), BF16)
    em.dma("sync", stage_f[:, :], bd1, writes=[stage_f], tag="ld_bd1")
    em.op("vector", lambda v: v.tensor_copy(out=bd1_s[:, :], in_=stage_f[:, :]),
          reads=[stage_f], writes=[bd1_s], tag="cast_bd1")
    for q4 in range(4):
        em.dma("sync", stage_f[:, :], bd2[:, 4096 * q4:4096 * (q4 + 1)],
               writes=[stage_f], tag="ld_bd2")
        em.op("vector", lambda v, q4=q4: v.tensor_copy(
            out=bd2_s[:, 4096 * q4:4096 * (q4 + 1)], in_=stage_f[:, :]),
            reads=[stage_f], writes=[bd2_s], tag="cast_bd2")

    NWG = 16
    wg_s = [em.sbuf(f"wg{i}", (128, D), BF16) for i in range(NWG)]
    wg_f = [stage_f[:, 0:1024], stage_f[:, 1024:2048]]
    wg_fk = ["stage_f", "stage_f"]
    ytile = [em.sbuf("ytile", (128, 256), BF16) for _ in range(2)]
    ztile = [em.sbuf("ztile", (128, 256), BF16) for _ in range(2)]
    TC = 256
    for kg in range(8):
        for j in range(NWG):
            kt = NWG * kg + j
            wf = wg_f[j % 2]
            em.dma("sync", wf, wcT[128 * kt:128 * (kt + 1), :],
                   writes=["stage_f"], tag="ld_wc")
            em.op("scalar", lambda s, j=j, wf=wf: s.activation(
                out=wg_s[j][:, :], in_=wf, func=AF.Copy),
                reads=["stage_f"], writes=[wg_s[j]], tag="cast_wc")
        for c0 in range(0, TH, TC):
            for j in range(NWG):
                kt = NWG * kg + j
                hg, m = kt // 4, kt % 4
                yt = ytile[(kt // 4) % 2]
                zt = ztile[j % 2]
                if m == 0:
                    em.op("tensor", lambda t, hg=hg, c0=c0: t.matmul(
                        out=cx.pp[4][:, :TC], lhsT=bd1_s[:, hg * 128:(hg + 1) * 128],
                        rhs=xn[hg % 8][:, c0:c0 + TC], start=True, stop=True),
                        reads=[bd1_s, xn[hg % 8]], writes=[cx.pp[4]], tag="mm_y")
                    em.op("vector", lambda v, yt=yt: v.tensor_copy(
                        out=yt[:, :], in_=cx.pp[4][:, :TC]),
                        reads=[cx.pp[4]], writes=[yt], tag="ev_y")
                em.op("tensor", lambda t, kt=kt, yt=yt: t.matmul(
                    out=cx.pp[5][:, :TC],
                    lhsT=bd2_s[:, kt * 128:(kt + 1) * 128],
                    rhs=yt[:, :], start=True, stop=True),
                    reads=[bd2_s, yt], writes=[cx.pp[5]], tag="mm_z")
                em.op("scalar", lambda s, zt=zt: s.activation(
                    out=zt[:, :], in_=cx.pp[5][:, :TC], func=AF.Gelu),
                    reads=[cx.pp[5]], writes=[zt], tag="gelu")
                for o in range(8):
                    po = cx.pp[o // 2]
                    oc = (o % 2) * TC
                    em.op("tensor", lambda t, j=j, o=o, po=po, oc=oc, zt=zt: t.matmul(
                        out=po[:, oc:oc + TC], lhsT=wg_s[j][:, 128 * o:128 * (o + 1)],
                        rhs=zt[:, :], start=(j == 0), stop=(j == NWG - 1)),
                        reads=[wg_s[j], zt], writes=[po], tag="mm_o")
            scale = 0.5
            for o in range(8):
                po = cx.pp[o // 2]
                oc = (o % 2) * TC
                em.op("vector", lambda v, o=o, po=po, oc=oc, c0=c0: v.scalar_tensor_tensor(
                    out=xT[o][:, c0:c0 + TC], in0=po[:, oc:oc + TC], scalar=scale,
                    in1=xT[o][:, c0:c0 + TC], op0=ALU.mult, op1=ALU.add),
                    reads=[po, xT[o]], writes=[xT[o]], tag="resid")
    for i in range(8):
        em.dma("gpsimd", xout[128 * i:128 * (i + 1), :], xT[i][:, :],
               reads=[xT[i]], tag="out")
    if final_output:
        identf = em.sbuf("identf", (128, 128), F32)
        ttile = [em.sbuf("ttile", (128, 128), F32) for _ in range(2)]
        em.op("vector", lambda v: v.tensor_copy(out=identf[:, :],
                                                in_=cx.ident[:, :]),
              reads=[cx.ident], writes=[identf], tag="identf")
        for i in range(8):
            for c0 in range(0, TH, 128):
                tt = ttile[(c0 // 128) % 2]
                em.op("tensor", lambda t, i=i, c0=c0: t.transpose(
                    out=cx.pp[6][:, :128], in_=xT[i][:, c0:c0 + 128],
                    identity=identf[:, :]),
                    reads=[xT[i], identf], writes=[cx.pp[6]], tag="tr")
                em.op("vector", lambda v, tt=tt: v.tensor_copy(
                    out=tt[:, :], in_=cx.pp[6][:, :128]),
                    reads=[cx.pp[6]], writes=[tt], tag="tr_ev")
                em.dma("gpsimd", xout_tm[c0:c0 + 128, 128 * i:128 * (i + 1)],
                       tt[:, :], reads=[tt], tag="out")
    em.finalize()
    em.close()
    return nc



def build_alibi_th(th):
    t0 = th * TH
    hi_default = (th == 0)
    nc = bass.Bass("TRN2", target_bir_lowering=False)
    em = Em(nc)
    xin = nc.dram_tensor("xTf", [D, T], F32, kind="ExternalInput").ap()
    ident_in = nc.dram_tensor("ident", [128, 128], BF16, kind="ExternalInput").ap()
    qkvT = nc.dram_tensor("qkvT", [D, 3 * D], F32, kind="ExternalInput").ap()
    qkvb = nc.dram_tensor("qkvb", [128, 24], F32, kind="ExternalInput").ap()
    outwT = nc.dram_tensor("outwT", [D, D], F32, kind="ExternalInput").ap()
    outb = nc.dram_tensor("outb", [128, 8], F32, kind="ExternalInput").ap()
    kxl = nc.dram_tensor("kxl", [16, 5, T], BF16, kind="ExternalInput").ap()
    kxn = nc.dram_tensor("kxn", [16, 5, T], BF16, kind="ExternalInput").ap()
    qx = nc.dram_tensor("qx", [5, T], BF16, kind="ExternalInput").ap()
    fcor = nc.dram_tensor("fcor", [16, 128, 4 * 512], F32, kind="ExternalInput").ap()
    xout = nc.dram_tensor("xoT", [D, TH], F32, kind="ExternalOutput").ap()
    qhb = nc.dram_tensor("qhb", [16, 64, TH], BF16).ap()
    khb = nc.dram_tensor("khb", [16, 64, T], BF16).ap()
    vhb = nc.dram_tensor("vhb", [32, 128, 8, 130], BF16).ap()

    cx = Ctx(em, ident_in, nbanks=7)
    ptb = em.psum("ptb", (128, 1024), BF16)
    lns = ln_scratch(em)
    stage = em.sbuf("stage", (128, 3072), F32)
    qkvT_s = [em.sbuf(f"qkvT{k}", (128, 3 * D), BF16) for k in range(8)]
    outwT_s = [em.sbuf(f"outwT{k}", (128, D), BF16) for k in range(8)]
    for k in range(8):
        em.dma("sync", stage[:, :], qkvT[128 * k:128 * (k + 1), :],
               writes=[stage], tag="ld_wq")
        em.op("vector", lambda v, k=k: v.tensor_copy(out=qkvT_s[k][:, :],
                                                     in_=stage[:, :]),
              reads=[stage], writes=[qkvT_s[k]], tag="cast_wq")
    for k in range(8):
        em.dma("sync", stage[:, 0:D], outwT[128 * k:128 * (k + 1), :],
               writes=[stage], tag="ld_wo")
        em.op("vector", lambda v, k=k: v.tensor_copy(out=outwT_s[k][:, :],
                                                     in_=stage[:, 0:D]),
              reads=[stage], writes=[outwT_s[k]], tag="cast_wo")
    qb_s = em.sbuf("qb", (128, 24), F32)
    qbq_s = em.sbuf("qbq", (128, 8), F32)
    ob_s = em.sbuf("ob", (128, 8), F32)
    em.dma("sync", qb_s[:, :], qkvb, writes=[qb_s], tag="ld_qb")
    em.dma("sync", ob_s[:, :], outb, writes=[ob_s], tag="ld_ob")
    em.op("vector", lambda v: v.tensor_scalar(out=qbq_s[:, :], in0=qb_s[:, 0:8],
                                              scalar1=0.125, scalar2=0.0,
                                              op0=ALU.mult, op1=ALU.add),
          reads=[qb_s], writes=[qbq_s], tag="qb_scale")

    xc = [em.sbuf(f"xc{i}", (128, 512), F32) for i in range(8)]
    xn = [em.sbuf(f"xnc{i}", (128, 512), BF16) for i in range(8)]
    qkvc = [em.sbuf(f"qkvc{i}", (128, 512), BF16) for i in range(2)]
    vstg = [em.sbuf(f"vstg{i}", (128, 130), BF16) for i in range(2)]
    for i in range(2):
        em.op("gpsimd", lambda g, i=i: g.memset(vstg[i][:, 64:65], 1.0),
              writes=[("vset", i)], tag="ones_v")
        em.op("gpsimd", lambda g, i=i: g.memset(vstg[i][:, 129:130], 1.0),
              writes=[("vset", i)], tag="ones_v")
    for c in range(8):
        c0 = 512 * c
        own = (c0 >= t0) and (c0 < t0 + TH)
        for i in range(8):
            em.dma("sync", xc[i][:, :], xin[128 * i:128 * (i + 1), c0:c0 + 512],
                   writes=[xc[i]], tag="ld_xc")
        ln_feature_major(em, cx, xc, 512, xn, 5, 6, name=f"lnA{c}", scratch=lns)
        for ot in range(24):
            kind = ot // 8
            if kind == 0 and not own:
                continue
            ps = cx.pp[ot % 2]
            for k in range(8):
                em.op("tensor", lambda t, ot=ot, k=k, ps=ps: t.matmul(
                    out=ps[:, :], lhsT=qkvT_s[k][:, 128 * ot:128 * (ot + 1)],
                    rhs=xn[k][:, :], start=(k == 0), stop=(k == 7)),
                    reads=[qkvT_s[k], xn[k]], writes=[ps], tag="mm_qkv")
            qv = qkvc[ot % 2]
            if kind == 0:
                em.op("vector", lambda v, ot=ot, ps=ps, qv=qv: v.tensor_scalar(
                    out=qv[:, :], in0=ps[:, :], scalar1=0.125,
                    scalar2=qbq_s[:, ot:ot + 1], op0=ALU.mult, op1=ALU.add),
                    reads=[ps, qbq_s], writes=[qv], tag="ev_q")
                loc = c0 - t0
                em.dma("gpsimd", qhb[2 * ot, :, loc:loc + 512], qv[0:64, :],
                       reads=[qv], writes=["qhb"], tag="sp_q")
                em.dma("gpsimd", qhb[2 * ot + 1, :, loc:loc + 512], qv[64:128, :],
                       reads=[qv], writes=["qhb"], tag="sp_q")
            elif kind == 1:
                em.op("vector", lambda v, ot=ot, ps=ps, qv=qv: v.tensor_scalar(
                    out=qv[:, :], in0=ps[:, :], scalar1=qb_s[:, ot:ot + 1],
                    scalar2=0.0, op0=ALU.add, op1=ALU.add),
                    reads=[ps, qb_s], writes=[qv], tag="ev_k")
                hh = 2 * (ot - 8)
                em.dma("gpsimd", khb[hh, :, c0:c0 + 512], qv[0:64, :],
                       reads=[qv], writes=["khb"], tag="sp_k")
                em.dma("gpsimd", khb[hh + 1, :, c0:c0 + 512], qv[64:128, :],
                       reads=[qv], writes=["khb"], tag="sp_k")
            else:
                em.op("vector", lambda v, ot=ot, ps=ps, qv=qv: v.tensor_scalar(
                    out=qv[:, :], in0=ps[:, :], scalar1=qb_s[:, ot:ot + 1],
                    scalar2=0.0, op0=ALU.add, op1=ALU.add),
                    reads=[ps, qb_s], writes=[qv], tag="ev_v")
                hp = ot - 16
                for s4 in range(4):
                    jt = 4 * c + s4
                    pth = 512 * (s4 % 2)
                    em.op("tensor", lambda t, qv=qv, s4=s4, pth=pth: t.transpose(
                        out=ptb[:, pth:pth + 128],
                        in_=qv[:, 128 * s4:128 * (s4 + 1)],
                        identity=cx.ident[:, :]),
                        reads=[qv, cx.ident], writes=[ptb], tag="tr_v")
                    vs = vstg[s4 % 2]
                    em.op("vector", lambda v, pth=pth, vs=vs: v.tensor_copy(
                        out=vs[:, 0:64], in_=ptb[:, pth:pth + 64]),
                        reads=[ptb], writes=[vs], tag="ev_v1")
                    em.op("vector", lambda v, pth=pth, vs=vs: v.tensor_copy(
                        out=vs[:, 65:129], in_=ptb[:, pth + 64:pth + 128]),
                        reads=[ptb], writes=[vs], tag="ev_v2")
                    em.dma("gpsimd", vhb[jt, :, hp, :], vs[:, :],
                           reads=[vs, ("vset", 0), ("vset", 1)], writes=["vhb"],
                           tag="sp_v")

    qhat = [em.sbuf(f"qhat{i}", (69, 512), BF16) for i in range(4)]
    khat8 = [em.sbuf(f"khat8{i}", (69, 512), BF16) for i in range(2)]
    kneg8 = [em.sbuf(f"kneg8{i}", (69, 512), BF16) for i in range(2)]
    vt4 = [em.sbuf(f"vt4{i}", (128, 4 * 130), BF16) for i in range(2)]
    att = [em.sbuf(f"att{i}", (128, 512), BF16) for i in range(4)]
    fc_s = [em.sbuf(f"fc{i}", (128, 4 * 512), F32) for i in range(2)]
    aoT = [em.sbuf(f"aoT{i}", (128, TH), BF16) for i in range(8)]
    rrow = em.sbuf("rrow", (1, 512), F32)
    rb = em.sbuf("rb", (64, 512), F32)
    nb8 = em.sbuf("nb8", (128, 1), F32)
    em.op("gpsimd", lambda g: g.memset(nb8[:, :], -8.0), writes=[nb8],
          tag="set_nb8")
    PSO = [0, 1, 4, 5]
    for h in range(16):
        hp, hs = h // 2, h % 2
        fcs = fc_s[h % 2]
        em.dma("sync", fcs[:, :], fcor[h], writes=[fcs], tag="ld_fc")
        for qc in range(4):
            qh = qhat[qc]
            em.dma("sync", qh[0:64, :], qhb[h, :, 512 * qc:512 * (qc + 1)],
                   reads=["qhb"], writes=[qh], tag="ld_qh")
            em.dma("sync", qh[64:69, :], qx[:, t0 + 512 * qc:t0 + 512 * (qc + 1)],
                   writes=[qh], tag="ld_qx")
        for jb in range(8):
            ka = khat8[jb % 2]
            kn = kneg8[jb % 2]
            vt = vt4[jb % 2]
            jB = 512 * jb
            em.dma("sync", ka[0:64, :], khb[h, :, jB:jB + 512],
                   reads=["khb"], writes=[ka], tag="ld_kh")
            em.dma("sync", ka[64:69, :], kxl[h, :, jB:jB + 512],
                   writes=[ka], tag="ld_kx")
            em.dma("sync", kn[64:69, :], kxn[h, :, jB:jB + 512],
                   writes=[kn], tag="ld_kn")
            em.dma("sync", bass.AP(vt, 0, [[520, 128], [130, 4], [1, 130]]),
                   vhb[4 * jb:4 * jb + 4, :, hp, :].rearrange("j p v -> p j v"),
                   reads=["vhb"], writes=[vt], tag="ld_vt")
            for s4 in range(4):
                jt = 4 * jb + s4
                j0 = 128 * jt
                for qc in range(4):
                    q0 = t0 + 512 * qc
                    qh = qhat[qc]
                    crossing = (j0 + 128 > q0) and (j0 < q0 + 512)
                    if hi_default:
                        fixup = (j0 + 128 <= q0)
                    else:
                        fixup = (j0 >= q0 + 512)
                    ps = cx.pp[2 + (jt + qc) % 2]
                    em.op("tensor", lambda t, ka=ka, s4=s4, qh=qh, ps=ps,
                          fixup=fixup: t.matmul(
                        out=ps[:, :], lhsT=ka[:, 128 * s4:128 * (s4 + 1)],
                        rhs=qh[:, :], start=True, stop=not fixup),
                        reads=[ka, qh], writes=[ps], tag="mm_s")
                    if fixup:
                        em.op("tensor", lambda t, kn=kn, s4=s4, qh=qh, ps=ps:
                              t.matmul(out=ps[:, :],
                                       lhsT=kn[64:69, 128 * s4:128 * (s4 + 1)],
                                       rhs=qh[64:69, :], start=False, stop=True),
                              reads=[kn, qh], writes=[ps], tag="mm_s2")
                    at = att[(4 * s4 + qc) % 4]
                    if crossing:
                        ph = (j0 - q0) // 128
                        em.op("vector", lambda v, ps=ps, ph=ph, fcs=fcs:
                              v.tensor_tensor(out=ps[:, :], in0=ps[:, :],
                                              in1=fcs[:, 512 * ph:512 * (ph + 1)],
                                              op=ALU.add),
                              reads=[ps, fcs], writes=[ps], tag="fcor")
                    em.op("scalar", lambda s, ps=ps, at=at: s.activation(
                        out=at[:, :], in_=ps[:, :], func=AF.Exp,
                        bias=nb8[:, 0:1]),
                        reads=[ps, nb8], writes=[at], tag="exp")
                    pso = cx.pp[PSO[qc]]
                    em.op("tensor", lambda t, vt=vt, s4=s4, hs=hs, at=at, pso=pso,
                          jt=jt: t.matmul(
                              out=pso[0:65, :],
                              lhsT=vt[:, 130 * s4 + 65 * hs:130 * s4 + 65 * hs + 65],
                              rhs=at[:, :], start=(jt == 0), stop=(jt == 31)),
                          reads=[vt, at], writes=[pso], tag="mm_av")
        for qc in range(4):
            pso = cx.pp[PSO[qc]]
            em.op("vector", lambda v, pso=pso: v.tensor_copy(
                out=rrow[:, :], in_=pso[64:65, :]),
                reads=[pso], writes=[rrow], tag="sumrow")
            em.op("vector", lambda v: v.reciprocal(out=rrow[:, :], in_=rrow[:, :]),
                  reads=[rrow], writes=[rrow], tag="recip")
            pbc = cx.pp[6]
            em.op("tensor", lambda t, pbc=pbc: t.matmul(
                out=pbc[0:64, :], lhsT=cx.ones_col[:, 0:64], rhs=rrow[:, :],
                start=True, stop=True),
                reads=[rrow, cx.ones_col], writes=[pbc], tag="bc_r")
            em.op("vector", lambda v, pbc=pbc: v.tensor_copy(
                out=rb[:, :], in_=pbc[0:64, :]),
                reads=[pbc], writes=[rb], tag="ev_rb")
            ao = aoT[h // 2]
            ro = 64 * (h % 2)
            em.op("vector", lambda v, pso=pso, ao=ao, ro=ro, qc=qc: v.tensor_tensor(
                out=ao[ro:ro + 64, 512 * qc:512 * (qc + 1)], in0=pso[0:64, :],
                in1=rb[:, :], op=ALU.mult),
                reads=[pso, rb], writes=[ao], tag="ev_ao")

    xoc = [em.sbuf(f"xoc{i}", (128, 512), F32) for i in range(2)]
    tmpo = em.sbuf("tmpo", (128, 512), F32)
    for qc in range(4):
        for ot in range(8):
            ps = cx.pp[ot % 2]
            for k in range(8):
                em.op("tensor", lambda t, ot=ot, k=k, qc=qc, ps=ps: t.matmul(
                    out=ps[:, :], lhsT=outwT_s[k][:, 128 * ot:128 * (ot + 1)],
                    rhs=aoT[k][:, 512 * qc:512 * (qc + 1)],
                    start=(k == 0), stop=(k == 7)),
                    reads=[outwT_s[k], aoT[k]], writes=[ps], tag="mm_out")
            xo = xoc[ot % 2]
            em.dma("sync", xo[:, :],
                   xin[128 * ot:128 * (ot + 1), t0 + 512 * qc:t0 + 512 * (qc + 1)],
                   writes=[xo], tag="ld_xo")
            em.op("vector", lambda v, ps=ps, ot=ot: v.tensor_scalar(
                out=tmpo[:, :], in0=ps[:, :], scalar1=ob_s[:, ot:ot + 1],
                scalar2=0.0, op0=ALU.add, op1=ALU.add),
                reads=[ps, ob_s], writes=[tmpo], tag="ev_out")
            em.op("vector", lambda v, xo=xo: v.tensor_tensor(
                out=xo[:, :], in0=xo[:, :], in1=tmpo[:, :], op=ALU.add),
                reads=[xo, tmpo], writes=[xo], tag="resid")
            em.dma("gpsimd", xout[128 * ot:128 * (ot + 1),
                                  512 * qc:512 * (qc + 1)], xo[:, :],
                   reads=[xo], tag="out")
    em.finalize()
    em.close()
    return nc


def _pack_alibi(qkv_w, qkv_b, out_w, out_b, th):
    perm = np.zeros(3 * D, np.int64)
    for h in range(16):
        for j in range(64):
            perm[h * 64 + j] = h * 192 + j
            perm[D + h * 64 + j] = h * 192 + 64 + j
            perm[2 * D + h * 64 + j] = h * 192 + 128 + j
    qkvT = np.ascontiguousarray(qkv_w[perm].T)
    qb = np.ascontiguousarray(qkv_b[perm].reshape(24, 128).T)
    outwT = np.ascontiguousarray(out_w.T)
    ob = np.ascontiguousarray(out_b.reshape(8, 128).T)
    m = 0.5 ** np.arange(16, dtype=np.float64)
    j = np.arange(T, dtype=np.float64)
    sgn = -1.0 if th == 0 else 1.0
    kxlv = np.zeros((16, 5, T), np.float32)
    for h in range(16):
        kxlv[h, 0] = sgn * m[h] * (j % 128)
        kxlv[h, 1] = sgn * m[h] * 128 * (j // 128)
        kxlv[h, 2] = sgn * m[h]
        kxlv[h, 3] = sgn * 128 * m[h]
        kxlv[h, 4] = sgn * m[h]
    kxnv = (-2.0 * kxlv).astype(np.float32)
    q = np.arange(T)
    qxv = np.zeros((5, T), np.float32)
    qxv[0] = 1.0
    qxv[1] = 1.0
    qxv[2] = -(q % 128)
    qxv[3] = -((q % 512) // 128)
    qxv[4] = -(512 * (q // 512))
    fc = np.zeros((16, 4, 128, 512), np.float32)
    jj = np.arange(128)[:, None]
    qq = np.arange(512)[None, :]
    for h in range(16):
        for ph in range(4):
            dd = (128 * ph + jj) - qq
            if th == 0:
                fc[h, ph] = np.where(dd < 0, 2 * m[h] * dd, 0.0)
            else:
                fc[h, ph] = np.where(dd > 0, -2 * m[h] * dd, 0.0)
    fc2 = np.ascontiguousarray(fc.transpose(0, 2, 1, 3).reshape(16, 128, 2048))
    return (qkvT, qb, outwT, ob, kxlv.astype(BF), kxnv.astype(BF),
            qxv.astype(BF), fc2.astype(np.float32))


def _alibi_stage(X_T, qkv_w, qkv_b, out_w, out_b):
    X_T2 = np.empty_like(X_T)
    for th in (0, 1):
        qkvT, qb, outwT, ob, kxlv, kxnv, qxv, fc = _pack_alibi(
            qkv_w, qkv_b, out_w, out_b, th)
        nc = _get_prog("alibi", build_alibi_th, th)
        ims = [{"xTf": np.ascontiguousarray(X_T[b]), "ident": _IDENT,
                "qkvT": qkvT, "qkvb": qb, "outwT": outwT, "outb": ob,
                "kxl": kxlv, "kxn": kxnv, "qx": qxv, "fcor": fc}
               for b in range(B)]
        res = run_bass_kernel_spmd(nc, ims, core_ids=list(range(B)))
        for b in range(B):
            X_T2[b][:, th * TH:(th + 1) * TH] = np.asarray(res.results[b]["xoT"])
    return X_T2



def _qk_process(em, cx, name, ps_qkv, srow, rsq_scale, ctab, stab, rmat_s,
                out_ring, ring_cols, blk_g, rows2, epscol, sqb, kb, krot, t1):
    pp = cx.pp
    em.op("scalar", lambda s: s.activation(out=sqb[:, :], in_=ps_qkv[:, :],
                                           func=AF.Square),
          reads=[ps_qkv], writes=[sqb], tag=f"{name}_sq")
    em.op("scalar", lambda s: s.activation(out=kb[:, :], in_=ps_qkv[:, :],
                                           func=AF.Copy),
          reads=[ps_qkv], writes=[kb], tag=f"{name}_cp")
    for hs in range(2):
        em.op("tensor", lambda t, hs=hs: t.matmul(
            out=pp[6][0:1, :], lhsT=cx.ones_red[64 * hs:64 * hs + 64, :],
            rhs=sqb[64 * hs:64 * hs + 64, :], start=True, stop=True),
            reads=[sqb, cx.ones_red], writes=[pp[6]], tag=f"{name}_ssq")
        em.op("vector", lambda v, hs=hs: v.tensor_copy(
            out=rows2[0:1, 512 * hs:512 * (hs + 1)], in_=pp[6][0:1, :]),
            reads=[pp[6]], writes=[rows2], tag=f"{name}_evsq")
    em.op("scalar", lambda s: s.activation(out=rows2[0:1, :], in_=rows2[0:1, :],
                                           func=AF.Ln, bias=epscol[:, 0:1]),
          reads=[rows2, epscol], writes=[rows2], tag=f"{name}_ln")
    em.op("scalar", lambda s: s.activation(out=rows2[0:1, :], in_=rows2[0:1, :],
                                           func=AF.Exp, scale=-0.5),
          reads=[rows2], writes=[rows2], tag=f"{name}_exp")
    if rsq_scale != 1.0:
        em.op("vector", lambda v: v.tensor_scalar(
            out=rows2[0:1, :], in0=rows2[0:1, :], scalar1=float(rsq_scale),
            scalar2=0.0, op0=ALU.mult, op1=ALU.add),
            reads=[rows2], writes=[rows2], tag=f"{name}_sc")
    em.op("tensor", lambda t: t.matmul(
        out=pp[6][:, :], lhsT=srow[0:1, 0:128], rhs=rows2[0:1, 0:512],
        start=True, stop=False),
        reads=[srow, rows2], writes=[pp[6]], tag=f"{name}_bc1")
    em.op("tensor", lambda t: t.matmul(
        out=pp[6][:, :], lhsT=srow[0:1, 128:256], rhs=rows2[0:1, 512:1024],
        start=False, stop=True),
        reads=[srow, rows2], writes=[pp[6]], tag=f"{name}_bc2")
    em.op("vector", lambda v: v.tensor_tensor(out=kb[:, :], in0=kb[:, :],
                                              in1=pp[6][:, :], op=ALU.mult),
          reads=[kb, pp[6]], writes=[kb], tag=f"{name}_nrm")
    em.op("tensor", lambda t: t.matmul(out=pp[6][:, :], lhsT=rmat_s[:, :],
                                       rhs=kb[:, :], start=True, stop=True),
          reads=[rmat_s, kb], writes=[pp[6]], tag=f"{name}_rot")
    em.op("vector", lambda v: v.tensor_copy(out=krot[:, :], in_=pp[6][:, :]),
          reads=[pp[6]], writes=[krot], tag=f"{name}_evrot")
    em.op("vector", lambda v: v.tensor_tensor(out=t1[:, :], in0=kb[:, :],
                                              in1=ctab, op=ALU.mult),
          reads=[kb, ("tabs",)], writes=[t1], tag=f"{name}_c")
    em.op("vector", lambda v: v.tensor_tensor(out=krot[:, :], in0=krot[:, :],
                                              in1=stab, op=ALU.mult),
          reads=[krot, ("tabs",)], writes=[krot], tag=f"{name}_s")
    em.op("vector", lambda v: v.tensor_tensor(
        out=out_ring[:, ring_cols[0]:ring_cols[1]], in0=t1[:, :],
        in1=krot[:, :], op=ALU.add),
        reads=[t1, krot], writes=[out_ring], tag=f"{name}_add")


def build_ffn_attn(th, first):
    t0 = th * TH
    g0 = t0 // 512
    nc = bass.Bass("TRN2", target_bir_lowering=False)
    em = Em(nc)
    zin = nc.dram_tensor("zin", [D, T], F32, kind="ExternalInput").ap()
    ident_in = nc.dram_tensor("ident", [128, 128], BF16, kind="ExternalInput").ap()
    wqkv = nc.dram_tensor("wqkv", [D, 1536], F32, kind="ExternalInput").ap()
    wo = nc.dram_tensor("wo", [512, D], F32, kind="ExternalInput").ap()
    qsr = nc.dram_tensor("qsr", [1, 256], F32, kind="ExternalInput").ap()
    ksr = nc.dram_tensor("ksr", [1, 256], F32, kind="ExternalInput").ap()
    rmat = nc.dram_tensor("rmat", [128, 128], BF16, kind="ExternalInput").ap()
    rtab = nc.dram_tensor("rtab", [4, 128, T], BF16, kind="ExternalInput").ap()
    amask = nc.dram_tensor("amask", [128, 8 * 512], BF16, kind="ExternalInput").ap()
    uout = nc.dram_tensor("uout", [D, TH], F32, kind="ExternalOutput").ap()

    cx = Ctx(em, ident_in, nbanks=7)
    ptb = em.psum("ptb", (128, 1024), BF16)
    lns = ln_scratch(em)
    stage = em.sbuf("stage", (128, 1536), F32)
    wqkv_s = [em.sbuf(f"wq{k}", (128, 1536), BF16) for k in range(8)]
    wo_s = [em.sbuf(f"wos{k}", (128, D), BF16) for k in range(4)]
    for k in range(8):
        em.dma("sync", stage[:, :], wqkv[128 * k:128 * (k + 1), :],
               writes=[stage], tag="ld_wq")
        em.op("vector", lambda v, k=k: v.tensor_copy(out=wqkv_s[k][:, :],
                                                     in_=stage[:, :]),
              reads=[stage], writes=[wqkv_s[k]], tag="cast_wq")
    for k in range(4):
        em.dma("sync", stage[:, 0:D], wo[128 * k:128 * (k + 1), :],
               writes=[stage], tag="ld_wo")
        em.op("vector", lambda v, k=k: v.tensor_copy(out=wo_s[k][:, :],
                                                     in_=stage[:, 0:D]),
              reads=[stage], writes=[wo_s[k]], tag="cast_wo")
    rmat_s = em.sbuf("rmat", (128, 128), BF16)
    em.dma("sync", rmat_s[:, :], rmat, writes=[rmat_s], tag="ld_rmat")
    qsr_s = em.sbuf("qsr", (1, 256), F32)
    ksr_s = em.sbuf("ksr", (1, 256), F32)
    em.dma("sync", qsr_s[:, :], qsr, writes=[qsr_s], tag="ld_qsr")
    em.dma("sync", ksr_s[:, :], ksr, writes=[ksr_s], tag="ld_ksr")
    amask_s = em.sbuf("amask", (128, 8 * 512), BF16)
    em.dma("sync", amask_s[:, :], amask, writes=[amask_s], tag="ld_amask")
    epscol = em.sbuf("epscol", (1, 1), F32)
    em.op("gpsimd", lambda g: g.memset(epscol[:, :], 1e-24), writes=[epscol],
          tag="eps")

    kring = [em.sbuf(f"kring{j}", (128, 3 * 512), BF16) for j in range(4)]
    vring = [em.sbuf(f"vring{s}", (128, 3 * 520), BF16) for s in range(4)]
    for s in range(4):
        for sl in range(3):
            for hh in range(8):
                em.op("gpsimd", lambda g, s=s, sl=sl, hh=hh: g.memset(
                    vring[s][:, 520 * sl + 65 * hh + 64:520 * sl + 65 * hh + 65],
                    1.0), writes=[("vones",)], tag="ones_v")
    zc = [em.sbuf(f"zc{i}", (128, 513), F32) for i in range(8)]
    zb = [em.sbuf(f"zb{i}", (128, 513), BF16) for i in range(8)]
    z1 = [em.sbuf(f"z1_{i}", (128, TH), F32) for i in range(8)]
    qkvc = em.sbuf("qkvc", (128, 512), BF16)
    sqb = em.sbuf("sqb", (128, 512), BF16)
    kb = em.sbuf("kb", (128, 512), BF16)
    krot = em.sbuf("krot", (128, 512), BF16)
    t1b = em.sbuf("t1b", (128, 512), BF16)
    rows2 = em.sbuf("rows2", (1, 1024), F32)
    tabs = [em.sbuf(f"tab{i}", (128, 512), BF16) for i in range(4)]
    qhat = [em.sbuf(f"qhat{i}", (128, 512), BF16) for i in range(4)]
    att = [em.sbuf(f"att{i}", (128, 512), BF16) for i in range(2)]
    aot = [em.sbuf(f"aot{i}", (128, 512), BF16) for i in range(4)]
    rrow = em.sbuf("rrow", (1, 512), F32)
    rb = em.sbuf("rb", (64, 512), F32)
    tmpo = em.sbuf("tmpo", (128, 512), F32)

    blk_start = g0 - (1 if th == 1 else 0)
    for gb in range(blk_start, g0 + 4):
        own = gb >= g0
        bs = 512 * gb
        slot = gb % 3
        for i in range(8):
            if bs == 0:
                em.op("gpsimd", lambda g, i=i: g.memset(zc[i][:, 0:1], 0.0),
                      writes=[zc[i]], tag="zpad")
                em.dma("sync", zc[i][:, 1:513], zin[128 * i:128 * (i + 1),
                                                    bs:bs + 512],
                       writes=[zc[i]], tag="ld_z")
            else:
                em.dma("sync", zc[i][:, :], zin[128 * i:128 * (i + 1),
                                                bs - 1:bs + 512],
                       writes=[zc[i]], tag="ld_z")
        if first:
            ln_feature_major(em, cx, zc, 513, zc, 4, 5, name=f"lnS{gb}",
                             scratch=lns)
        for i in range(8):
            em.op("scalar", lambda s, i=i: s.activation(
                out=zb[i][:, :], in_=zc[i][:, :], func=AF.Copy),
                reads=[zc[i]], writes=[zb[i]], tag="cast_z")
        for ti in range(4):
            em.dma("sync", tabs[ti][:, :], rtab[ti, :, bs:bs + 512],
                   writes=[("tabs",)], tag="ld_tab")
        for ot in range(12):
            kind = 0 if ot < 4 else (1 if ot < 8 else 2)
            if kind == 0 and not own:
                continue
            ps = cx.pp[ot % 2]
            for k in range(8):
                em.op("tensor", lambda t, ot=ot, k=k, ps=ps: t.matmul(
                    out=ps[:, :], lhsT=wqkv_s[k][:, 128 * ot:128 * (ot + 1)],
                    rhs=(zb[k][:, 1:513] if k < 4 else zb[k][:, 0:512]),
                    start=(k == 0), stop=(k == 7)),
                    reads=[wqkv_s[k], zb[k]], writes=[ps], tag="mm_qkv")
            if kind == 2:
                em.op("scalar", lambda s, ps=ps: s.activation(
                    out=qkvc[:, :], in_=ps[:, :], func=AF.Copy),
                    reads=[ps], writes=[qkvc], tag="ev_vv")
                jj = ot - 8
                for s4 in range(4):
                    em.op("tensor", lambda t, s4=s4: t.transpose(
                        out=ptb[:, 512 * (s4 % 2):512 * (s4 % 2) + 128],
                        in_=qkvc[:, 128 * s4:128 * (s4 + 1)],
                        identity=cx.ident[:, :]),
                        reads=[qkvc, cx.ident], writes=[ptb], tag="tr_v")
                    for hs in range(2):
                        hh = 2 * jj + hs
                        em.op("vector", lambda v, s4=s4, hs=hs, hh=hh, slot=slot:
                              v.tensor_copy(
                                  out=vring[s4][:, 520 * slot + 65 * hh:
                                                520 * slot + 65 * hh + 64],
                                  in_=ptb[:, 512 * (s4 % 2) + 64 * hs:
                                          512 * (s4 % 2) + 64 * hs + 64]),
                              reads=[ptb], writes=[vring[s4]], tag="ev_vt")
            else:
                srow = qsr_s if kind == 0 else ksr_s
                scl = 8.0 if kind == 0 else 1.0
                ct = tabs[0] if kind == 0 else tabs[2]
                st = tabs[1] if kind == 0 else tabs[3]
                if kind == 0:
                    oring, rc = qhat[ot], (0, 512)
                else:
                    oring, rc = kring[ot - 4], (512 * slot, 512 * slot + 512)
                _qk_process(em, cx, f"qk{gb}_{ot}", ps, srow, scl,
                            ct[:, :], st[:, :], rmat_s, oring, rc, gb, rows2,
                            epscol, sqb, kb, krot, t1b)
        if not own:
            continue
        ob = gb - g0
        for h in range(8):
            pso = cx.pp[4 + (h % 2)]
            jts = range(4, 8) if gb == 0 else range(8)
            first_jt = 4 if gb == 0 else 0
            for jtl in jts:
                jgb = gb - 1 + jtl // 4
                jslot = jgb % 3
                s4 = jtl % 4
                ps = cx.pp[2 + (jtl % 2)]
                em.op("tensor", lambda t, h=h, jslot=jslot, s4=s4, ps=ps:
                      t.matmul(out=ps[:, :],
                               lhsT=kring[h // 2]
                               [64 * (h % 2):64 * (h % 2) + 64,
                                512 * jslot + 128 * s4:512 * jslot + 128 * (s4 + 1)],
                               rhs=qhat[h // 2][64 * (h % 2):64 * (h % 2) + 64, :],
                               start=True, stop=True),
                      reads=[kring[h // 2], qhat[h // 2]], writes=[ps], tag="mm_s")
                em.op("vector", lambda v, ps=ps, jtl=jtl: v.tensor_tensor(
                    out=ps[:, :], in0=ps[:, :],
                    in1=amask_s[:, 512 * jtl:512 * (jtl + 1)], op=ALU.add),
                    reads=[ps, amask_s], writes=[ps], tag="mask")
                at = att[jtl % 2]
                em.op("scalar", lambda s, ps=ps, at=at: s.activation(
                    out=at[:, :], in_=ps[:, :], func=AF.Exp),
                    reads=[ps], writes=[at], tag="exp")
                em.op("tensor", lambda t, at=at, h=h, jslot=jslot, s4=s4,
                      pso=pso, jtl=jtl, first_jt=first_jt: t.matmul(
                          out=pso[0:65, :],
                          lhsT=vring[s4][:, 520 * jslot + 65 * h:
                                         520 * jslot + 65 * h + 65],
                          rhs=at[:, :], start=(jtl == first_jt),
                          stop=(jtl == 7)),
                      reads=[vring[s4], at, ("vones",)], writes=[pso], tag="mm_av")
            em.op("vector", lambda v, pso=pso: v.tensor_copy(
                out=rrow[:, :], in_=pso[64:65, :]),
                reads=[pso], writes=[rrow], tag="sumrow")
            em.op("vector", lambda v: v.reciprocal(out=rrow[:, :],
                                                   in_=rrow[:, :]),
                  reads=[rrow], writes=[rrow], tag="recip")
            em.op("tensor", lambda t: t.matmul(
                out=cx.pp[6][0:64, :], lhsT=cx.ones_col[:, 0:64],
                rhs=rrow[:, :], start=True, stop=True),
                reads=[rrow, cx.ones_col], writes=[cx.pp[6]], tag="bc_r")
            em.op("vector", lambda v: v.tensor_copy(out=rb[:, :],
                                                    in_=cx.pp[6][0:64, :]),
                  reads=[cx.pp[6]], writes=[rb], tag="ev_rb")
            ao = aot[h // 2]
            ro = 64 * (h % 2)
            em.op("vector", lambda v, pso=pso, ao=ao, ro=ro: v.tensor_tensor(
                out=ao[ro:ro + 64, :], in0=pso[0:64, :], in1=rb[:, :],
                op=ALU.mult),
                reads=[pso, rb], writes=[ao], tag="ev_ao")
        for ot in range(8):
            ps = cx.pp[ot % 2]
            for k in range(4):
                em.op("tensor", lambda t, ot=ot, k=k, ps=ps: t.matmul(
                    out=ps[:, :], lhsT=wo_s[k][:, 128 * ot:128 * (ot + 1)],
                    rhs=aot[k][:, :], start=(k == 0), stop=(k == 3)),
                    reads=[wo_s[k], aot[k]], writes=[ps], tag="mm_wo")
            em.op("vector", lambda v, ps=ps, ot=ot, ob=ob: v.tensor_tensor(
                out=z1[ot][:, 512 * ob:512 * (ob + 1)], in0=zc[ot][:, 1:513],
                in1=ps[:, :], op=ALU.add),
                reads=[ps, zc[ot]], writes=[z1[ot]], tag="resid")
    for i in range(8):
        em.dma("gpsimd", uout[128 * i:128 * (i + 1), :], z1[i][:, :],
               reads=[z1[i]], tag="out")
    em.finalize()
    em.close()
    return nc


def build_ffn_ffn(th, last):
    t0 = th * TH
    nc = bass.Bass("TRN2", target_bir_lowering=False)
    em = Em(nc)
    uin = nc.dram_tensor("uin", [D, T], F32, kind="ExternalInput").ap()
    ident_in = nc.dram_tensor("ident", [128, 128], BF16, kind="ExternalInput").ap()
    w1p = nc.dram_tensor("w1p", [D, 5632], F32, kind="ExternalInput").ap()
    w2p = nc.dram_tensor("w2p", [2816, D], F32, kind="ExternalInput").ap()
    b1p = nc.dram_tensor("b1p", [128, 44], F32, kind="ExternalInput").ap()
    b2r = nc.dram_tensor("b2r", [1, D], F32, kind="ExternalInput").ap()
    lng = nc.dram_tensor("lng", [128, 8], F32, kind="ExternalInput").ap()
    lnb = nc.dram_tensor("lnb", [128, 8], F32, kind="ExternalInput").ap()
    if last:
        xst = nc.dram_tensor("xst", [D, TH], F32, kind="ExternalInput").ap()
    zout = nc.dram_tensor("zoT", [D, TH], F32, kind="ExternalOutput").ap()

    cx = Ctx(em, ident_in)
    lns = ln_scratch(em)
    stage = em.sbuf("stage", (128, 5632), F32)
    w1_s = [em.sbuf(f"w1s{k}", (128, 5632), BF16) for k in range(8)]
    for k in range(8):
        em.dma("sync", stage[:, :], w1p[128 * k:128 * (k + 1), :],
               writes=[stage], tag="ld_w1")
        em.op("vector", lambda v, k=k: v.tensor_copy(out=w1_s[k][:, :],
                                                     in_=stage[:, :]),
              reads=[stage], writes=[w1_s[k]], tag="cast_w1")
    b1_s = em.sbuf("b1s", (128, 44), F32)
    lng_s = em.sbuf("lngs", (128, 8), F32)
    lnb_s = em.sbuf("lnbs", (128, 8), F32)
    em.dma("sync", b1_s[:, :], b1p, writes=[b1_s], tag="ld_b1")
    em.dma("sync", lng_s[:, :], lng, writes=[lng_s], tag="ld_lng")
    em.dma("sync", lnb_s[:, :], lnb, writes=[lnb_s], tag="ld_lnb")
    w2_s = [em.sbuf(f"w2s{i}", (128, D), BF16) for i in range(2)]
    w2_f = [stage[:, 0:1024], stage[:, 1024:2048]]
    b2r_f = em.sbuf("b2rf", (1, D), F32)
    b2r_s = em.sbuf("b2rs", (1, D), BF16)
    ones512 = em.sbuf("ones512", (1, 512), BF16)
    em.dma("sync", b2r_f[:, :], b2r, writes=[b2r_f], tag="ld_b2")
    em.op("vector", lambda v: v.tensor_copy(out=b2r_s[:, :], in_=b2r_f[:, :]),
          reads=[b2r_f], writes=[b2r_s], tag="cast_b2")
    em.op("gpsimd", lambda g: g.memset(ones512[:, :], 1.0), writes=[ones512],
          tag="ones512")

    uc = [em.sbuf(f"uc{i}", (128, 513), F32) for i in range(8)]
    hb = [em.sbuf(f"hb{i}", (128, 513), BF16) for i in range(8)]
    ag = [em.sbuf(f"ag{i}", (128, 512), BF16) for i in range(22)]
    gg = em.sbuf("gg", (128, 512), BF16)
    xoc = [em.sbuf(f"xoc{i}", (128, 512), F32) for i in range(5)]
    gb_cols = [(lng_s[:, i:i + 1], lnb_s[:, i:i + 1]) for i in range(8)]
    for c in range(4):
        bs = t0 + 512 * c
        for i in range(8):
            if bs == 0:
                em.op("gpsimd", lambda g, i=i: g.memset(uc[i][:, 0:1], 0.0),
                      writes=[uc[i]], tag="zpad")
                em.dma("sync", uc[i][:, 1:513],
                       uin[128 * i:128 * (i + 1), bs:bs + 512],
                       writes=[uc[i]], tag="ld_u")
            else:
                em.dma("sync", uc[i][:, :],
                       uin[128 * i:128 * (i + 1), bs - 1:bs + 512],
                       writes=[uc[i]], tag="ld_u")
        ln_feature_major(em, cx, uc, 512, hb, 4, 5, gain_bias=gb_cols,
                         name=f"ln2_{c}", scratch=lns,
                         x_offs=[1 if i < 4 else 0 for i in range(8)])
        for ht in range(44):
            ps = cx.pp[ht % 2]
            for k in range(8):
                em.op("tensor", lambda t, ht=ht, k=k, ps=ps: t.matmul(
                    out=ps[:, :], lhsT=w1_s[k][:, 128 * ht:128 * (ht + 1)],
                    rhs=hb[k][:, 0:512], start=(k == 0), stop=(k == 7)),
                    reads=[w1_s[k], hb[k]], writes=[ps], tag="mm_w1")
            if ht < 22:
                em.op("vector", lambda v, ps=ps, ht=ht: v.tensor_scalar(
                    out=ag[ht][:, :], in0=ps[:, :], scalar1=b1_s[:, ht:ht + 1],
                    scalar2=0.0, op0=ALU.add, op1=ALU.add),
                    reads=[ps, b1_s], writes=[ag[ht]], tag="ev_a")
            else:
                em.op("scalar", lambda s, ps=ps, ht=ht: s.activation(
                    out=gg[:, :], in_=ps[:, :], func=AF.Gelu,
                    bias=b1_s[:, ht:ht + 1]),
                    reads=[ps, b1_s], writes=[gg], tag="gelu")
                em.op("vector", lambda v, ht=ht: v.tensor_tensor(
                    out=ag[ht - 22][:, :], in0=ag[ht - 22][:, :], in1=gg[:, :],
                    op=ALU.mult),
                    reads=[ag[ht - 22], gg], writes=[ag[ht - 22]], tag="hgm")
        for oh in range(2):
            for kk in range(22):
                wf = w2_f[kk % 2]
                em.dma("sync", wf, w2p[128 * kk:128 * (kk + 1), :],
                       writes=[stage], tag="ld_w2")
                em.op("scalar", lambda s, kk=kk, wf=wf: s.activation(
                    out=w2_s[kk % 2][:, :], in_=wf, func=AF.Copy),
                    reads=[stage], writes=[w2_s[kk % 2]], tag="cast_w2")
                for oi in range(4):
                    o = 4 * oh + oi
                    em.op("tensor", lambda t, kk=kk, o=o, oi=oi: t.matmul(
                        out=cx.pp[2 + oi][:, :],
                        lhsT=w2_s[kk % 2][:, 128 * o:128 * (o + 1)],
                        rhs=ag[kk][:, :], start=(kk == 0), stop=False),
                        reads=[w2_s[kk % 2], ag[kk]], writes=[cx.pp[2 + oi]],
                        tag="mm_w2")
            for oi in range(4):
                o = 4 * oh + oi
                em.op("tensor", lambda t, o=o, oi=oi: t.matmul(
                    out=cx.pp[2 + oi][:, :],
                    lhsT=b2r_s[0:1, 128 * o:128 * (o + 1)],
                    rhs=ones512[:, :], start=False, stop=True),
                    reads=[b2r_s, ones512], writes=[cx.pp[2 + oi]],
                    tag="mm_b2")
            for oi in range(4):
                o = 4 * oh + oi
                xo = xoc[oi]
                em.op("vector", lambda v, o=o, oi=oi, xo=xo: v.tensor_tensor(
                    out=xo[:, :], in0=cx.pp[2 + oi][:, :],
                    in1=uc[o][:, 1:513], op=ALU.add),
                    reads=[cx.pp[2 + oi], uc[o]], writes=[xo], tag="resid")
                if last:
                    xs2 = xoc[4]
                    em.dma("sync", xs2[:, :],
                           xst[128 * o:128 * (o + 1), 512 * c:512 * (c + 1)],
                           writes=[xs2], tag="ld_xst")
                    em.op("vector", lambda v, xo=xo, xs2=xs2: v.scalar_tensor_tensor(
                        out=xo[:, :], in0=xo[:, :], scalar=0.5, in1=xs2[:, :],
                        op0=ALU.mult, op1=ALU.add),
                        reads=[xo, xs2], writes=[xo], tag="final")
                em.dma("gpsimd", zout[128 * o:128 * (o + 1),
                                      512 * c:512 * (c + 1)], xo[:, :],
                       reads=[xo], tag="out")
    em.finalize()
    em.close()
    return nc


def _pack_ffn(l, wqkv, q_scale, k_scale, wo, ln_g, ln_b, w1, b1, w2, b2):
    wq = np.ascontiguousarray(wqkv[l])
    wos = np.ascontiguousarray(wo[l])
    qsr = np.zeros((1, 256), np.float32)
    qsr[0, 0:64] = q_scale[l]; qsr[0, 64:128] = 0
    qsr[0, 128 + 64:128 + 128] = q_scale[l]
    ksr = np.zeros((1, 256), np.float32)
    ksr[0, 0:64] = k_scale[l]; ksr[0, 128 + 64:256] = k_scale[l]
    R = np.zeros((128, 128), np.float32)
    for hh in range(2):
        b0 = 64 * hh
        for r in range(32):
            R[b0 + r + 32, b0 + r] = -1.0
            R[b0 + r, b0 + r + 32] = 1.0
    n = T
    half = np.arange(0, 64, 2, dtype=np.float64)
    inv_freq = 1.0 / 10000.0 ** (half / 64.0)
    t = np.arange(n, dtype=np.float64)
    freqs = t[:, None] * inv_freq[None, :]
    freqs = np.concatenate([freqs, freqs], axis=1)
    base = (half + 0.4 * 64.0) / (1.4 * 64.0)
    power = (t - n // 2) / 256.0
    scale = base[None, :] ** power[:, None]
    scale = np.concatenate([scale, scale], axis=1)
    cos, sin = np.cos(freqs), np.sin(freqs)
    cq = (cos * scale).T; sq = (sin * scale).T
    ck = (cos / scale).T; sk = (sin / scale).T
    rtab = np.stack([np.concatenate([a, a], 0) for a in (cq, sq, ck, sk)])
    am = np.zeros((8, 128, 512), np.float32)
    jl = np.arange(128)[:, None]
    qq = np.arange(512)[None, :]
    for jtl in range(8):
        dd = 512 - 128 * jtl - jl + qq
        am[jtl] = np.where((dd >= 0) & (dd <= 512), 0.0, -100.0)
    amp = np.ascontiguousarray(am.transpose(1, 0, 2).reshape(128, 8 * 512)).astype(BF)
    w1l, b1l, w2l, b2l = w1[l], b1[l], w2[l], b2[l]
    w1p = np.zeros((1024, 5632), np.float32)
    w1p[:, 0:2730] = w1l[:, 0:2730]
    w1p[:, 2816:2816 + 2730] = w1l[:, 2730:5460]
    b1pk = np.zeros((128, 44), np.float32)
    b1f = np.zeros(5632, np.float32)
    b1f[0:2730] = b1l[0:2730]
    b1f[2816:2816 + 2730] = b1l[2730:5460]
    b1pk[:, :] = b1f.reshape(44, 128).T
    w2p = np.zeros((2816, 1024), np.float32)
    w2p[0:2730, :] = w2l
    lngp = np.ascontiguousarray(ln_g[l].reshape(8, 128).T)
    lnbp = np.ascontiguousarray(ln_b[l].reshape(8, 128).T)
    return dict(wqkv=wq, wo=wos, qsr=qsr, ksr=ksr,
                rmat=R.astype(BF), rtab=rtab.astype(BF), amask=amp,
                w1p=w1p, w2p=w2p, b1p=b1pk, lng=lngp, lnb=lnbp,
                b2r=np.ascontiguousarray(b2l[None, :]))


def _ffn_stage(X_T, wqkv, q_scale, k_scale, wo, ln_g, ln_b, w1, b1, w2, b2,
               final_unused=False):
    Z = X_T
    X0 = X_T
    for l in range(2):
        pk = _pack_ffn(l, wqkv, q_scale, k_scale, wo, ln_g, ln_b, w1, b1, w2, b2)
        first = (l == 0)
        last = (l == 1)
        U = np.empty_like(Z)
        for th in (0, 1):
            nc = _get_prog("ffn_attn", build_ffn_attn, th, first)
            ims = [{"zin": np.ascontiguousarray(Z[b]), "ident": _IDENT,
                    "wqkv": pk["wqkv"], "wo": pk["wo"], "qsr": pk["qsr"],
                    "ksr": pk["ksr"], "rmat": pk["rmat"], "rtab": pk["rtab"],
                    "amask": pk["amask"]} for b in range(B)]
            res = run_bass_kernel_spmd(nc, ims, core_ids=list(range(B)))
            for b in range(B):
                U[b][:, th * TH:(th + 1) * TH] = np.asarray(res.results[b]["uout"])
        Z2 = np.empty_like(Z)
        for th in (0, 1):
            nc = _get_prog("ffn_ffn", build_ffn_ffn, th, last)
            ims = []
            for b in range(B):
                im = {"uin": np.ascontiguousarray(U[b]), "ident": _IDENT,
                      "w1p": pk["w1p"], "w2p": pk["w2p"], "b1p": pk["b1p"],
                      "lng": pk["lng"], "lnb": pk["lnb"], "b2r": pk["b2r"]}
                if last:
                    im["xst"] = np.ascontiguousarray(
                        X0[b][:, th * TH:(th + 1) * TH])
                ims.append(im)
            res = run_bass_kernel_spmd(nc, ims, core_ids=list(range(B)))
            for b in range(B):
                Z2[b][:, th * TH:(th + 1) * TH] = np.asarray(res.results[b]["zoT"])
        Z = Z2
    return Z



def build_mamba_ab(first_layer):
    nc = bass.Bass("TRN2", target_bir_lowering=False)
    em = Em(nc)
    ap = lambda n, s, d=F32, k="ExternalInput": nc.dram_tensor(n, s, d, kind=k).ap()
    hin = ap("hin", [D, T])
    ident_in = ap("ident", [128, 128], BF16)
    if not first_layer:
        rin = ap("rin", [D, T])
    wpA = ap("wpA", [D, 1024])
    wpB = ap("wpB", [D, 1024])
    wx = ap("wx", [1024, 96])
    wdt = ap("wdt", [64, 1024])
    nw = ap("nw", [128, 8])
    nb = ap("nb", [128, 8])
    wcp = ap("wcp", [128, 32])
    wcb = ap("wcb", [128, 8])
    dtb = ap("dtb", [128, 8])
    dsk = ap("dsk", [128, 8])
    alog = ap("alog", [128, 8 * 16])
    sel = ap("sel", [32, 32 * 128], BF16)
    rout = ap("rout", [D, T], F32, "ExternalOutput")
    yout = ap("yout", [1024, T], BF16, "ExternalOutput")
    xdb_in = nc.dram_tensor("xdb_i", [96, T], F32)
    xdb_out = nc.dram_tensor("xdb_o", [96, T], F32)
    xcs_sp = nc.dram_tensor("xcs_sp", [1024, T], BF16)
    z_sp = nc.dram_tensor("z_sp", [1024, T], BF16)

    cx = Ctx(em, ident_in)
    lns = ln_scratch(em)
    NCH = T // 512

    stage = em.sbuf("stage", (128, 2048), F32)
    wpA_s = [em.sbuf(f"wpA{k}", (128, 1024), BF16) for k in range(8)]
    wpB_s = [em.sbuf(f"wpB{k}", (128, 1024), BF16) for k in range(8)]
    for k in range(8):
        em.dma("sync", stage[:, 0:1024], wpA[128 * k:128 * (k + 1), :],
               writes=[stage], tag="ld_wpA")
        em.op("vector", lambda v, k=k: v.tensor_copy(out=wpA_s[k][:, :],
                                                     in_=stage[:, 0:1024]),
              reads=[stage], writes=[wpA_s[k]], tag="c_wpA")
        em.dma("sync", stage[:, 0:1024], wpB[128 * k:128 * (k + 1), :],
               writes=[stage], tag="ld_wpB")
        em.op("vector", lambda v, k=k: v.tensor_copy(out=wpB_s[k][:, :],
                                                     in_=stage[:, 0:1024]),
              reads=[stage], writes=[wpB_s[k]], tag="c_wpB")
    wx_s = [em.sbuf(f"wx{k}", (128, 96), BF16) for k in range(8)]
    for k in range(8):
        em.dma("sync", stage[:, 0:96], wx[128 * k:128 * (k + 1), :],
               writes=[stage], tag="ld_wx")
        em.op("scalar", lambda s, k=k: s.activation(
            out=wx_s[k][:, :], in_=stage[:, 0:96], func=AF.Copy, scale=0.5),
            reads=[stage], writes=[wx_s[k]], tag="c_wx")
    wdt_s = em.sbuf("wdt", (64, 1024), BF16)
    em.dma("sync", stage[0:64, 0:1024], wdt, writes=[stage], tag="ld_wdt")
    em.op("vector", lambda v: v.tensor_copy(out=wdt_s[:, :],
                                            in_=stage[0:64, 0:1024]),
          reads=[stage], writes=[wdt_s], tag="c_wdt")
    sel_s = em.sbuf("sel", (32, 32 * 128), BF16)
    em.dma("sync", sel_s[:, :], sel, writes=[sel_s], tag="ld_sel")
    nw_s = em.sbuf("nw", (128, 8), F32)
    nb_s = em.sbuf("nb", (128, 8), F32)
    wcp_s = em.sbuf("wcp", (128, 32), F32)
    wcb_s = em.sbuf("wcb", (128, 8), F32)
    dtb_s = em.sbuf("dtb", (128, 8), F32)
    dsk_s = em.sbuf("dsk", (128, 8), F32)
    alog_s = em.sbuf("alog", (128, 128), F32)
    for nm, dst, srcp in [("nw", nw_s, nw), ("nb", nb_s, nb), ("wcp", wcp_s, wcp),
                          ("wcb", wcb_s, wcb), ("dtb", dtb_s, dtb),
                          ("dsk", dsk_s, dsk), ("alog", alog_s, alog)]:
        em.dma("sync", dst[:, :], srcp, writes=[dst], tag=f"ld_{nm}")
    As_s = em.sbuf("As", (128, 128), F32)
    em.op("scalar", lambda s: s.activation(out=As_s[:, :], in_=alog_s[:, :],
                                           func=AF.Exp),
          reads=[alog_s], writes=[As_s], tag="mkA")
    em.op("vector", lambda v: v.tensor_scalar(out=As_s[:, :], in0=As_s[:, :],
                                              scalar1=-1.0, scalar2=0.0,
                                              op0=ALU.mult, op1=ALU.add),
          reads=[As_s], writes=[As_s], tag="negA")
    Dh_s = em.sbuf("Dh", (128, 8), F32)
    em.op("vector", lambda v: v.tensor_scalar(out=Dh_s[:, :], in0=dsk_s[:, :],
                                              scalar1=0.5, scalar2=0.0,
                                              op0=ALU.mult, op1=ALU.add),
          reads=[dsk_s], writes=[Dh_s], tag="Dh")

    build_mamba_ab._acc = em.sbuf("cacc", (128, 512), F32)
    build_mamba_ab._th = em.sbuf("thm", (128, 512), F32)
    gb_cols = [(nw_s[:, i:i + 1], nb_s[:, i:i + 1]) for i in range(8)]
    hc = [em.sbuf(f"hc{i}", (128, 512), F32) for i in range(8)]
    rc = [em.sbuf(f"rc{i}", (128, 512), F32) for i in range(8)]
    xnc = [em.sbuf(f"xnc{i}", (128, 512), BF16) for i in range(8)]
    xct = [em.sbuf(f"xct{i}", (128, 515), F32) for i in range(8)]
    zb = em.sbuf("zbm", (128, 512), BF16)
    xcb = em.sbuf("xcbm", (128, 512), BF16)
    xdb_s = em.sbuf("xdbs", (96, T), F32)

    for c in range(NCH):
        c0 = 512 * c
        for i in range(8):
            em.dma("sync", hc[i][:, :], hin[128 * i:128 * (i + 1), c0:c0 + 512],
                   writes=[hc[i]], tag="ld_h")
            if not first_layer:
                em.dma("sync", rc[i][:, :], rin[128 * i:128 * (i + 1),
                                                c0:c0 + 512],
                       writes=[rc[i]], tag="ld_r")
                em.op("vector", lambda v, i=i: v.tensor_tensor(
                    out=rc[i][:, :], in0=rc[i][:, :], in1=hc[i][:, :],
                    op=ALU.add),
                    reads=[rc[i], hc[i]], writes=[rc[i]], tag="hr")
        rsrc = hc if first_layer else rc
        if first_layer:
            ln_feature_major(em, cx, hc, 512, rc, 6, 7, name=f"lnS{c}",
                             scratch=lns)
            rsrc = rc
        for i in range(8):
            em.dma("gpsimd", rout[128 * i:128 * (i + 1), c0:c0 + 512],
                   rsrc[i][:, :], reads=[rsrc[i]], tag="out")
        ln_feature_major(em, cx, rsrc, 512, xnc, 6, 7, gain_bias=gb_cols,
                         name=f"lnM{c}", scratch=lns)
        for half, ws in ((0, wpA_s), (1, wpB_s)):
            for ot in range(8):
                ps = cx.pp[ot % 2]
                for k in range(8):
                    em.op("tensor", lambda t, ot=ot, k=k, ps=ps, ws=ws: t.matmul(
                        out=ps[:, :], lhsT=ws[k][:, 128 * ot:128 * (ot + 1)],
                        rhs=xnc[k][:, :], start=(k == 0), stop=(k == 7)),
                        reads=[ws[k], xnc[k]], writes=[ps], tag="mm_ip")
                if half == 1:
                    em.op("scalar", lambda s, ps=ps: s.activation(
                        out=zb[:, :], in_=ps[:, :], func=AF.Copy),
                        reads=[ps], writes=[zb], tag="ev_z")
                    em.dma("gpsimd", z_sp.ap()[128 * ot:128 * (ot + 1),
                                               c0:c0 + 512], zb[:, :],
                           reads=[zb], writes=["z_sp"], tag="sp_z")
                else:
                    xt = xct[ot]
                    if c == 0:
                        em.op("gpsimd", lambda g, xt=xt: g.memset(
                            xt[:, 0:3], 0.0), writes=[xt], tag="cpad")
                    em.op("vector", lambda v, ps=ps, xt=xt: v.tensor_copy(
                        out=xt[:, 3:515], in_=ps[:, :]),
                        reads=[ps], writes=[xt], tag="ev_xc")
                    acc = build_mamba_ab._acc
                    em.op("vector", lambda v, xt=xt, ot=ot: v.tensor_scalar(
                        out=acc[:, :], in0=xt[:, 0:512],
                        scalar1=wcp_s[:, 4 * ot:4 * ot + 1],
                        scalar2=wcb_s[:, ot:ot + 1],
                        op0=ALU.mult, op1=ALU.add),
                        reads=[xt, wcp_s, wcb_s], writes=[acc], tag="cv0")
                    for kk in range(1, 4):
                        em.op("vector", lambda v, xt=xt, ot=ot, kk=kk, acc=acc:
                              v.scalar_tensor_tensor(
                                  out=acc[:, :], in0=xt[:, kk:kk + 512],
                                  scalar=wcp_s[:, 4 * ot + kk:4 * ot + kk + 1],
                                  in1=acc[:, :], op0=ALU.mult, op1=ALU.add),
                              reads=[xt, wcp_s, acc], writes=[acc], tag="cvk")
                    th_ = build_mamba_ab._th
                    em.op("scalar", lambda s, acc=acc, th_=th_: s.activation(
                        out=th_[:, :], in_=acc[:, :], func=AF.Tanh, scale=0.5),
                        reads=[acc], writes=[th_], tag="tanh")
                    em.op("vector", lambda v, acc=acc, th_=th_: v.scalar_tensor_tensor(
                        out=xcb[:, :], in0=th_[:, :], scalar=1.0, in1=acc[:, :],
                        op0=ALU.add, op1=ALU.mult),
                        reads=[th_, acc], writes=[xcb], tag="silu2")
                    em.dma("gpsimd", xcs_sp.ap()[128 * ot:128 * (ot + 1),
                                                 c0:c0 + 512], xcb[:, :],
                           reads=[xcb], writes=["xcs_sp"], tag="sp_xc")
                    em.op("vector", lambda v, xt=xt: v.tensor_copy(
                        out=xt[:, 0:3], in_=xt[:, 512:515]),
                        reads=[xt], writes=[xt], tag="c3")
                    em.op("tensor", lambda t, ot=ot: t.matmul(
                        out=cx.pp[3][0:96, :], lhsT=wx_s[ot][:, :],
                        rhs=xcb[:, :], start=(ot == 0), stop=(ot == 7)),
                        reads=[wx_s[ot], xcb], writes=[cx.pp[3]], tag="mm_xp")
        em.op("vector", lambda v, c0=c0: v.tensor_copy(
            out=xdb_s[:, c0:c0 + 512], in_=cx.pp[3][0:96, :]),
            reads=[cx.pp[3]], writes=[xdb_s], tag="ev_xdb")
    em.dma("gpsimd", xdb_in.ap(), xdb_s[:, :], reads=[xdb_s],
           writes=["xdb_i"], tag="sp_xdb")
    em.cc("AllReduce", ALU.add, [[0, 1], [2, 3], [4, 5], [6, 7]],
          [xdb_in.ap().opt()], [xdb_out.ap().opt()],
          reads=["xdb_i"], writes=["xdb_o"], tag="cc_xdb")
    em.dma("sync", xdb_s[:, :], xdb_out.ap(), reads=["xdb_o"],
           writes=[xdb_s], tag="ld_xdb")
    xdbD = em.sbuf("xdbD", (64, T), BF16)
    em.op("scalar", lambda s: s.activation(out=xdbD[:, :], in_=xdb_s[0:64, :],
                                           func=AF.Copy),
          reads=[xdb_s], writes=[xdbD], tag="c_xdbD")
    xdbB_f = em.sbuf("xdbBf", (32, 512), F32)
    em.dma("gpsimd", xdbB_f[:, :], xdb_s[64:96, :], reads=[xdb_s],
           writes=[xdbB_f], tag="mv_bc")
    xdbB = em.sbuf("xdbB", (32, T), BF16)
    em.op("scalar", lambda s: s.activation(out=xdbB[:, :], in_=xdbB_f[:, :],
                                           func=AF.Copy),
          reads=[xdbB_f], writes=[xdbB], tag="c_xdbB")

    dtt = em.sbuf("dtt", (128, 512), F32)
    dtx = em.sbuf("dtx", (128, 512), F32)
    xcf = em.sbuf("xcf", (128, 512), F32)
    zf = em.sbuf("zf", (128, 512), F32)
    sb1 = em.sbuf("sb1", (128, 512), F32)
    uu = [em.sbuf(f"uu{i}", (128, 512), F32) for i in range(2)]
    dA = [em.sbuf(f"dA{i}", (128, 512), F32) for i in range(2)]
    hh = [em.sbuf(f"hh{i}", (128, 512), F32) for i in range(2)]
    pb = [em.sbuf(f"pb{i}", (128, 512), BF16) for i in range(2)]
    carry = [em.sbuf(f"carry{i}", (128, 16), F32) for i in range(8)]
    yv = em.sbuf("yv", (128, 512), F32)
    yf = em.sbuf("yf", (128, 512), BF16)
    for i in range(8):
        em.op("gpsimd", lambda g, i=i: g.memset(carry[i][:, :], 0.0),
              writes=[carry[i]], tag="c0")
    for c in range(NCH):
        c0 = 512 * c
        for dtl in range(8):
            em.op("tensor", lambda t, dtl=dtl, c0=c0: t.matmul(
                out=cx.pp[2][:, :], lhsT=wdt_s[:, 128 * dtl:128 * (dtl + 1)],
                rhs=xdbD[:, c0:c0 + 512], start=True, stop=True),
                reads=[wdt_s, xdbD], writes=[cx.pp[2]], tag="mm_dt")
            em.op("scalar", lambda s, dtl=dtl: s.activation(
                out=sb1[:, :], in_=cx.pp[2][:, :], func=AF.Abs,
                bias=dtb_s[:, dtl:dtl + 1]),
                reads=[cx.pp[2], dtb_s], writes=[sb1], tag="sp_abs")
            em.op("scalar", lambda s: s.activation(
                out=sb1[:, :], in_=sb1[:, :], func=AF.Exp, scale=-1.0),
                reads=[sb1], writes=[sb1], tag="sp_exp")
            em.op("scalar", lambda s: s.activation(
                out=sb1[:, :], in_=sb1[:, :], func=AF.Ln, bias=1.0),
                reads=[sb1], writes=[sb1], tag="sp_ln")
            em.op("scalar", lambda s, dtl=dtl: s.activation(
                out=dtt[:, :], in_=cx.pp[2][:, :], func=AF.Relu,
                bias=dtb_s[:, dtl:dtl + 1]),
                reads=[cx.pp[2], dtb_s], writes=[dtt], tag="sp_relu")
            em.op("vector", lambda v: v.tensor_tensor(
                out=dtt[:, :], in0=dtt[:, :], in1=sb1[:, :], op=ALU.add),
                reads=[dtt, sb1], writes=[dtt], tag="sp_add")
            em.dma("sync", xcb[:, :], xcs_sp.ap()[128 * dtl:128 * (dtl + 1),
                                                  c0:c0 + 512],
                   reads=["xcs_sp"], writes=[xcb], tag="ld_xcs")
            em.op("scalar", lambda s: s.activation(out=xcf[:, :], in_=xcb[:, :],
                                                   func=AF.Copy),
                  reads=[xcb], writes=[xcf], tag="c_xcf")
            em.op("scalar", lambda s: s.activation(out=xcf[:, :], in_=xcf[:, :],
                                                   func=AF.Copy, scale=0.5),
                  reads=[xcf], writes=[xcf], tag="half_xc")
            em.op("vector", lambda v: v.tensor_tensor(
                out=dtx[:, :], in0=dtt[:, :], in1=xcf[:, :], op=ALU.mult),
                reads=[dtt, xcf], writes=[dtx], tag="dtx")
            for n in range(16):
                ps_b = cx.pp[3]
                ps_c = cx.pp[4]
                da_ = dA[n % 2]
                u_ = uu[n % 2]
                h_ = hh[n % 2]
                p_ = pb[n % 2]
                em.op("tensor", lambda t, n=n, c0=c0, ps_b=ps_b: t.matmul(
                    out=ps_b[:, :], lhsT=sel_s[:, 128 * n:128 * (n + 1)],
                    rhs=xdbB[:, c0:c0 + 512], start=True, stop=True),
                    reads=[sel_s, xdbB], writes=[ps_b], tag="bc_B")
                em.op("scalar", lambda s, n=n, dtl=dtl, da_=da_: s.activation(
                    out=da_[:, :], in_=dtt[:, :], func=AF.Exp,
                    scale=As_s[:, 16 * dtl + n:16 * dtl + n + 1]),
                    reads=[dtt, As_s], writes=[da_], tag="dA")
                em.op("vector", lambda v, u_=u_, ps_b=ps_b: v.tensor_tensor(
                    out=u_[:, :], in0=dtx[:, :], in1=ps_b[:, :], op=ALU.mult),
                    reads=[dtx, ps_b], writes=[u_], tag="u")
                em.op("vector", lambda v, da_=da_, u_=u_, h_=h_, dtl=dtl, n=n:
                      v.tensor_tensor_scan(
                          out=h_[:, :], data0=da_[:, :], data1=u_[:, :],
                          initial=carry[dtl][:, n:n + 1], op0=ALU.mult,
                          op1=ALU.add),
                      reads=[da_, u_, carry[dtl]], writes=[h_], tag="scan")
                em.op("vector", lambda v, h_=h_, dtl=dtl, n=n: v.tensor_copy(
                    out=carry[dtl][:, n:n + 1], in_=h_[:, 511:512]),
                    reads=[h_], writes=[carry[dtl]], tag="carry")
                em.op("tensor", lambda t, n=n, c0=c0, ps_c=ps_c: t.matmul(
                    out=ps_c[:, :], lhsT=sel_s[:, 2048 + 128 * n:2048 + 128 * (n + 1)],
                    rhs=xdbB[:, c0:c0 + 512], start=True, stop=True),
                    reads=[sel_s, xdbB], writes=[ps_c], tag="bc_C")
                em.op("vector", lambda v, h_=h_, p_=p_, ps_c=ps_c: v.tensor_tensor(
                    out=p_[:, :], in0=h_[:, :], in1=ps_c[:, :], op=ALU.mult),
                    reads=[h_, ps_c], writes=[p_], tag="p")
                em.op("tensor", lambda t, n=n, p_=p_: t.matmul(
                    out=cx.pp[5][:, :], lhsT=cx.ident[:, :], rhs=p_[:, :],
                    start=(n == 0), stop=(n == 15)),
                    reads=[cx.ident, p_], writes=[cx.pp[5]], tag="mm_y")
            em.dma("sync", zb[:, :], z_sp.ap()[128 * dtl:128 * (dtl + 1),
                                               c0:c0 + 512],
                   reads=["z_sp"], writes=[zb], tag="ld_z")
            em.op("scalar", lambda s: s.activation(out=zf[:, :], in_=zb[:, :],
                                                   func=AF.Copy),
                  reads=[zb], writes=[zf], tag="c_zf")
            em.op("scalar", lambda s: s.activation(out=sb1[:, :], in_=zf[:, :],
                                                   func=AF.Tanh, scale=0.5),
                  reads=[zf], writes=[sb1], tag="tanh_z")
            em.op("vector", lambda v: v.scalar_tensor_tensor(
                out=zf[:, :], in0=sb1[:, :], scalar=1.0, in1=zf[:, :],
                op0=ALU.add, op1=ALU.mult),
                reads=[sb1, zf], writes=[zf], tag="siluz")
            em.op("vector", lambda v, dtl=dtl: v.scalar_tensor_tensor(
                out=yv[:, :], in0=xcf[:, :], scalar=dsk_s[:, dtl:dtl + 1],
                in1=cx.pp[5][:, :], op0=ALU.mult, op1=ALU.add),
                reads=[xcf, dsk_s, cx.pp[5]], writes=[yv], tag="yv")
            em.op("vector", lambda v: v.tensor_tensor(
                out=yf[:, :], in0=yv[:, :], in1=zf[:, :], op=ALU.mult),
                reads=[yv, zf], writes=[yf], tag="yf")
            em.dma("gpsimd", yout[128 * dtl:128 * (dtl + 1), c0:c0 + 512],
                   yf[:, :], reads=[yf], tag="out")
    em.finalize()
    em.close()
    return nc


def build_mamba_c():
    nc = bass.Bass("TRN2", target_bir_lowering=False)
    em = Em(nc)
    yin = nc.dram_tensor("yin", [2048, TH], BF16, kind="ExternalInput").ap()
    ident_in = nc.dram_tensor("ident", [128, 128], BF16, kind="ExternalInput").ap()
    wot = nc.dram_tensor("wot", [2048, D], F32, kind="ExternalInput").ap()
    hout = nc.dram_tensor("hout", [D, TH], F32, kind="ExternalOutput").ap()
    cx = Ctx(em, ident_in)
    stage = em.sbuf("stage", (128, D), F32)
    wot_s = [em.sbuf(f"wot{k}", (128, D), BF16) for k in range(16)]
    for k in range(16):
        em.dma("sync", stage[:, :], wot[128 * k:128 * (k + 1), :],
               writes=[stage], tag="ld_wot")
        em.op("scalar", lambda s, k=k: s.activation(
            out=wot_s[k][:, :], in_=stage[:, :], func=AF.Copy, scale=0.5),
            reads=[stage], writes=[wot_s[k]], tag="c_wot")
    yc = [em.sbuf(f"yc{i}", (128, 512), BF16) for i in range(16)]
    ho = [em.sbuf(f"ho{i}", (128, 512), F32) for i in range(2)]
    for c in range(TH // 512):
        c0 = 512 * c
        for k in range(16):
            em.dma("sync", yc[k][:, :], yin[128 * k:128 * (k + 1), c0:c0 + 512],
                   writes=[yc[k]], tag="ld_y")
        for ot in range(8):
            ps = cx.pp[ot % 2]
            for k in range(16):
                em.op("tensor", lambda t, ot=ot, k=k, ps=ps: t.matmul(
                    out=ps[:, :], lhsT=wot_s[k][:, 128 * ot:128 * (ot + 1)],
                    rhs=yc[k][:, :], start=(k == 0), stop=(k == 15)),
                    reads=[wot_s[k], yc[k]], writes=[ps], tag="mm_op")
            hx = ho[ot % 2]
            em.op("vector", lambda v, ps=ps, hx=hx: v.tensor_copy(
                out=hx[:, :], in_=ps[:, :]),
                reads=[ps], writes=[hx], tag="ev_h")
            em.dma("gpsimd", hout[128 * ot:128 * (ot + 1), c0:c0 + 512],
                   hx[:, :], reads=[hx], tag="out")
    em.finalize()
    em.close()
    return nc


def build_mamba_f():
    nc = bass.Bass("TRN2", target_bir_lowering=False)
    em = Em(nc)
    hin = nc.dram_tensor("hin", [D, TH], F32, kind="ExternalInput").ap()
    rin = nc.dram_tensor("rin", [D, TH], F32, kind="ExternalInput").ap()
    xst = nc.dram_tensor("xst", [D, TH], F32, kind="ExternalInput").ap()
    ident_in = nc.dram_tensor("ident", [128, 128], BF16, kind="ExternalInput").ap()
    nfw = nc.dram_tensor("nfw", [128, 8], F32, kind="ExternalInput").ap()
    nfb = nc.dram_tensor("nfb", [128, 8], F32, kind="ExternalInput").ap()
    xout = nc.dram_tensor("xoT", [D, TH], F32, kind="ExternalOutput").ap()
    cx = Ctx(em, ident_in)
    lns = ln_scratch(em)
    nfw_s = em.sbuf("nfw", (128, 8), F32)
    nfb_s = em.sbuf("nfb", (128, 8), F32)
    em.dma("sync", nfw_s[:, :], nfw, writes=[nfw_s], tag="ld_nfw")
    em.dma("sync", nfb_s[:, :], nfb, writes=[nfb_s], tag="ld_nfb")
    gbc = [(nfw_s[:, i:i + 1], nfb_s[:, i:i + 1]) for i in range(8)]
    hc = [em.sbuf(f"hc{i}", (128, 512), F32) for i in range(8)]
    rcn = [em.sbuf(f"rc{i}", (128, 512), F32) for i in range(8)]
    lno = [em.sbuf(f"lno{i}", (128, 512), F32) for i in range(8)]
    xs = em.sbuf("xs", (128, 512), F32)
    for c in range(TH // 512):
        c0 = 512 * c
        for i in range(8):
            em.dma("sync", hc[i][:, :], hin[128 * i:128 * (i + 1), c0:c0 + 512],
                   writes=[hc[i]], tag="ld_h")
            em.dma("sync", rcn[i][:, :], rin[128 * i:128 * (i + 1), c0:c0 + 512],
                   writes=[rcn[i]], tag="ld_r")
            em.op("vector", lambda v, i=i: v.tensor_tensor(
                out=rcn[i][:, :], in0=rcn[i][:, :], in1=hc[i][:, :], op=ALU.add),
                reads=[rcn[i], hc[i]], writes=[rcn[i]], tag="hr")
        if c == 0:
            build_mamba_f._lns = ln_scratch(em)
        ln_feature_major(em, cx, rcn, 512, lno, 6, 7, gain_bias=gbc,
                         name=f"lnF{c}", scratch=build_mamba_f._lns)
        for i in range(8):
            em.dma("sync", xs[:, :], xst[128 * i:128 * (i + 1), c0:c0 + 512],
                   writes=[xs], tag="ld_xs")
            em.op("vector", lambda v, i=i: v.tensor_tensor(
                out=lno[i][:, :], in0=lno[i][:, :], in1=xs[:, :], op=ALU.add),
                reads=[lno[i], xs], writes=[lno[i]], tag="resid")
            em.dma("gpsimd", xout[128 * i:128 * (i + 1), c0:c0 + 512],
                   lno[i][:, :], reads=[lno[i]], tag="out")
    em.finalize()
    em.close()
    return nc


def _mamba_stage(X_T, norm_w, norm_b, in_proj_w, conv_w, conv_b, x_proj_w,
                 dt_proj_w, dt_proj_b, A_log, D_skip, out_proj_w,
                 normf_w, normf_b):
    NL = 4
    pairs = [(b, dh) for b in range(B) for dh in range(2)]
    H = X_T
    R = None
    for l in range(NL):
        nc_ab = _get_prog("m_ab", build_mamba_ab, l == 0)
        ims = []
        for (b, dh) in pairs:
            d0 = dh * 1024
            wpAl = np.ascontiguousarray(in_proj_w[l].T[:, d0:d0 + 1024])
            wpBl = np.ascontiguousarray(in_proj_w[l].T[:, 2048 + d0:2048 + d0 + 1024])
            wxl = np.ascontiguousarray(x_proj_w[l].T[d0:d0 + 1024, :])
            wdtl = np.ascontiguousarray(dt_proj_w[l].T[:, d0:d0 + 1024])
            pk8 = lambda a: np.ascontiguousarray(a.reshape(8, 128).T)
            sel = np.zeros((32, 32 * 128), np.float32)
            for j in range(32):
                sel[j, 128 * j:128 * (j + 1)] = 0.5 if j < 16 else 1.0
            wcpk = np.zeros((128, 32), np.float32)
            for dt_ in range(8):
                wcpk[:, 4 * dt_:4 * dt_ + 4] = conv_w[l][d0 + 128 * dt_:
                                                         d0 + 128 * (dt_ + 1), :]
            alogk = np.zeros((128, 128), np.float32)
            for dt_ in range(8):
                alogk[:, 16 * dt_:16 * (dt_ + 1)] = A_log[l][d0 + 128 * dt_:
                                                             d0 + 128 * (dt_ + 1), :]
            im = {"hin": np.ascontiguousarray(H[b]), "ident": _IDENT,
                  "wpA": wpAl, "wpB": wpBl, "wx": wxl, "wdt": wdtl,
                  "nw": pk8(norm_w[l]), "nb": pk8(norm_b[l]),
                  "wcp": wcpk, "wcb": pk8(conv_b[l][d0:d0 + 1024]),
                  "dtb": pk8(dt_proj_b[l][d0:d0 + 1024]),
                  "dsk": pk8(D_skip[l][d0:d0 + 1024]),
                  "alog": alogk, "sel": sel.astype(BF)}
            if l > 0:
                im["rin"] = np.ascontiguousarray(R[b])
            ims.append(im)
        res = run_bass_kernel_spmd(nc_ab, ims, core_ids=list(range(8)))
        Rn = np.empty((B, D, T), np.float32)
        YF = np.empty((B, 2048, T), dtype=BF)
        for ci, (b, dh) in enumerate(pairs):
            if dh == 0:
                Rn[b] = np.asarray(res.results[ci]["rout"])
            YF[b][1024 * dh:1024 * (dh + 1)] = np.asarray(res.results[ci]["yout"])
        R = Rn
        nc_c = _get_prog("m_c", build_mamba_c)
        wotl = np.ascontiguousarray(out_proj_w[l].T)
        ims = []
        for c in range(NCORES):
            b, th = c // 2, c % 2
            ims.append({"yin": np.ascontiguousarray(
                YF[b][:, th * TH:(th + 1) * TH]), "ident": _IDENT, "wot": wotl})
        res = run_bass_kernel_spmd(nc_c, ims, core_ids=list(range(NCORES)))
        Hn = np.empty((B, D, T), np.float32)
        for c in range(NCORES):
            b, th = c // 2, c % 2
            Hn[b][:, th * TH:(th + 1) * TH] = np.asarray(res.results[c]["hout"])
        H = Hn
    nc_f = _get_prog("m_f", build_mamba_f)
    ims = []
    for c in range(NCORES):
        b, th = c // 2, c % 2
        sl = slice(th * TH, (th + 1) * TH)
        ims.append({"hin": np.ascontiguousarray(H[b][:, sl]),
                    "rin": np.ascontiguousarray(R[b][:, sl]),
                    "xst": np.ascontiguousarray(X_T[b][:, sl]),
                    "ident": _IDENT,
                    "nfw": np.ascontiguousarray(normf_w.reshape(8, 128).T),
                    "nfb": np.ascontiguousarray(normf_b.reshape(8, 128).T)})
    res = run_bass_kernel_spmd(nc_f, ims, core_ids=list(range(NCORES)))
    X2 = np.empty_like(X_T)
    for c in range(NCORES):
        b, th = c // 2, c % 2
        X2[b][:, th * TH:(th + 1) * TH] = np.asarray(res.results[c]["xoT"])
    return X2



PAIRS = [[0, 1], [2, 3], [4, 5], [6, 7]]


def build_mamba_fused():
    nc = bass.Bass("TRN2", target_bir_lowering=False)
    em = Em(nc)
    ap = lambda n, s, d=F32: nc.dram_tensor(n, s, d, kind="ExternalInput").ap()
    xin = ap("xin", [D, T])
    ident_in = ap("ident", [128, 128], BF16)
    wpA = [ap(f"m{l}_wpA", [D, 1024]) for l in range(4)]
    wpB = [ap(f"m{l}_wpB", [D, 1024]) for l in range(4)]
    wx = [ap(f"m{l}_wx", [1024, 96]) for l in range(4)]
    wdt = [ap(f"m{l}_wdt", [64, 1024]) for l in range(4)]
    wot = [ap(f"m{l}_wot", [1024, D]) for l in range(4)]
    nw = [ap(f"m{l}_nw", [128, 8]) for l in range(4)]
    nb = [ap(f"m{l}_nb", [128, 8]) for l in range(4)]
    wcp = [ap(f"m{l}_wcp", [128, 32]) for l in range(4)]
    wcb = [ap(f"m{l}_wcb", [128, 8]) for l in range(4)]
    dtb = [ap(f"m{l}_dtb", [128, 8]) for l in range(4)]
    dsk = [ap(f"m{l}_dsk", [128, 8]) for l in range(4)]
    alog = [ap(f"m{l}_alog", [128, 128]) for l in range(4)]
    sel = ap("m_sel", [32, 32 * 128], BF16)
    nfw = ap("m_nfw", [128, 8])
    nfb = ap("m_nfb", [128, 8])
    xo = nc.dram_tensor("xoT", [D, T], F32, kind="ExternalOutput").ap()
    dr = lambda n, s, d=F32: nc.dram_tensor(n, s, d)
    RS = dr("RS", [D, T])
    xdb_i = dr("xdb_i", [96, T])
    xdb_o = dr("xdb_o", [96, T])
    xcs_sp = dr("xcs_sp", [1024, T], BF16)
    z_sp = dr("z_sp", [1024, T], BF16)
    ar_i = dr("ar_i", [D, T], BF16)
    ar_o = [dr(f"ar_o{i}", [D, T], BF16) for i in range(2)]

    cx = Ctx(em, ident_in)
    lns = ln_scratch(em)
    stage = em.sbuf("stage", (128, 1024), F32)
    sel_s = em.sbuf("sel", (32, 32 * 128), BF16)
    em.dma("sync", sel_s[:, :], sel, writes=[sel_s], tag="ld_sel")
    nfw_s = em.sbuf("nfw", (128, 8), F32)
    nfb_s = em.sbuf("nfb", (128, 8), F32)
    em.dma("sync", nfw_s[:, :], nfw, writes=[nfw_s], tag="ld_nfw")
    em.dma("sync", nfb_s[:, :], nfb, writes=[nfb_s], tag="ld_nfb")

    par_s = {}
    for l in range(4):
        for nm, srcp, sh in [("nw", nw[l], (128, 8)), ("nb", nb[l], (128, 8)),
                             ("wcp", wcp[l], (128, 32)), ("wcb", wcb[l], (128, 8)),
                             ("dtb", dtb[l], (128, 8)), ("dsk", dsk[l], (128, 8)),
                             ("alog", alog[l], (128, 128))]:
            t = em.sbuf(f"m{l}_{nm}", sh, F32)
            em.dma("sync", t[:, :], srcp, writes=[t], tag=f"ld_{nm}")
            par_s[(l, nm)] = t
        As = em.sbuf(f"m{l}_As", (128, 128), F32)
        em.op("scalar", lambda s, l=l, As=As: s.activation(
            out=As[:, :], in_=par_s[(l, "alog")][:, :], func=AF.Exp),
            reads=[par_s[(l, "alog")]], writes=[As], tag="mkA")
        em.op("vector", lambda v, As=As: v.tensor_scalar(
            out=As[:, :], in0=As[:, :], scalar1=-1.0, scalar2=0.0,
            op0=ALU.mult, op1=ALU.add),
            reads=[As], writes=[As], tag="negA")
        par_s[(l, "As")] = As

    wpA_s = [em.sbuf(f"wpAs{k}", (128, 1024), BF16) for k in range(8)]
    wpB_s = [em.sbuf(f"wpBs{k}", (128, 1024), BF16) for k in range(8)]
    wx_s = [em.sbuf(f"wxs{k}", (128, 96), BF16) for k in range(8)]
    wdt_s = em.sbuf("wdts", (64, 1024), BF16)
    wot_s = [em.sbuf(f"wots{k}", (128, D), BF16) for k in range(8)]

    hc = [em.sbuf(f"hc{i}", (128, 512), F32) for i in range(8)]
    rc = [em.sbuf(f"rc{i}", (128, 512), F32) for i in range(8)]
    xnc = [em.sbuf(f"xnc{i}", (128, 512), BF16) for i in range(8)]
    xct = [em.sbuf(f"xct{i}", (128, 515), F32) for i in range(8)]
    zb = em.sbuf("zbm", (128, 512), BF16)
    xcb = em.sbuf("xcbm", (128, 512), BF16)
    hbb = em.sbuf("hbb", (128, 512), BF16)
    xdb_s = em.sbuf("xdbs", (96, T), F32)
    cacc = em.sbuf("cacc", (128, 512), F32)
    thm = em.sbuf("thm", (128, 512), F32)
    xdbD = em.sbuf("xdbD", (64, 512), BF16)
    xdbDf = em.sbuf("xdbDf", (64, 512), F32)
    xdbB_f = em.sbuf("xdbBf", (32, 512), F32)
    xdbB = em.sbuf("xdbB", (32, 512), BF16)
    dtt = em.sbuf("dtt", (128, 512), F32)
    dtx = em.sbuf("dtx", (128, 512), F32)
    xcf = em.sbuf("xcf", (128, 512), F32)
    zf = em.sbuf("zf", (128, 512), F32)
    sb1 = em.sbuf("sb1", (128, 512), F32)
    uu = [em.sbuf("uu0", (128, 512), F32)] * 2
    dA = [em.sbuf(f"dA{i}", (128, 512), F32) for i in range(2)]
    hh = [em.sbuf(f"hh{i}", (128, 512), F32) for i in range(2)]
    pb = [em.sbuf("pb0", (128, 512), BF16)] * 2
    carry = [em.sbuf(f"carry{i}", (128, 16), F32) for i in range(8)]
    yv = em.sbuf("yv", (128, 512), F32)
    yf = [em.sbuf(f"yft{i}", (128, 512), BF16) for i in range(8)]

    NCH = T // 512
    for l in range(4):
        for k in range(8):
            em.dma("sync", stage[:, 0:1024], wpA[l][128 * k:128 * (k + 1), :],
                   writes=[stage], tag="ld_wpA")
            em.op("vector", lambda v, k=k: v.tensor_copy(
                out=wpA_s[k][:, :], in_=stage[:, 0:1024]),
                reads=[stage], writes=[wpA_s[k]], tag="c_wpA")
            em.dma("sync", stage[:, 0:1024], wpB[l][128 * k:128 * (k + 1), :],
                   writes=[stage], tag="ld_wpB")
            em.op("vector", lambda v, k=k: v.tensor_copy(
                out=wpB_s[k][:, :], in_=stage[:, 0:1024]),
                reads=[stage], writes=[wpB_s[k]], tag="c_wpB")
            em.dma("sync", stage[:, 0:96], wx[l][128 * k:128 * (k + 1), :],
                   writes=[stage], tag="ld_wx")
            em.op("scalar", lambda s, k=k: s.activation(
                out=wx_s[k][:, :], in_=stage[:, 0:96], func=AF.Copy, scale=0.5),
                reads=[stage], writes=[wx_s[k]], tag="c_wx")
            em.dma("sync", stage[:, 0:1024], wot[l][128 * k:128 * (k + 1), :],
                   writes=[stage], tag="ld_wot")
            em.op("scalar", lambda s, k=k: s.activation(
                out=wot_s[k][:, :], in_=stage[:, 0:1024], func=AF.Copy,
                scale=0.5),
                reads=[stage], writes=[wot_s[k]], tag="c_wot")
        em.dma("sync", stage[0:64, :], wdt[l], writes=[stage], tag="ld_wdt")
        em.op("vector", lambda v: v.tensor_copy(out=wdt_s[:, :],
                                                in_=stage[0:64, :]),
              reads=[stage], writes=[wdt_s], tag="c_wdt")

        gb_cols = [(par_s[(l, "nw")][:, i:i + 1], par_s[(l, "nb")][:, i:i + 1])
                   for i in range(8)]
        for c in range(NCH):
            c0 = 512 * c
            for i in range(8):
                if l == 0:
                    em.dma("sync", hc[i][:, :],
                           xin[128 * i:128 * (i + 1), c0:c0 + 512],
                           writes=[hc[i]], tag="ld_h")
                else:
                    em.dma("sync", hbb[:, :],
                           ar_o[(l - 1) % 2].ap()[128 * i:128 * (i + 1),
                                                  c0:c0 + 512],
                           reads=[f"ar_o{(l - 1) % 2}"], writes=[hbb],
                           tag="ld_hb")
                    em.op("scalar", lambda s, i=i: s.activation(
                        out=hc[i][:, :], in_=hbb[:, :], func=AF.Copy),
                        reads=[hbb], writes=[hc[i]], tag="cv_h")
                    em.dma("sync", rc[i][:, :],
                           RS.ap()[128 * i:128 * (i + 1), c0:c0 + 512],
                           reads=["RS"], writes=[rc[i]], tag="ld_r")
                    em.op("vector", lambda v, i=i: v.tensor_tensor(
                        out=rc[i][:, :], in0=rc[i][:, :], in1=hc[i][:, :],
                        op=ALU.add),
                        reads=[rc[i], hc[i]], writes=[rc[i]], tag="hr")
            if l == 0:
                ln_feature_major(em, cx, hc, 512, rc, 6, 7, name=f"lnS{l}_{c}",
                                 scratch=lns)
            for i in range(8):
                em.dma("gpsimd", RS.ap()[128 * i:128 * (i + 1), c0:c0 + 512],
                       rc[i][:, :], reads=[rc[i]], writes=["RS"], tag="sp_r")
            ln_feature_major(em, cx, rc, 512, xnc, 6, 7, gain_bias=gb_cols,
                             name=f"lnM{l}_{c}", scratch=lns)
            for half, ws in ((0, wpA_s), (1, wpB_s)):
                for ot in range(8):
                    ps = cx.pp[ot % 2]
                    for k in range(8):
                        em.op("tensor", lambda t, ot=ot, k=k, ps=ps, ws=ws:
                              t.matmul(out=ps[:, :],
                                       lhsT=ws[k][:, 128 * ot:128 * (ot + 1)],
                                       rhs=xnc[k][:, :], start=(k == 0),
                                       stop=(k == 7)),
                              reads=[ws[k], xnc[k]], writes=[ps], tag="mm_ip")
                    if half == 1:
                        em.op("scalar", lambda s, ps=ps: s.activation(
                            out=zb[:, :], in_=ps[:, :], func=AF.Copy),
                            reads=[ps], writes=[zb], tag="ev_z")
                        em.dma("gpsimd", z_sp.ap()[128 * ot:128 * (ot + 1),
                                                   c0:c0 + 512], zb[:, :],
                               reads=[zb], writes=["z_sp"], tag="sp_z")
                    else:
                        xt = xct[ot]
                        if c == 0:
                            em.op("gpsimd", lambda g, xt=xt: g.memset(
                                xt[:, 0:3], 0.0), writes=[xt], tag="cpad")
                        em.op("vector", lambda v, ps=ps, xt=xt: v.tensor_copy(
                            out=xt[:, 3:515], in_=ps[:, :]),
                            reads=[ps], writes=[xt], tag="ev_xc")
                        acc = cacc
                        em.op("vector", lambda v, xt=xt, ot=ot, l=l: v.tensor_scalar(
                            out=acc[:, :], in0=xt[:, 0:512],
                            scalar1=par_s[(l, "wcp")][:, 4 * ot:4 * ot + 1],
                            scalar2=par_s[(l, "wcb")][:, ot:ot + 1],
                            op0=ALU.mult, op1=ALU.add),
                            reads=[xt, par_s[(l, "wcp")], par_s[(l, "wcb")]],
                            writes=[acc], tag="cv0")
                        for kk in range(1, 4):
                            em.op("vector", lambda v, xt=xt, ot=ot, kk=kk, l=l:
                                  v.scalar_tensor_tensor(
                                      out=acc[:, :], in0=xt[:, kk:kk + 512],
                                      scalar=par_s[(l, "wcp")][:, 4 * ot + kk:
                                                               4 * ot + kk + 1],
                                      in1=acc[:, :], op0=ALU.mult, op1=ALU.add),
                                  reads=[xt, par_s[(l, "wcp")], acc],
                                  writes=[acc], tag="cvk")
                        em.op("scalar", lambda s: s.activation(
                            out=thm[:, :], in_=acc[:, :], func=AF.Tanh,
                            scale=0.5),
                            reads=[acc], writes=[thm], tag="tanh")
                        em.op("vector", lambda v: v.scalar_tensor_tensor(
                            out=xcb[:, :], in0=thm[:, :], scalar=1.0,
                            in1=acc[:, :], op0=ALU.add, op1=ALU.mult),
                            reads=[thm, acc], writes=[xcb], tag="silu2")
                        em.dma("gpsimd", xcs_sp.ap()[128 * ot:128 * (ot + 1),
                                                     c0:c0 + 512], xcb[:, :],
                               reads=[xcb], writes=["xcs_sp"], tag="sp_xc")
                        em.op("vector", lambda v, xt=xt: v.tensor_copy(
                            out=xt[:, 0:3], in_=xt[:, 512:515]),
                            reads=[xt], writes=[xt], tag="c3")
                        em.op("tensor", lambda t, ot=ot: t.matmul(
                            out=cx.pp[3][0:96, :], lhsT=wx_s[ot][:, :],
                            rhs=xcb[:, :], start=(ot == 0), stop=(ot == 7)),
                            reads=[wx_s[ot], xcb], writes=[cx.pp[3]],
                            tag="mm_xp")
            em.op("vector", lambda v, c0=c0: v.tensor_copy(
                out=xdb_s[:, c0:c0 + 512], in_=cx.pp[3][0:96, :]),
                reads=[cx.pp[3]], writes=[xdb_s], tag="ev_xdb")
        em.dma("gpsimd", xdb_i.ap(), xdb_s[:, :], reads=[xdb_s],
               writes=["xdb_i"], tag="sp_xdb")
        em.cc("AllReduce", ALU.add, PAIRS, [xdb_i.ap().opt()],
              [xdb_o.ap().opt()], reads=["xdb_i"], writes=["xdb_o"],
              tag="cc_xdb")
        for i in range(8):
            em.op("gpsimd", lambda g, i=i: g.memset(carry[i][:, :], 0.0),
                  writes=[carry[i]], tag="c0")
        for c in range(NCH):
            c0 = 512 * c
            em.dma("sync", xdbDf[:, :], xdb_o.ap()[0:64, c0:c0 + 512],
                   reads=["xdb_o"], writes=[xdbDf], tag="ld_xdbD")
            em.op("scalar", lambda s: s.activation(
                out=xdbD[:, :], in_=xdbDf[:, :], func=AF.Copy),
                reads=[xdbDf], writes=[xdbD], tag="c_xdbD")
            em.dma("sync", xdbB_f[:, :], xdb_o.ap()[64:96, c0:c0 + 512],
                   reads=["xdb_o"], writes=[xdbB_f], tag="ld_xdbB")
            em.op("scalar", lambda s: s.activation(
                out=xdbB[:, :], in_=xdbB_f[:, :], func=AF.Copy),
                reads=[xdbB_f], writes=[xdbB], tag="c_xdbB")
            for dtl in range(8):
                em.op("tensor", lambda t, dtl=dtl: t.matmul(
                    out=cx.pp[2][:, :],
                    lhsT=wdt_s[:, 128 * dtl:128 * (dtl + 1)],
                    rhs=xdbD[:, :], start=True, stop=True),
                    reads=[wdt_s, xdbD], writes=[cx.pp[2]], tag="mm_dt")
                em.op("scalar", lambda s, dtl=dtl, l=l: s.activation(
                    out=sb1[:, :], in_=cx.pp[2][:, :], func=AF.Abs,
                    bias=par_s[(l, "dtb")][:, dtl:dtl + 1]),
                    reads=[cx.pp[2], par_s[(l, "dtb")]], writes=[sb1],
                    tag="sp_abs")
                em.op("scalar", lambda s: s.activation(
                    out=sb1[:, :], in_=sb1[:, :], func=AF.Exp, scale=-1.0),
                    reads=[sb1], writes=[sb1], tag="sp_exp")
                em.op("scalar", lambda s: s.activation(
                    out=sb1[:, :], in_=sb1[:, :], func=AF.Ln, bias=1.0),
                    reads=[sb1], writes=[sb1], tag="sp_ln")
                em.op("scalar", lambda s, dtl=dtl, l=l: s.activation(
                    out=dtt[:, :], in_=cx.pp[2][:, :], func=AF.Relu,
                    bias=par_s[(l, "dtb")][:, dtl:dtl + 1]),
                    reads=[cx.pp[2], par_s[(l, "dtb")]], writes=[dtt],
                    tag="sp_relu")
                em.op("vector", lambda v: v.tensor_tensor(
                    out=dtt[:, :], in0=dtt[:, :], in1=sb1[:, :], op=ALU.add),
                    reads=[dtt, sb1], writes=[dtt], tag="sp_add")
                em.dma("sync", xcb[:, :],
                       xcs_sp.ap()[128 * dtl:128 * (dtl + 1), c0:c0 + 512],
                       reads=["xcs_sp"], writes=[xcb], tag="ld_xcs")
                em.op("scalar", lambda s: s.activation(
                    out=xcf[:, :], in_=xcb[:, :], func=AF.Copy, scale=0.5),
                    reads=[xcb], writes=[xcf], tag="c_xcf")
                em.op("vector", lambda v: v.tensor_tensor(
                    out=dtx[:, :], in0=dtt[:, :], in1=xcf[:, :], op=ALU.mult),
                    reads=[dtt, xcf], writes=[dtx], tag="dtx")
                for n in range(16):
                    ps_b = cx.pp[3]
                    ps_c = cx.pp[4]
                    da_ = dA[n % 2]
                    u_ = uu[n % 2]
                    h_ = hh[n % 2]
                    p_ = pb[n % 2]
                    em.op("tensor", lambda t, n=n, ps_b=ps_b: t.matmul(
                        out=ps_b[:, :], lhsT=sel_s[:, 128 * n:128 * (n + 1)],
                        rhs=xdbB[:, :], start=True, stop=True),
                        reads=[sel_s, xdbB], writes=[ps_b], tag="bc_B")
                    em.op("scalar", lambda s, n=n, dtl=dtl, l=l, da_=da_:
                          s.activation(
                              out=da_[:, :], in_=dtt[:, :], func=AF.Exp,
                              scale=par_s[(l, "As")][:, 16 * dtl + n:
                                                     16 * dtl + n + 1]),
                          reads=[dtt, par_s[(l, "As")]], writes=[da_],
                          tag="dA")
                    em.op("vector", lambda v, u_=u_, ps_b=ps_b: v.tensor_tensor(
                        out=u_[:, :], in0=dtx[:, :], in1=ps_b[:, :],
                        op=ALU.mult),
                        reads=[dtx, ps_b], writes=[u_], tag="u")
                    em.op("vector", lambda v, da_=da_, u_=u_, h_=h_, dtl=dtl,
                          n=n: v.tensor_tensor_scan(
                              out=h_[:, :], data0=da_[:, :], data1=u_[:, :],
                              initial=carry[dtl][:, n:n + 1], op0=ALU.mult,
                              op1=ALU.add),
                          reads=[da_, u_, carry[dtl]], writes=[h_], tag="scan")
                    em.op("vector", lambda v, h_=h_, dtl=dtl, n=n: v.tensor_copy(
                        out=carry[dtl][:, n:n + 1], in_=h_[:, 511:512]),
                        reads=[h_], writes=[carry[dtl]], tag="carry")
                    em.op("tensor", lambda t, n=n, ps_c=ps_c: t.matmul(
                        out=ps_c[:, :],
                        lhsT=sel_s[:, 2048 + 128 * n:2048 + 128 * (n + 1)],
                        rhs=xdbB[:, :], start=True, stop=True),
                        reads=[sel_s, xdbB], writes=[ps_c], tag="bc_C")
                    em.op("vector", lambda v, h_=h_, p_=p_, ps_c=ps_c:
                          v.tensor_tensor(out=p_[:, :], in0=h_[:, :],
                                          in1=ps_c[:, :], op=ALU.mult),
                          reads=[h_, ps_c], writes=[p_], tag="p")
                    em.op("tensor", lambda t, n=n, p_=p_: t.matmul(
                        out=cx.pp[5][:, :], lhsT=cx.ident[:, :], rhs=p_[:, :],
                        start=(n == 0), stop=(n == 15)),
                        reads=[cx.ident, p_], writes=[cx.pp[5]], tag="mm_y")
                em.dma("sync", zb[:, :],
                       z_sp.ap()[128 * dtl:128 * (dtl + 1), c0:c0 + 512],
                       reads=["z_sp"], writes=[zb], tag="ld_z")
                em.op("scalar", lambda s: s.activation(
                    out=zf[:, :], in_=zb[:, :], func=AF.Copy),
                    reads=[zb], writes=[zf], tag="c_zf")
                em.op("scalar", lambda s: s.activation(
                    out=sb1[:, :], in_=zf[:, :], func=AF.Tanh, scale=0.5),
                    reads=[zf], writes=[sb1], tag="tanh_z")
                em.op("vector", lambda v: v.scalar_tensor_tensor(
                    out=zf[:, :], in0=sb1[:, :], scalar=1.0, in1=zf[:, :],
                    op0=ALU.add, op1=ALU.mult),
                    reads=[sb1, zf], writes=[zf], tag="siluz")
                em.op("vector", lambda v, dtl=dtl, l=l: v.scalar_tensor_tensor(
                    out=yv[:, :], in0=xcf[:, :],
                    scalar=par_s[(l, "dsk")][:, dtl:dtl + 1],
                    in1=cx.pp[5][:, :], op0=ALU.mult, op1=ALU.add),
                    reads=[xcf, par_s[(l, "dsk")], cx.pp[5]], writes=[yv],
                    tag="yv")
                em.op("vector", lambda v, dtl=dtl: v.tensor_tensor(
                    out=yf[dtl][:, :], in0=yv[:, :], in1=zf[:, :],
                    op=ALU.mult),
                    reads=[yv, zf], writes=[yf[dtl]], tag="yf")
            for ot in range(8):
                ps = cx.pp[ot % 2]
                for k in range(8):
                    em.op("tensor", lambda t, ot=ot, k=k, ps=ps: t.matmul(
                        out=ps[:, :], lhsT=wot_s[k][:, 128 * ot:128 * (ot + 1)],
                        rhs=yf[k][:, :], start=(k == 0), stop=(k == 7)),
                        reads=[wot_s[k], yf[k]], writes=[ps], tag="mm_op")
                em.op("scalar", lambda s, ps=ps: s.activation(
                    out=hbb[:, :], in_=ps[:, :], func=AF.Copy),
                    reads=[ps], writes=[hbb], tag="ev_op")
                em.dma("gpsimd", ar_i.ap()[128 * ot:128 * (ot + 1),
                                           c0:c0 + 512], hbb[:, :],
                       reads=[hbb], writes=["ar_i"], tag="sp_op")
        em.cc("AllReduce", ALU.add, PAIRS, [ar_i.ap().opt()],
              [ar_o[l % 2].ap().opt()], reads=["ar_i"],
              writes=[f"ar_o{l % 2}"], tag="cc_op")
    gbf = [(nfw_s[:, i:i + 1], nfb_s[:, i:i + 1]) for i in range(8)]
    lno = xnc
    for c in range(NCH):
        c0 = 512 * c
        for i in range(8):
            em.dma("sync", hbb[:, :],
                   ar_o[1].ap()[128 * i:128 * (i + 1), c0:c0 + 512],
                   reads=["ar_o1"], writes=[hbb], tag="ld_h4")
            em.op("scalar", lambda s, i=i: s.activation(
                out=hc[i][:, :], in_=hbb[:, :], func=AF.Copy),
                reads=[hbb], writes=[hc[i]], tag="cv_h4")
            em.dma("sync", rc[i][:, :],
                   RS.ap()[128 * i:128 * (i + 1), c0:c0 + 512],
                   reads=["RS"], writes=[rc[i]], tag="ld_r4")
            em.op("vector", lambda v, i=i: v.tensor_tensor(
                out=rc[i][:, :], in0=rc[i][:, :], in1=hc[i][:, :], op=ALU.add),
                reads=[rc[i], hc[i]], writes=[rc[i]], tag="hr4")
        ln_feature_major(em, cx, rc, 512, hc, 6, 7, gain_bias=gbf,
                         name=f"lnF{c}", scratch=lns)
        for i in range(8):
            em.dma("sync", xct[i][:, 0:512],
                   xin[128 * i:128 * (i + 1), c0:c0 + 512],
                   writes=[xct[i]], tag="ld_xst")
            em.op("vector", lambda v, i=i: v.tensor_tensor(
                out=hc[i][:, :], in0=hc[i][:, :], in1=xct[i][:, 0:512],
                op=ALU.add),
                reads=[hc[i], xct[i]], writes=[hc[i]], tag="fres")
            em.dma("gpsimd", xo[128 * i:128 * (i + 1), c0:c0 + 512],
                   hc[i][:, :], reads=[hc[i]], tag="out")
    em.finalize()
    em.close()
    return nc


def _mamba_stage2(X_T, norm_w, norm_b, in_proj_w, conv_w, conv_b, x_proj_w,
                  dt_proj_w, dt_proj_b, A_log, D_skip, out_proj_w,
                  normf_w, normf_b):
    nc = _get_prog("m_fused", build_mamba_fused)
    pk8 = lambda a: np.ascontiguousarray(a.reshape(8, 128).T)
    sel = np.zeros((32, 32 * 128), np.float32)
    for j in range(32):
        sel[j, 128 * j:128 * (j + 1)] = 0.5 if j < 16 else 1.0
    sel = sel.astype(BF)
    ims = []
    for c in range(NCORES):
        b, dh = c // 2, c % 2
        d0 = dh * 1024
        im = {"xin": np.ascontiguousarray(X_T[b]), "ident": _IDENT,
              "m_sel": sel,
              "m_nfw": pk8(normf_w), "m_nfb": pk8(normf_b)}
        for l in range(4):
            im[f"m{l}_wpA"] = np.ascontiguousarray(in_proj_w[l].T[:, d0:d0 + 1024])
            im[f"m{l}_wpB"] = np.ascontiguousarray(
                in_proj_w[l].T[:, 2048 + d0:2048 + d0 + 1024])
            im[f"m{l}_wx"] = np.ascontiguousarray(x_proj_w[l].T[d0:d0 + 1024, :])
            im[f"m{l}_wdt"] = np.ascontiguousarray(dt_proj_w[l].T[:, d0:d0 + 1024])
            im[f"m{l}_wot"] = np.ascontiguousarray(out_proj_w[l].T[d0:d0 + 1024, :])
            im[f"m{l}_nw"] = pk8(norm_w[l])
            im[f"m{l}_nb"] = pk8(norm_b[l])
            wcpk = np.zeros((128, 32), np.float32)
            alogk = np.zeros((128, 128), np.float32)
            for dt_ in range(8):
                wcpk[:, 4 * dt_:4 * dt_ + 4] = conv_w[l][d0 + 128 * dt_:
                                                         d0 + 128 * (dt_ + 1), :]
                alogk[:, 16 * dt_:16 * (dt_ + 1)] = A_log[l][d0 + 128 * dt_:
                                                             d0 + 128 * (dt_ + 1), :]
            im[f"m{l}_wcp"] = wcpk
            im[f"m{l}_alog"] = alogk
            im[f"m{l}_wcb"] = pk8(conv_b[l][d0:d0 + 1024])
            im[f"m{l}_dtb"] = pk8(dt_proj_b[l][d0:d0 + 1024])
            im[f"m{l}_dsk"] = pk8(D_skip[l][d0:d0 + 1024])
        ims.append(im)
    res = run_bass_kernel_spmd(nc, ims, core_ids=list(range(NCORES)))
    X2 = np.empty_like(X_T)
    for b in range(B):
        X2[b] = np.asarray(res.results[2 * b]["xoT"])
    return X2



def emit_stream_add(em, name, xin_key, xin_ap, add_key, add_ap, xo_key, xo_ap,
                    scale=1.0):
    a_ = em.sbuf(f"{name}_a", (128, 512), BF16)
    af = em.sbuf(f"{name}_af", (128, 512), F32)
    d_ = em.sbuf(f"{name}_d", (128, 512), F32)
    for c in range(8):
        c0 = 512 * c
        for i in range(8):
            em.dma("sync", a_[:, :], add_ap[128 * i:128 * (i + 1), c0:c0 + 512],
                   reads=[add_key], writes=[a_], tag=f"{name}_lda")
            em.dma("sync", d_[:, :], xin_ap[128 * i:128 * (i + 1), c0:c0 + 512],
                   reads=[xin_key], writes=[d_], tag=f"{name}_ldd")
            em.op("scalar", lambda s: s.activation(out=af[:, :], in_=a_[:, :],
                                                   func=AF.Copy),
                  reads=[a_], writes=[af], tag=f"{name}_cv")
            em.op("vector", lambda v: v.scalar_tensor_tensor(
                out=d_[:, :], in0=af[:, :], scalar=float(scale), in1=d_[:, :],
                op0=ALU.mult, op1=ALU.add),
                reads=[af, d_], writes=[d_], tag=f"{name}_acc")
            em.dma("gpsimd", xo_ap[128 * i:128 * (i + 1), c0:c0 + 512],
                   d_[:, :], reads=[d_], writes=[xo_key], tag=f"{name}_st")
    em.free(a_, af, d_)


def emit_rail(em, cx, lns, A, xin_key, xin_ap, xo_key, xo_ap):
    xn = [em.sbuf("r_xn", (128, T), BF16) for _ in range(8)]
    xc = [em.sbuf("r_xc", (128, 512), F32) for _ in range(8)]
    for c in range(8):
        c0 = 512 * c
        for i in range(8):
            em.dma("sync", xc[i][:, :], xin_ap[128 * i:128 * (i + 1),
                                               c0:c0 + 512],
                   reads=[xin_key], writes=[xc[i]], tag="r_ldx")
        ln_feature_major(em, cx, xc, 512, xn, 6, 7, name=f"rln{c}",
                         scratch=lns, o_off=c0)
    em.free(*xc)
    bd1_s = em.sbuf("r_bd1s", (128, 32 * 128), BF16)
    bd2_s = em.sbuf("r_bd2s", (128, 128 * 128), BF16)
    stage_f = em.sbuf("r_stf", (128, 4096), F32)
    em.dma("sync", stage_f[:, :], A["bd1"], writes=[stage_f], tag="r_ldbd1")
    em.op("vector", lambda v: v.tensor_copy(out=bd1_s[:, :], in_=stage_f[:, :]),
          reads=[stage_f], writes=[bd1_s], tag="r_cbd1")
    for q4 in range(4):
        em.dma("sync", stage_f[:, :], A["bd2"][:, 4096 * q4:4096 * (q4 + 1)],
               writes=[stage_f], tag="r_ldbd2")
        em.op("vector", lambda v, q4=q4: v.tensor_copy(
            out=bd2_s[:, 4096 * q4:4096 * (q4 + 1)], in_=stage_f[:, :]),
            reads=[stage_f], writes=[bd2_s], tag="r_cbd2")
    wcbh = A["wcb_h"]
    wtmp = em.sbuf("r_wtmp", (128, 512), BF16)
    for kt in range(128):
        em.dma("sync", stage_f[:, 0:512], A["wcTh"][128 * kt:128 * (kt + 1), :],
               writes=[stage_f], tag="r_ldw")
        em.op("scalar", lambda s: s.activation(out=wtmp[:, :],
                                               in_=stage_f[:, 0:512],
                                               func=AF.Copy),
              reads=[stage_f], writes=[wtmp], tag="r_cw")
        em.dma("gpsimd", wcbh.ap()[128 * kt:128 * (kt + 1), :], wtmp[:, :],
               reads=[wtmp], writes=["r_wcbh"], tag="r_spw")
    em.free(stage_f, wtmp)
    wg_s = [em.sbuf("r_wg", (128, 512), BF16) for _ in range(4)]
    ytile = [em.sbuf("r_yt", (128, 512), BF16) for _ in range(2)]
    ztile = [em.sbuf("r_zt", (128, 512), BF16) for _ in range(2)]
    ob = em.sbuf("r_ob", (128, 512), BF16)
    for c0 in range(0, T, 512):
        for kt in range(128):
            hg, m = kt // 4, kt % 4
            yt = ytile[(kt // 4) % 2]
            zt = ztile[kt % 2]
            wgt = wg_s[kt % 4]
            em.dma("sync", wgt[:, :], wcbh.ap()[128 * kt:128 * (kt + 1), :],
                   reads=["r_wcbh"], writes=[wgt], tag="r_ldwg")
            if m == 0:
                em.op("tensor", lambda t, hg=hg, c0=c0: t.matmul(
                    out=cx.pp[4][:, :], lhsT=bd1_s[:, hg * 128:(hg + 1) * 128],
                    rhs=xn[hg % 8][:, c0:c0 + 512], start=True, stop=True),
                    reads=[bd1_s, xn[hg % 8]], writes=[cx.pp[4]], tag="r_mmy")
                em.op("vector", lambda v, yt=yt: v.tensor_copy(
                    out=yt[:, :], in_=cx.pp[4][:, :]),
                    reads=[cx.pp[4]], writes=[yt], tag="r_evy")
            em.op("tensor", lambda t, kt=kt, yt=yt: t.matmul(
                out=cx.pp[5][:, :], lhsT=bd2_s[:, kt * 128:(kt + 1) * 128],
                rhs=yt[:, :], start=True, stop=True),
                reads=[bd2_s, yt], writes=[cx.pp[5]], tag="r_mmz")
            em.op("scalar", lambda s, zt=zt: s.activation(
                out=zt[:, :], in_=cx.pp[5][:, :], func=AF.Gelu),
                reads=[cx.pp[5]], writes=[zt], tag="r_gelu")
            for o in range(4):
                em.op("tensor", lambda t, kt=kt, o=o, zt=zt, wgt=wgt: t.matmul(
                    out=cx.pp[o][:, :], lhsT=wgt[:, 128 * o:128 * (o + 1)],
                    rhs=zt[:, :], start=(kt == 0), stop=(kt == 127)),
                    reads=[wgt, zt], writes=[cx.pp[o]], tag="r_mmo")
        for o in range(4):
            em.op("scalar", lambda s, o=o: s.activation(
                out=ob[:, :], in_=cx.pp[o][:, :], func=AF.Copy),
                reads=[cx.pp[o]], writes=[ob], tag="r_evo")
            em.dma("gpsimd", A["agi"].ap()[128 * o:128 * (o + 1), c0:c0 + 512],
                   ob[:, :], reads=[ob], writes=["r_agi"], tag="r_spo")
    em.cc("AllGather", ALU.bypass, PAIRS, [A["agi"].ap().opt()],
          [A["ago"].ap().opt()], reads=["r_agi"], writes=["r_ago"],
          tag="r_ag")
    em.free(*(xn + wg_s + ytile + ztile + [ob, bd1_s, bd2_s]))
    emit_stream_add(em, "r_sa", xin_key, xin_ap, "r_ago", A["ago"].ap(),
                    xo_key, xo_ap, scale=0.5)


def build_rail2():
    nc = bass.Bass("TRN2", target_bir_lowering=False)
    em = Em(nc)
    xin = nc.dram_tensor("xin", [D, T], F32, kind="ExternalInput").ap()
    ident_in = nc.dram_tensor("ident", [128, 128], BF16, kind="ExternalInput").ap()
    A = {"bd1": nc.dram_tensor("bd1", [128, 32 * 128], F32,
                               kind="ExternalInput").ap(),
         "bd2": nc.dram_tensor("bd2", [128, 128 * 128], F32,
                               kind="ExternalInput").ap(),
         "wcTh": nc.dram_tensor("wcTh", [16384, 512], F32,
                                kind="ExternalInput").ap(),
         "wcb_h": nc.dram_tensor("wcb_h", [16384, 512], BF16),
         "agi": nc.dram_tensor("r_agi", [512, T], BF16),
         "ago": nc.dram_tensor("r_ago", [1024, T], BF16)}
    xo = nc.dram_tensor("xoT", [D, T], F32, kind="ExternalOutput").ap()
    cx = Ctx(em, ident_in)
    lns = ln_scratch(em)
    emit_rail(em, cx, lns, A, ("xin",), xin, ("xo",), xo)
    em.finalize()
    em.close()
    return nc


def _rail_stage2(X_T, rail_maker, rail_up_proj, w_cproj):
    bd1p, bd2p, wcT = _pack_rail(rail_maker, rail_up_proj, w_cproj)
    nc = _get_prog("rail2", build_rail2)
    ims = []
    for c in range(NCORES):
        b, p = c // 2, c % 2
        ims.append({"xin": np.ascontiguousarray(X_T[b]), "ident": _IDENT,
                    "bd1": bd1p, "bd2": bd2p,
                    "wcTh": np.ascontiguousarray(wcT[:, 512 * p:512 * (p + 1)])})
    res = run_bass_kernel_spmd(nc, ims, core_ids=list(range(NCORES)))
    X2 = np.empty_like(X_T)
    for b in range(B):
        X2[b] = np.asarray(res.results[2 * b]["xoT"])
    return X2


def emit_alibi(em, cx, lns, A, xin_key, xin_ap, xo_key, xo_ap):
    ptb = A["ptb"]
    qkvT_s = [em.sbuf("a_wq", (128, 1536), BF16) for _ in range(8)]
    outwT_s = [em.sbuf("a_wo", (128, 512), BF16) for _ in range(8)]
    qb_s = em.sbuf("a_qb", (128, 12), F32)
    qbq_s = em.sbuf("a_qbq", (128, 4), F32)
    ob_s = em.sbuf("a_ob", (128, 4), F32)
    stage = em.sbuf("a_stage", (128, 1536), F32)
    for k in range(8):
        em.dma("sync", stage[:, :], A["qkvT"][128 * k:128 * (k + 1), :],
               writes=[stage], tag="a_ldwq")
        em.op("vector", lambda v, k=k: v.tensor_copy(out=qkvT_s[k][:, :],
                                                     in_=stage[:, :]),
              reads=[stage], writes=[qkvT_s[k]], tag="a_cwq")
        em.dma("sync", stage[:, 0:512], A["outwT"][128 * k:128 * (k + 1), :],
               writes=[stage], tag="a_ldwo")
        em.op("vector", lambda v, k=k: v.tensor_copy(out=outwT_s[k][:, :],
                                                     in_=stage[:, 0:512]),
              reads=[stage], writes=[outwT_s[k]], tag="a_cwo")
    em.dma("sync", qb_s[:, :], A["qkvb"], writes=[qb_s], tag="a_ldqb")
    em.dma("sync", ob_s[:, :], A["outb"], writes=[ob_s], tag="a_ldob")
    em.op("vector", lambda v: v.tensor_scalar(out=qbq_s[:, :], in0=qb_s[:, 0:4],
                                              scalar1=0.125, scalar2=0.0,
                                              op0=ALU.mult, op1=ALU.add),
          reads=[qb_s], writes=[qbq_s], tag="a_qbs")
    xc = [em.sbuf("a_xc", (128, 512), F32) for _ in range(8)]
    xn = [em.sbuf("a_xn", (128, 512), BF16) for _ in range(8)]
    qkvc = [em.sbuf("a_qkvc", (128, 512), BF16) for _ in range(2)]
    vstg = [em.sbuf("a_vstg", (128, 130), BF16) for _ in range(2)]
    for i in range(2):
        em.op("gpsimd", lambda g, i=i: g.memset(vstg[i][:, 64:65], 1.0),
              writes=[("a_vset", i)], tag="a_ones")
        em.op("gpsimd", lambda g, i=i: g.memset(vstg[i][:, 129:130], 1.0),
              writes=[("a_vset", i)], tag="a_ones")
    for c in range(8):
        c0 = 512 * c
        for i in range(8):
            em.dma("sync", xc[i][:, :], xin_ap[128 * i:128 * (i + 1),
                                               c0:c0 + 512],
                   reads=[xin_key], writes=[xc[i]], tag="a_ldx")
        ln_feature_major(em, cx, xc, 512, xn, 5, 6, name=f"aln{c}",
                         scratch=lns)
        for ot in range(12):
            kind = ot // 4
            ps = cx.pp[ot % 2]
            for k in range(8):
                em.op("tensor", lambda t, ot=ot, k=k, ps=ps: t.matmul(
                    out=ps[:, :], lhsT=qkvT_s[k][:, 128 * ot:128 * (ot + 1)],
                    rhs=xn[k][:, :], start=(k == 0), stop=(k == 7)),
                    reads=[qkvT_s[k], xn[k]], writes=[ps], tag="a_mmqkv")
            qv = qkvc[ot % 2]
            if kind == 0:
                em.op("vector", lambda v, ot=ot, ps=ps, qv=qv: v.tensor_scalar(
                    out=qv[:, :], in0=ps[:, :], scalar1=0.125,
                    scalar2=qbq_s[:, ot:ot + 1], op0=ALU.mult, op1=ALU.add),
                    reads=[ps, qbq_s], writes=[qv], tag="a_evq")
                em.dma("gpsimd", A["qhb"].ap()[2 * ot, :, c0:c0 + 512],
                       qv[0:64, :], reads=[qv], writes=["a_qhb"], tag="a_spq")
                em.dma("gpsimd", A["qhb"].ap()[2 * ot + 1, :, c0:c0 + 512],
                       qv[64:128, :], reads=[qv], writes=["a_qhb"], tag="a_spq")
            elif kind == 1:
                em.op("vector", lambda v, ot=ot, ps=ps, qv=qv: v.tensor_scalar(
                    out=qv[:, :], in0=ps[:, :], scalar1=qb_s[:, ot:ot + 1],
                    scalar2=0.0, op0=ALU.add, op1=ALU.add),
                    reads=[ps, qb_s], writes=[qv], tag="a_evk")
                hh = 2 * (ot - 4)
                em.dma("gpsimd", A["khb"].ap()[hh, :, c0:c0 + 512], qv[0:64, :],
                       reads=[qv], writes=["a_khb"], tag="a_spk")
                em.dma("gpsimd", A["khb"].ap()[hh + 1, :, c0:c0 + 512],
                       qv[64:128, :], reads=[qv], writes=["a_khb"], tag="a_spk")
            else:
                em.op("vector", lambda v, ot=ot, ps=ps, qv=qv: v.tensor_scalar(
                    out=qv[:, :], in0=ps[:, :], scalar1=qb_s[:, ot:ot + 1],
                    scalar2=0.0, op0=ALU.add, op1=ALU.add),
                    reads=[ps, qb_s], writes=[qv], tag="a_evv")
                hp = ot - 8
                for s4 in range(4):
                    jt = 4 * c + s4
                    pth = 512 * (s4 % 2)
                    em.op("tensor", lambda t, qv=qv, s4=s4, pth=pth: t.transpose(
                        out=ptb[:, pth:pth + 128],
                        in_=qv[:, 128 * s4:128 * (s4 + 1)],
                        identity=cx.ident[:, :]),
                        reads=[qv, cx.ident], writes=[ptb], tag="a_trv")
                    vs = vstg[s4 % 2]
                    em.op("vector", lambda v, pth=pth, vs=vs: v.tensor_copy(
                        out=vs[:, 0:64], in_=ptb[:, pth:pth + 64]),
                        reads=[ptb], writes=[vs], tag="a_evv1")
                    em.op("vector", lambda v, pth=pth, vs=vs: v.tensor_copy(
                        out=vs[:, 65:129], in_=ptb[:, pth + 64:pth + 128]),
                        reads=[ptb], writes=[vs], tag="a_evv2")
                    em.dma("gpsimd", A["vhb"].ap()[jt, :, hp, :], vs[:, :],
                           reads=[vs, ("a_vset", 0), ("a_vset", 1)],
                           writes=["a_vhb"], tag="a_spv")
    em.free(*(xc + xn + qkvc + vstg))
    qhat = [em.sbuf("a_qhat", (69, 512), BF16) for _ in range(4)]
    klo = [em.sbuf("a_klo", (69, 512), BF16) for _ in range(2)]
    khi = [em.sbuf("a_khi", (69, 512), BF16) for _ in range(2)]
    vt4 = [em.sbuf("a_vt4", (128, 4 * 130), BF16) for _ in range(2)]
    att = [em.sbuf("a_att", (128, 512), BF16) for _ in range(4)]
    fc_s = [em.sbuf("a_fc", (128, 4 * 512), F32) for _ in range(2)]
    aoT = [em.sbuf("a_aoT", (128, T), BF16) for _ in range(4)]
    rrow = em.sbuf("a_rrow", (1, 512), F32)
    rb = em.sbuf("a_rb", (64, 512), F32)
    nb8 = em.sbuf("a_nb8", (128, 1), F32)
    em.op("gpsimd", lambda g: g.memset(nb8[:, :], -8.0), writes=[nb8],
          tag="a_nb8")
    PSO = [0, 1, 4, 5]
    for h in range(8):
        hp, hs = h // 2, h % 2
        fcs = fc_s[h % 2]
        em.dma("sync", fcs[:, :], A["fcor"][h], writes=[fcs], tag="a_ldfc")
        for qg in range(2):
            for qi in range(4):
                qc = 4 * qg + qi
                qh = qhat[qi]
                em.dma("sync", qh[0:64, :],
                       A["qhb"].ap()[h, :, 512 * qc:512 * (qc + 1)],
                       reads=["a_qhb"], writes=[qh], tag="a_ldqh")
                em.dma("sync", qh[64:69, :],
                       A["qx"][:, 512 * qc:512 * (qc + 1)],
                       writes=[qh], tag="a_ldqx")
            for jb in range(8):
                ka = klo[jb % 2]
                kb_ = khi[jb % 2]
                vt = vt4[jb % 2]
                jB = 512 * jb
                em.dma("sync", ka[0:64, :], A["khb"].ap()[h, :, jB:jB + 512],
                       reads=["a_khb"], writes=[ka], tag="a_ldkl")
                em.dma("sync", ka[64:69, :], A["kxlo"][h, :, jB:jB + 512],
                       writes=[ka], tag="a_ldkxl")
                em.dma("sync", kb_[0:64, :], A["khb"].ap()[h, :, jB:jB + 512],
                       reads=["a_khb"], writes=[kb_], tag="a_ldkh")
                em.dma("sync", kb_[64:69, :], A["kxhi"][h, :, jB:jB + 512],
                       writes=[kb_], tag="a_ldkxh")
                em.dma("sync", bass.AP(vt, 0, [[520, 128], [130, 4], [1, 130]]),
                       A["vhb"].ap()[4 * jb:4 * jb + 4, :, hp, :]
                       .rearrange("j p v -> p j v"),
                       reads=["a_vhb"], writes=[vt], tag="a_ldvt")
                for s4 in range(4):
                    jt = 4 * jb + s4
                    j0 = 128 * jt
                    for qi in range(4):
                        qc = 4 * qg + qi
                        q0 = 512 * qc
                        qh = qhat[qi]
                        crossing = (j0 + 128 > q0) and (j0 < q0 + 512)
                        hi = (j0 >= q0 + 512)
                        src = kb_ if hi else ka
                        ps = cx.pp[2 + (jt + qi) % 2]
                        em.op("tensor", lambda t, src=src, s4=s4, qh=qh, ps=ps:
                              t.matmul(out=ps[:, :],
                                       lhsT=src[:, 128 * s4:128 * (s4 + 1)],
                                       rhs=qh[:, :], start=True, stop=True),
                              reads=[src, qh], writes=[ps], tag="a_mms")
                        at = att[(4 * s4 + qi) % 4]
                        if crossing:
                            ph = (j0 - q0) // 128
                            em.op("vector", lambda v, ps=ps, ph=ph, fcs=fcs:
                                  v.tensor_tensor(
                                      out=ps[:, :], in0=ps[:, :],
                                      in1=fcs[:, 512 * ph:512 * (ph + 1)],
                                      op=ALU.add),
                                  reads=[ps, fcs], writes=[ps], tag="a_fcor")
                        em.op("scalar", lambda s, ps=ps, at=at: s.activation(
                            out=at[:, :], in_=ps[:, :], func=AF.Exp,
                            bias=nb8[:, 0:1]),
                            reads=[ps, nb8], writes=[at], tag="a_exp")
                        pso = cx.pp[PSO[qi]]
                        em.op("tensor", lambda t, vt=vt, s4=s4, hs=hs, at=at,
                              pso=pso, jt=jt: t.matmul(
                                  out=pso[0:65, :],
                                  lhsT=vt[:, 130 * s4 + 65 * hs:
                                          130 * s4 + 65 * hs + 65],
                                  rhs=at[:, :], start=(jt == 0),
                                  stop=(jt == 31)),
                              reads=[vt, at], writes=[pso], tag="a_mmav")
            for qi in range(4):
                qc = 4 * qg + qi
                pso = cx.pp[PSO[qi]]
                em.op("vector", lambda v, pso=pso: v.tensor_copy(
                    out=rrow[:, :], in_=pso[64:65, :]),
                    reads=[pso], writes=[rrow], tag="a_sumr")
                em.op("vector", lambda v: v.reciprocal(out=rrow[:, :],
                                                       in_=rrow[:, :]),
                      reads=[rrow], writes=[rrow], tag="a_recip")
                pbc = cx.pp[6]
                em.op("tensor", lambda t, pbc=pbc: t.matmul(
                    out=pbc[0:64, :], lhsT=cx.ones_col[:, 0:64],
                    rhs=rrow[:, :], start=True, stop=True),
                    reads=[rrow, cx.ones_col], writes=[pbc], tag="a_bcr")
                em.op("vector", lambda v, pbc=pbc: v.tensor_copy(
                    out=rb[:, :], in_=pbc[0:64, :]),
                    reads=[pbc], writes=[rb], tag="a_evrb")
                ao = aoT[h // 2]
                ro = 64 * (h % 2)
                em.op("vector", lambda v, pso=pso, ao=ao, ro=ro, qc=qc:
                      v.tensor_tensor(
                          out=ao[ro:ro + 64, 512 * qc:512 * (qc + 1)],
                          in0=pso[0:64, :], in1=rb[:, :], op=ALU.mult),
                      reads=[pso, rb], writes=[ao], tag="a_evao")
    for k in range(4):
        for c in range(8):
            em.dma("gpsimd", A["agi"].ap()[128 * k:128 * (k + 1),
                                           512 * c:512 * (c + 1)],
                   aoT[k][:, 512 * c:512 * (c + 1)], reads=[aoT[k]],
                   writes=["a_agi"], tag="a_spao")
    em.cc("AllGather", ALU.bypass, PAIRS, [A["agi"].ap().opt()],
          [A["ago"].ap().opt()], reads=["a_agi"], writes=["a_ago"], tag="a_ag")
    em.free(*(qhat + klo + khi + vt4 + att + fc_s + aoT + [rrow, rb, nb8]))
    agf = [em.sbuf("a_agf", (128, 512), BF16) for _ in range(8)]
    oo = em.sbuf("a_oo", (128, 512), BF16)
    for c in range(8):
        c0 = 512 * c
        for k in range(8):
            em.dma("sync", agf[k][:, :],
                   A["ago"].ap()[128 * k:128 * (k + 1), c0:c0 + 512],
                   reads=["a_ago"], writes=[agf[k]], tag="a_ldag")
        for ot in range(4):
            ps = cx.pp[ot % 2]
            for k in range(8):
                em.op("tensor", lambda t, ot=ot, k=k, ps=ps: t.matmul(
                    out=ps[:, :], lhsT=outwT_s[k][:, 128 * ot:128 * (ot + 1)],
                    rhs=agf[k][:, :], start=(k == 0), stop=(k == 7)),
                    reads=[outwT_s[k], agf[k]], writes=[ps], tag="a_mmout")
            em.op("vector", lambda v, ps=ps, ot=ot: v.tensor_scalar(
                out=oo[:, :], in0=ps[:, :], scalar1=ob_s[:, ot:ot + 1],
                scalar2=0.0, op0=ALU.add, op1=ALU.add),
                reads=[ps, ob_s], writes=[oo], tag="a_evout")
            em.dma("gpsimd", A["agi2"].ap()[128 * ot:128 * (ot + 1),
                                            c0:c0 + 512], oo[:, :],
                   reads=[oo], writes=["a_agi2"], tag="a_spout")
    em.cc("AllGather", ALU.bypass, PAIRS, [A["agi2"].ap().opt()],
          [A["ago2"].ap().opt()], reads=["a_agi2"], writes=["a_ago2"],
          tag="a_ag2")
    em.free(*(agf + [oo, stage] + qkvT_s + outwT_s + [qb_s, qbq_s, ob_s]))
    emit_stream_add(em, "a_sa", xin_key, xin_ap, "a_ago2", A["ago2"].ap(),
                    xo_key, xo_ap, scale=1.0)


def build_alibi2():
    nc = bass.Bass("TRN2", target_bir_lowering=False)
    em = Em(nc)
    gi = lambda n, s, d=F32: nc.dram_tensor(n, s, d, kind="ExternalInput").ap()
    xin = gi("xin", [D, T])
    ident_in = gi("ident", [128, 128], BF16)
    A = {"qkvT": gi("qkvT", [D, 1536]), "qkvb": gi("qkvb", [128, 12]),
         "outwT": gi("outwT", [D, 512]), "outb": gi("outb", [128, 4]),
         "kxlo": gi("kxlo", [8, 5, T], BF16), "kxhi": gi("kxhi", [8, 5, T], BF16),
         "qx": gi("qx", [5, T], BF16), "fcor": gi("fcor", [8, 128, 2048]),
         "qhb": nc.dram_tensor("qhb", [8, 64, T], BF16),
         "khb": nc.dram_tensor("khb", [8, 64, T], BF16),
         "vhb": nc.dram_tensor("vhb", [32, 128, 4, 130], BF16),
         "agi": nc.dram_tensor("a_agi", [512, T], BF16),
         "ago": nc.dram_tensor("a_ago", [1024, T], BF16),
         "agi2": nc.dram_tensor("a_agi2", [512, T], BF16),
         "ago2": nc.dram_tensor("a_ago2", [1024, T], BF16)}
    xo = nc.dram_tensor("xoT", [D, T], F32, kind="ExternalOutput").ap()
    cx = Ctx(em, ident_in, nbanks=7)
    A["ptb"] = em.psum("ptb", (128, 1024), BF16)
    lns = ln_scratch(em)
    emit_alibi(em, cx, lns, A, ("xin",), xin, ("xo",), xo)
    em.finalize()
    em.close()
    return nc


def _pack_alibi2(qkv_w, qkv_b, out_w, out_b, p):
    h0 = 8 * p
    perm = np.zeros(1536, np.int64)
    for hh in range(8):
        hsrc = h0 + hh
        for j in range(64):
            perm[hh * 64 + j] = hsrc * 192 + j
            perm[512 + hh * 64 + j] = hsrc * 192 + 64 + j
            perm[1024 + hh * 64 + j] = hsrc * 192 + 128 + j
    qkvT = np.ascontiguousarray(qkv_w[perm].T)
    qb = np.ascontiguousarray(qkv_b[perm].reshape(12, 128).T)
    outwT = np.ascontiguousarray(out_w.T[:, 512 * p:512 * (p + 1)])
    ob = np.ascontiguousarray(out_b[512 * p:512 * (p + 1)].reshape(4, 128).T)
    m = 0.5 ** (np.arange(8) + h0).astype(np.float64)
    j = np.arange(T, dtype=np.float64)
    kxlo = np.zeros((8, 5, T), np.float32)
    for hh in range(8):
        kxlo[hh, 0] = m[hh] * (j % 128)
        kxlo[hh, 1] = m[hh] * 128 * (j // 128)
        kxlo[hh, 2] = m[hh]
        kxlo[hh, 3] = 128 * m[hh]
        kxlo[hh, 4] = m[hh]
    kxhi = -kxlo
    q = np.arange(T)
    qx = np.zeros((5, T), np.float32)
    qx[0] = 1.0
    qx[1] = 1.0
    qx[2] = -(q % 128)
    qx[3] = -((q % 512) // 128)
    qx[4] = -(512 * (q // 512))
    fc = np.zeros((8, 4, 128, 512), np.float32)
    jj = np.arange(128)[:, None]
    qq = np.arange(512)[None, :]
    for hh in range(8):
        for ph in range(4):
            dd = (128 * ph + jj) - qq
            fc[hh, ph] = np.where(dd > 0, -2 * m[hh] * dd, 0.0)
    fc2 = np.ascontiguousarray(fc.transpose(0, 2, 1, 3).reshape(8, 128, 2048))
    return dict(qkvT=qkvT, qkvb=qb, outwT=outwT, outb=ob,
                kxlo=kxlo.astype(BF), kxhi=kxhi.astype(BF), qx=qx.astype(BF),
                fcor=fc2)


def _alibi_stage2(X_T, qkv_w, qkv_b, out_w, out_b):
    nc = _get_prog("alibi2", build_alibi2)
    ims = []
    for c in range(NCORES):
        b, p = c // 2, c % 2
        pk = _pack_alibi2(qkv_w, qkv_b, out_w, out_b, p)
        pk["xin"] = np.ascontiguousarray(X_T[b])
        pk["ident"] = _IDENT
        ims.append(pk)
    res = run_bass_kernel_spmd(nc, ims, core_ids=list(range(NCORES)))
    X2 = np.empty_like(X_T)
    for b in range(B):
        X2[b] = np.asarray(res.results[2 * b]["xoT"])
    return X2


def emit_ffn(em, cx, lns, A, xin_key, xin_ap, xo_key, xo_ap):
    ptb = A["ptb"]
    ZS, UB = A["ZS"], A["UB"]
    rmat_s = em.sbuf("f_rmat", (128, 128), BF16)
    em.dma("sync", rmat_s[:, :], A["rmat"], writes=[rmat_s], tag="f_ldrm")
    amask_s = em.sbuf("f_amask", (128, 8 * 512), BF16)
    em.dma("sync", amask_s[:, :], A["amask"], writes=[amask_s], tag="f_ldam")
    epscol = em.sbuf("f_eps", (1, 1), F32)
    em.op("gpsimd", lambda g: g.memset(epscol[:, :], 1e-24), writes=[epscol],
          tag="f_eps")
    xc = [em.sbuf("f_xc", (128, 512), F32) for _ in range(8)]
    lo = [em.sbuf("f_lo", (128, 512), F32) for _ in range(8)]
    lb_ = em.sbuf("f_lb", (128, 512), BF16)
    for c in range(8):
        c0 = 512 * c
        for i in range(8):
            em.dma("sync", xc[i][:, :], xin_ap[128 * i:128 * (i + 1),
                                               c0:c0 + 512],
                   reads=[xin_key], writes=[xc[i]], tag="f_ldx")
        ln_feature_major(em, cx, xc, 512, lo, 5, 6, name=f"fln{c}",
                         scratch=lns)
        for i in range(8):
            em.dma("gpsimd", ZS.ap()[128 * i:128 * (i + 1), c0:c0 + 512],
                   lo[i][:, :], reads=[lo[i]], writes=["f_ZS"], tag="f_spz")
            em.op("scalar", lambda s, i=i: s.activation(
                out=lb_[:, :], in_=lo[i][:, :], func=AF.Copy),
                reads=[lo[i]], writes=[lb_], tag="f_cb")
            em.dma("gpsimd", UB.ap()[128 * i:128 * (i + 1), c0:c0 + 512],
                   lb_[:, :], reads=[lb_], writes=["f_UB"], tag="f_spu")
    em.free(*(xc + lo + [lb_]))

    for l in range(2):
        wqkv_s = [em.sbuf("f_wq", (128, 768), BF16) for _ in range(8)]
        wo_s = [em.sbuf("f_wo", (128, 1024), BF16) for _ in range(4)]
        stage = em.sbuf("f_stage", (128, 1024), F32)
        for k in range(8):
            em.dma("sync", stage[:, 0:768], A["wqkv"][l][128 * k:128 * (k + 1), :],
                   writes=[stage], tag="f_ldwq")
            em.op("vector", lambda v, k=k: v.tensor_copy(
                out=wqkv_s[k][:, :], in_=stage[:, 0:768]),
                reads=[stage], writes=[wqkv_s[k]], tag="f_cwq")
        for k in range(4):
            em.dma("sync", stage[:, :], A["wo"][l][128 * k:128 * (k + 1), :],
                   writes=[stage], tag="f_ldwo")
            em.op("vector", lambda v, k=k: v.tensor_copy(
                out=wo_s[k][:, :], in_=stage[:, :]),
                reads=[stage], writes=[wo_s[k]], tag="f_cwo")
        qsr_s = em.sbuf("f_qsr", (1, 256), F32)
        ksr_s = em.sbuf("f_ksr", (1, 256), F32)
        em.dma("sync", qsr_s[:, :], A["qsr"][l], writes=[qsr_s], tag="f_ldqs")
        em.dma("sync", ksr_s[:, :], A["ksr"][l], writes=[ksr_s], tag="f_ldks")
        kring = [em.sbuf("f_kring", (128, 3 * 512), BF16) for _ in range(2)]
        vring = [em.sbuf("f_vring", (128, 3 * 260), BF16) for _ in range(4)]
        for s in range(4):
            for sl in range(3):
                for hh2 in range(4):
                    em.op("gpsimd", lambda g, s=s, sl=sl, hh2=hh2: g.memset(
                        vring[s][:, 260 * sl + 65 * hh2 + 64:
                                 260 * sl + 65 * hh2 + 65], 1.0),
                        writes=[("f_vones",)], tag="f_ones")
        zc = [em.sbuf("f_zc", (128, 513), BF16) for i in range(8)]
        qkvc = em.sbuf("f_qkvc", (128, 512), BF16)
        sqb = em.sbuf("f_sqb", (128, 512), BF16)
        kb = em.sbuf("f_kb", (128, 512), BF16)
        krot = em.sbuf("f_krot", (128, 512), BF16)
        t1b = em.sbuf("f_t1b", (128, 512), BF16)
        rows2 = em.sbuf("f_rows2", (1, 1024), F32)
        tabs = [em.sbuf("f_tab", (128, 512), BF16) for _ in range(4)]
        qhat = [em.sbuf("f_qhat", (128, 512), BF16) for _ in range(2)]
        att = [em.sbuf("f_att", (128, 512), BF16) for _ in range(2)]
        aom = [em.sbuf("f_aom", (128, T), BF16) for _ in range(2)]
        rrow = em.sbuf("f_rrow", (1, 512), F32)
        rb = em.sbuf("f_rb", (64, 512), F32)
        for gb in range(8):
            bs = 512 * gb
            slot = gb % 3
            src_ub = UB if l == 0 else A["UB2"]
            src_key = "f_UB" if l == 0 else "f_UB2"
            for i in range(8):
                if bs == 0:
                    em.op("gpsimd", lambda g, i=i: g.memset(zc[i][:, 0:1], 0.0),
                          writes=[zc[i]], tag="f_zpad")
                    em.dma("sync", zc[i][:, 1:513],
                           src_ub.ap()[128 * i:128 * (i + 1), bs:bs + 512],
                           reads=[src_key], writes=[zc[i]], tag="f_ldz")
                else:
                    em.dma("sync", zc[i][:, :],
                           src_ub.ap()[128 * i:128 * (i + 1), bs - 1:bs + 512],
                           reads=[src_key], writes=[zc[i]], tag="f_ldz")
            for ti in range(4):
                em.dma("sync", tabs[ti][:, :], A["rtab"][ti, :, bs:bs + 512],
                       writes=[("f_tabs",)], tag="f_ldtab")
            for ot in range(6):
                kind = ot // 2
                ps = cx.pp[ot % 2]
                for k in range(8):
                    em.op("tensor", lambda t, ot=ot, k=k, ps=ps: t.matmul(
                        out=ps[:, :], lhsT=wqkv_s[k][:, 128 * ot:128 * (ot + 1)],
                        rhs=(zc[k][:, 1:513] if k < 4 else zc[k][:, 0:512]),
                        start=(k == 0), stop=(k == 7)),
                        reads=[wqkv_s[k], zc[k]], writes=[ps], tag="f_mmqkv")
                if kind == 2:
                    em.op("scalar", lambda s, ps=ps: s.activation(
                        out=qkvc[:, :], in_=ps[:, :], func=AF.Copy),
                        reads=[ps], writes=[qkvc], tag="f_evv")
                    jj = ot - 4
                    for s4 in range(4):
                        pth = 512 * (s4 % 2)
                        em.op("tensor", lambda t, s4=s4, pth=pth: t.transpose(
                            out=ptb[:, pth:pth + 128],
                            in_=qkvc[:, 128 * s4:128 * (s4 + 1)],
                            identity=cx.ident[:, :]),
                            reads=[qkvc, cx.ident], writes=[ptb], tag="f_trv")
                        for hs in range(2):
                            hh2 = 2 * jj + hs
                            em.op("vector", lambda v, s4=s4, hs=hs, hh2=hh2,
                                  slot=slot, pth=pth: v.tensor_copy(
                                      out=vring[s4][:, 260 * slot + 65 * hh2:
                                                    260 * slot + 65 * hh2 + 64],
                                      in_=ptb[:, pth + 64 * hs:
                                              pth + 64 * hs + 64]),
                                  reads=[ptb], writes=[vring[s4]], tag="f_evvt")
                else:
                    srow = qsr_s if kind == 0 else ksr_s
                    scl = 8.0 if kind == 0 else 1.0
                    ct = tabs[0] if kind == 0 else tabs[2]
                    st = tabs[1] if kind == 0 else tabs[3]
                    if kind == 0:
                        oring, rc_ = qhat[ot], (0, 512)
                    else:
                        oring, rc_ = kring[ot - 2], (512 * slot, 512 * slot + 512)
                    _qk_process(em, cx, f"fqk{l}_{gb}_{ot}", ps, srow, scl,
                                ct[:, :], st[:, :], rmat_s, oring, rc_, gb,
                                rows2, epscol, sqb, kb, krot, t1b)
            for h in range(4):
                pso = cx.pp[4 + (h % 2)]
                jts = range(4, 8) if gb == 0 else range(8)
                first_jt = 4 if gb == 0 else 0
                for jtl in jts:
                    jgb = gb - 1 + jtl // 4
                    jslot = jgb % 3
                    s4 = jtl % 4
                    ps = cx.pp[2 + (jtl % 2)]
                    em.op("tensor", lambda t, h=h, jslot=jslot, s4=s4, ps=ps:
                          t.matmul(out=ps[:, :],
                                   lhsT=kring[h // 2]
                                   [64 * (h % 2):64 * (h % 2) + 64,
                                    512 * jslot + 128 * s4:
                                    512 * jslot + 128 * (s4 + 1)],
                                   rhs=qhat[h // 2][64 * (h % 2):
                                                    64 * (h % 2) + 64, :],
                                   start=True, stop=True),
                          reads=[kring[h // 2], qhat[h // 2]], writes=[ps],
                          tag="f_mms")
                    em.op("vector", lambda v, ps=ps, jtl=jtl: v.tensor_tensor(
                        out=ps[:, :], in0=ps[:, :],
                        in1=amask_s[:, 512 * jtl:512 * (jtl + 1)], op=ALU.add),
                        reads=[ps, amask_s], writes=[ps], tag="f_mask")
                    at = att[jtl % 2]
                    em.op("scalar", lambda s, ps=ps, at=at: s.activation(
                        out=at[:, :], in_=ps[:, :], func=AF.Exp),
                        reads=[ps], writes=[at], tag="f_exp")
                    em.op("tensor", lambda t, at=at, h=h, jslot=jslot, s4=s4,
                          pso=pso, jtl=jtl, first_jt=first_jt: t.matmul(
                              out=pso[0:65, :],
                              lhsT=vring[s4][:, 260 * jslot + 65 * h:
                                             260 * jslot + 65 * h + 65],
                              rhs=at[:, :], start=(jtl == first_jt),
                              stop=(jtl == 7)),
                          reads=[vring[s4], at, ("f_vones",)], writes=[pso],
                          tag="f_mmav")
                em.op("vector", lambda v, pso=pso: v.tensor_copy(
                    out=rrow[:, :], in_=pso[64:65, :]),
                    reads=[pso], writes=[rrow], tag="f_sumr")
                em.op("vector", lambda v: v.reciprocal(out=rrow[:, :],
                                                       in_=rrow[:, :]),
                      reads=[rrow], writes=[rrow], tag="f_recip")
                em.op("tensor", lambda t: t.matmul(
                    out=cx.pp[6][0:64, :], lhsT=cx.ones_col[:, 0:64],
                    rhs=rrow[:, :], start=True, stop=True),
                    reads=[rrow, cx.ones_col], writes=[cx.pp[6]], tag="f_bcr")
                em.op("vector", lambda v: v.tensor_copy(
                    out=rb[:, :], in_=cx.pp[6][0:64, :]),
                    reads=[cx.pp[6]], writes=[rb], tag="f_evrb")
                ao = aom[h // 2]
                ro = 64 * (h % 2)
                em.op("vector", lambda v, pso=pso, ao=ao, ro=ro, bs=bs:
                      v.tensor_tensor(out=ao[ro:ro + 64, bs:bs + 512],
                                      in0=pso[0:64, :], in1=rb[:, :],
                                      op=ALU.mult),
                      reads=[pso, rb], writes=[ao], tag="f_evao")
        for k in range(2):
            for c in range(8):
                em.dma("gpsimd", A["agi"].ap()[128 * k:128 * (k + 1),
                                               512 * c:512 * (c + 1)],
                       aom[k][:, 512 * c:512 * (c + 1)], reads=[aom[k]],
                       writes=["f_agi"], tag="f_spao")
        em.cc("AllGather", ALU.bypass, PAIRS, [A["agi"].ap().opt()],
              [A["ago"].ap().opt()], reads=["f_agi"], writes=["f_ago"],
              tag="f_ag")
        em.free(*([qkvc, sqb, kb, krot, t1b, rows2] + tabs + qhat + att + aom
                  + [rrow, rb] + zc + kring + vring))
        agf = [em.sbuf("f_agf", (128, 512), BF16) for _ in range(4)]
        uo = em.sbuf("f_uo", (128, 512), F32)
        ub2 = em.sbuf("f_ub2", (128, 512), BF16)
        dst_ub = UB if l == 0 else A["UB2"]
        dst_ukey = "f_UB" if l == 0 else "f_UB2"
        for c in range(8):
            c0 = 512 * c
            for k in range(4):
                em.dma("sync", agf[k][:, :],
                       A["ago"].ap()[128 * k:128 * (k + 1), c0:c0 + 512],
                       reads=["f_ago"], writes=[agf[k]], tag="f_ldag")
            for ot in range(8):
                ps = cx.pp[ot % 2]
                for k in range(4):
                    em.op("tensor", lambda t, ot=ot, k=k, ps=ps: t.matmul(
                        out=ps[:, :], lhsT=wo_s[k][:, 128 * ot:128 * (ot + 1)],
                        rhs=agf[k][:, :], start=(k == 0), stop=(k == 3)),
                        reads=[wo_s[k], agf[k]], writes=[ps], tag="f_mmwo")
                em.dma("sync", uo[:, :],
                       ZS.ap()[128 * ot:128 * (ot + 1), c0:c0 + 512],
                       reads=["f_ZS"], writes=[uo], tag="f_lduz")
                em.op("vector", lambda v, ps=ps: v.tensor_tensor(
                    out=uo[:, :], in0=uo[:, :], in1=ps[:, :], op=ALU.add),
                    reads=[uo, ps], writes=[uo], tag="f_ures")
                em.dma("gpsimd", ZS.ap()[128 * ot:128 * (ot + 1), c0:c0 + 512],
                       uo[:, :], reads=[uo], writes=["f_ZS"], tag="f_spuz")
                em.op("scalar", lambda s: s.activation(
                    out=ub2[:, :], in_=uo[:, :], func=AF.Copy),
                    reads=[uo], writes=[ub2], tag="f_cub")
                em.dma("gpsimd", dst_ub.ap()[128 * ot:128 * (ot + 1),
                                             c0:c0 + 512],
                       ub2[:, :], reads=[ub2], writes=[dst_ukey], tag="f_spub")
        em.free(*(agf + [uo, ub2]))
        w1_s = [em.sbuf("f_w1s", (128, 2816), BF16) for _ in range(8)]
        w2_s = [em.sbuf("f_w2s", (128, 1024), BF16) for _ in range(2)]
        b1_s = em.sbuf("f_b1s", (128, 22), F32)
        lng_s = em.sbuf("f_lngs", (128, 8), F32)
        lnb_s = em.sbuf("f_lnbs", (128, 8), F32)
        stage2 = em.sbuf("f_stage2", (128, 2816), F32)
        for k in range(8):
            em.dma("sync", stage2[:, :], A["w1"][l][128 * k:128 * (k + 1), :],
                   writes=[stage2], tag="f_ldw1")
            em.op("vector", lambda v, k=k: v.tensor_copy(
                out=w1_s[k][:, :], in_=stage2[:, :]),
                reads=[stage2], writes=[w1_s[k]], tag="f_cw1")
        em.dma("sync", b1_s[:, :], A["b1"][l], writes=[b1_s], tag="f_ldb1")
        em.dma("sync", lng_s[:, :], A["lng"][l], writes=[lng_s], tag="f_ldlg")
        em.dma("sync", lnb_s[:, :], A["lnb"][l], writes=[lnb_s], tag="f_ldlb")
        uc = [em.sbuf("f_uc", (128, 513), F32) for _ in range(8)]
        hb = [em.sbuf("f_hb", (128, 512), BF16) for _ in range(8)]
        ag_ = [em.sbuf("f_ag", (128, 512), BF16) for _ in range(11)]
        gg = em.sbuf("f_gg", (128, 512), BF16)
        pp_o = em.sbuf("f_ppo", (128, 512), BF16)
        gbc = [(lng_s[:, i:i + 1], lnb_s[:, i:i + 1]) for i in range(8)]
        for c in range(8):
            bs = 512 * c
            for i in range(8):
                if bs == 0:
                    em.op("gpsimd", lambda g, i=i: g.memset(uc[i][:, 0:1], 0.0),
                          writes=[uc[i]], tag="f_upad")
                    em.dma("sync", uc[i][:, 1:513],
                           ZS.ap()[128 * i:128 * (i + 1), bs:bs + 512],
                           reads=["f_ZS"], writes=[uc[i]], tag="f_ldu")
                else:
                    em.dma("sync", uc[i][:, :],
                           ZS.ap()[128 * i:128 * (i + 1), bs - 1:bs + 512],
                           reads=["f_ZS"], writes=[uc[i]], tag="f_ldu")
            ln_feature_major(em, cx, uc, 512, hb, 5, 6, gain_bias=gbc,
                             name=f"f2ln{l}_{c}", scratch=lns,
                             x_offs=[1 if i < 4 else 0 for i in range(8)])
            for ht in range(22):
                ps = cx.pp[ht % 2]
                for k in range(8):
                    em.op("tensor", lambda t, ht=ht, k=k, ps=ps: t.matmul(
                        out=ps[:, :], lhsT=w1_s[k][:, 128 * ht:128 * (ht + 1)],
                        rhs=hb[k][:, :], start=(k == 0), stop=(k == 7)),
                        reads=[w1_s[k], hb[k]], writes=[ps], tag="f_mmw1")
                if ht < 11:
                    em.op("vector", lambda v, ps=ps, ht=ht: v.tensor_scalar(
                        out=ag_[ht][:, :], in0=ps[:, :],
                        scalar1=b1_s[:, ht:ht + 1], scalar2=0.0,
                        op0=ALU.add, op1=ALU.add),
                        reads=[ps, b1_s], writes=[ag_[ht]], tag="f_eva")
                else:
                    em.op("scalar", lambda s, ps=ps, ht=ht: s.activation(
                        out=gg[:, :], in_=ps[:, :], func=AF.Gelu,
                        bias=b1_s[:, ht:ht + 1]),
                        reads=[ps, b1_s], writes=[gg], tag="f_gelu")
                    em.op("vector", lambda v, ht=ht: v.tensor_tensor(
                        out=ag_[ht - 11][:, :], in0=ag_[ht - 11][:, :],
                        in1=gg[:, :], op=ALU.mult),
                        reads=[ag_[ht - 11], gg], writes=[ag_[ht - 11]],
                        tag="f_hgm")
            for oh in range(2):
                for kk in range(11):
                    wf = stage2[:, 0:1024]
                    em.dma("sync", wf, A["w2"][l][128 * kk:128 * (kk + 1), :],
                           writes=[stage2], tag="f_ldw2")
                    em.op("scalar", lambda s, kk=kk: s.activation(
                        out=w2_s[kk % 2][:, :], in_=stage2[:, 0:1024],
                        func=AF.Copy),
                        reads=[stage2], writes=[w2_s[kk % 2]], tag="f_cw2")
                    for oi in range(4):
                        o = 4 * oh + oi
                        em.op("tensor", lambda t, kk=kk, o=o, oi=oi: t.matmul(
                            out=cx.pp[2 + oi][:, :],
                            lhsT=w2_s[kk % 2][:, 128 * o:128 * (o + 1)],
                            rhs=ag_[kk][:, :], start=(kk == 0), stop=(kk == 10)),
                            reads=[w2_s[kk % 2], ag_[kk]],
                            writes=[cx.pp[2 + oi]], tag="f_mmw2")
                for oi in range(4):
                    o = 4 * oh + oi
                    em.op("scalar", lambda s, oi=oi: s.activation(
                        out=pp_o[:, :], in_=cx.pp[2 + oi][:, :], func=AF.Copy),
                        reads=[cx.pp[2 + oi]], writes=[pp_o], tag="f_evw2")
                    em.dma("gpsimd", A["ari"].ap()[128 * o:128 * (o + 1),
                                                   bs:bs + 512], pp_o[:, :],
                           reads=[pp_o], writes=["f_ari"], tag="f_spw2")
        em.cc("AllReduce", ALU.add, PAIRS, [A["ari"].ap().opt()],
              [A["aro"].ap().opt()], reads=["f_ari"], writes=["f_aro"],
              tag="f_ar")
        b2c = em.sbuf("f_b2c", (128, 8), F32)
        em.dma("sync", b2c[:, :], A["b2"][l], writes=[b2c], tag="f_ldb2")
        arb = em.sbuf("f_arb", (128, 512), BF16)
        arf = em.sbuf("f_arf", (128, 512), F32)
        zz = em.sbuf("f_zz", (128, 512), F32)
        zb2 = em.sbuf("f_zb2", (128, 512), BF16)
        for c in range(8):
            c0 = 512 * c
            for i in range(8):
                em.dma("sync", arb[:, :],
                       A["aro"].ap()[128 * i:128 * (i + 1), c0:c0 + 512],
                       reads=["f_aro"], writes=[arb], tag="f_ldar")
                em.op("scalar", lambda s: s.activation(
                    out=arf[:, :], in_=arb[:, :], func=AF.Copy),
                    reads=[arb], writes=[arf], tag="f_cvar")
                em.dma("sync", zz[:, :],
                       ZS.ap()[128 * i:128 * (i + 1), c0:c0 + 512],
                       reads=["f_ZS"], writes=[zz], tag="f_ldzz")
                em.op("vector", lambda v: v.tensor_tensor(
                    out=zz[:, :], in0=zz[:, :], in1=arf[:, :], op=ALU.add),
                    reads=[zz, arf], writes=[zz], tag="f_zres")
                em.op("vector", lambda v, i=i: v.tensor_scalar(
                    out=zz[:, :], in0=zz[:, :], scalar1=b2c[:, i:i + 1],
                    scalar2=0.0, op0=ALU.add, op1=ALU.add),
                    reads=[zz, b2c], writes=[zz], tag="f_b2a")
                if l == 0:
                    em.dma("gpsimd", ZS.ap()[128 * i:128 * (i + 1),
                                             c0:c0 + 512], zz[:, :],
                           reads=[zz], writes=["f_ZS"], tag="f_spz2")
                    em.op("scalar", lambda s: s.activation(
                        out=zb2[:, :], in_=zz[:, :], func=AF.Copy),
                        reads=[zz], writes=[zb2], tag="f_cz2")
                    em.dma("gpsimd", A["UB2"].ap()[128 * i:128 * (i + 1),
                                                   c0:c0 + 512], zb2[:, :],
                           reads=[zb2], writes=["f_UB2"], tag="f_spub2")
                else:
                    em.dma("sync", arf[:, :],
                           xin_ap[128 * i:128 * (i + 1), c0:c0 + 512],
                           reads=[xin_key], writes=[arf], tag="f_ldxi")
                    em.op("vector", lambda v: v.scalar_tensor_tensor(
                        out=zz[:, :], in0=zz[:, :], scalar=0.5, in1=arf[:, :],
                        op0=ALU.mult, op1=ALU.add),
                        reads=[zz, arf], writes=[zz], tag="f_fin")
                    em.dma("gpsimd", xo_ap[128 * i:128 * (i + 1), c0:c0 + 512],
                           zz[:, :], reads=[zz], writes=[xo_key], tag="f_out")
        em.free(*([b2c, arb, arf, zz, zb2] + uc + hb + ag_ + [gg, pp_o]
                  + w1_s + w2_s + [b1_s, lng_s, lnb_s, stage2]
                  + wqkv_s + wo_s + [stage, qsr_s, ksr_s]))


def build_ffn2():
    nc = bass.Bass("TRN2", target_bir_lowering=False)
    em = Em(nc)
    gi = lambda n, s, d=F32: nc.dram_tensor(n, s, d, kind="ExternalInput").ap()
    xin = gi("xin", [D, T])
    ident_in = gi("ident", [128, 128], BF16)
    A = {"wqkv": [gi(f"f{l}_wqkv", [D, 768]) for l in range(2)],
         "wo": [gi(f"f{l}_wo", [512, D]) for l in range(2)],
         "qsr": [gi(f"f{l}_qsr", [1, 256]) for l in range(2)],
         "ksr": [gi(f"f{l}_ksr", [1, 256]) for l in range(2)],
         "rmat": gi("rmat", [128, 128], BF16),
         "rtab": gi("rtab", [4, 128, T], BF16),
         "amask": gi("amask", [128, 8 * 512], BF16),
         "w1": [gi(f"f{l}_w1", [D, 2816]) for l in range(2)],
         "w2": [gi(f"f{l}_w2", [1408, D]) for l in range(2)],
         "b1": [gi(f"f{l}_b1", [128, 22]) for l in range(2)],
         "b2": [gi(f"f{l}_b2", [128, 8]) for l in range(2)],
         "lng": [gi(f"f{l}_lng", [128, 8]) for l in range(2)],
         "lnb": [gi(f"f{l}_lnb", [128, 8]) for l in range(2)],
         "ZS": nc.dram_tensor("ZS", [D, T], F32),
         "UB": nc.dram_tensor("UB", [D, T], BF16),
         "UB2": nc.dram_tensor("UB2", [D, T], BF16),
         "agi": nc.dram_tensor("f_agi", [256, T], BF16),
         "ago": nc.dram_tensor("f_ago", [512, T], BF16),
         "ari": nc.dram_tensor("f_ari", [1024, T], BF16),
         "aro": nc.dram_tensor("f_aro", [1024, T], BF16)}
    xo = nc.dram_tensor("xoT", [D, T], F32, kind="ExternalOutput").ap()
    cx = Ctx(em, ident_in, nbanks=7)
    A["ptb"] = em.psum("ptb", (128, 1024), BF16)
    lns = ln_scratch(em)
    emit_ffn(em, cx, lns, A, ("xin",), xin, ("xo",), xo)
    em.finalize()
    em.close()
    return nc


def _pack_ffn2(wqkv, q_scale, k_scale, wo, ln_g, ln_b, w1, b1, w2, b2, p):
    out = {}
    h0 = 4 * p
    for l in range(2):
        cols = (list(range(64 * h0, 64 * h0 + 256))
                + list(range(512 + 64 * h0, 512 + 64 * h0 + 256))
                + list(range(1024 + 64 * h0, 1024 + 64 * h0 + 256)))
        out[f"f{l}_wqkv"] = np.ascontiguousarray(wqkv[l][:, cols])
        out[f"f{l}_wo"] = np.ascontiguousarray(wo[l])
        qsr = np.zeros((1, 256), np.float32)
        qsr[0, 0:64] = q_scale[l]; qsr[0, 128 + 64:128 + 128] = q_scale[l]
        ksr = np.zeros((1, 256), np.float32)
        ksr[0, 0:64] = k_scale[l]; ksr[0, 128 + 64:256] = k_scale[l]
        out[f"f{l}_qsr"] = qsr
        out[f"f{l}_ksr"] = ksr
        a_lo = 1365 * p
        asl = list(range(a_lo, a_lo + 1365))
        w1p = np.zeros((1024, 2816), np.float32)
        w1p[:, 0:1365] = w1[l][:, asl]
        w1p[:, 1408:1408 + 1365] = w1[l][:, [2730 + i for i in asl]]
        out[f"f{l}_w1"] = w1p
        b1p = np.zeros(2816, np.float32)
        b1p[0:1365] = b1[l][asl]
        b1p[1408:1408 + 1365] = b1[l][[2730 + i for i in asl]]
        out[f"f{l}_b1"] = np.ascontiguousarray(b1p.reshape(22, 128).T)
        w2p = np.zeros((1408, 1024), np.float32)
        w2p[0:1365, :] = w2[l][asl, :]
        out[f"f{l}_w2"] = w2p
        out[f"f{l}_b2"] = np.ascontiguousarray(b2[l].reshape(8, 128).T)
        out[f"f{l}_lng"] = np.ascontiguousarray(ln_g[l].reshape(8, 128).T)
        out[f"f{l}_lnb"] = np.ascontiguousarray(ln_b[l].reshape(8, 128).T)
    pk0 = _pack_ffn(0, wqkv, q_scale, k_scale, wo, ln_g, ln_b, w1, b1, w2, b2)
    out["rmat"] = pk0["rmat"]
    out["rtab"] = pk0["rtab"]
    out["amask"] = pk0["amask"]
    return out


def _ffn_stage2(X_T, wqkv, q_scale, k_scale, wo, ln_g, ln_b, w1, b1, w2, b2):
    nc = _get_prog("ffn2", build_ffn2)
    ims = []
    for c in range(NCORES):
        b, p = c // 2, c % 2
        pk = _pack_ffn2(wqkv, q_scale, k_scale, wo, ln_g, ln_b, w1, b1, w2,
                        b2, p)
        pk["xin"] = np.ascontiguousarray(X_T[b])
        pk["ident"] = _IDENT
        ims.append(pk)
    res = run_bass_kernel_spmd(nc, ims, core_ids=list(range(NCORES)))
    X2 = np.empty_like(X_T)
    for b in range(B):
        X2[b] = np.asarray(res.results[2 * b]["xoT"])
    return X2



_IDENT = np.eye(128, dtype=BF)


def _pack_rail(rail_maker, rail_up_proj, w_cproj):
    bd1 = np.zeros((32, 128, 128), np.float32)
    for h in range(4):
        for g in range(8):
            for r in range(8):
                bd1[h * 8 + g, 16 * r:16 * r + 16, 16 * r:16 * r + 16] = \
                    rail_maker[h, 8 * g + r]
    bd1p = np.ascontiguousarray(bd1.transpose(1, 0, 2).reshape(128, 32 * 128))
    bd2p = np.zeros((128, 128 * 128), np.float32)
    for h in range(4):
        for g in range(8):
            hg = h * 8 + g
            for m in range(4):
                kt = hg * 4 + m
                for r2 in range(2):
                    r = 8 * g + 2 * m + r2
                    bd2p[32 * m + 16 * r2:32 * m + 16 * r2 + 16,
                         kt * 128 + 64 * r2:kt * 128 + 64 * r2 + 64] = \
                        rail_up_proj[h, r].T
    wcT = np.zeros((16384, 1024), np.float32)
    wc = np.ascontiguousarray(w_cproj)
    for h in range(4):
        for g in range(8):
            for m in range(4):
                kt = (h * 8 + g) * 4 + m
                src0 = h * 4096 + (8 * g + 2 * m) * 64
                wcT[128 * kt:128 * kt + 128, :] = wc[:, src0:src0 + 128].T
    return bd1p, bd2p, wcT


_prog_cache = {}


def _get_prog(name, builder, *args):
    key = (name,) + args
    if key not in _prog_cache:
        _prog_cache[key] = builder(*args)
    return _prog_cache[key]


def _rail_stage(X_T, rail_maker, rail_up_proj, w_cproj, final):
    bd1p, bd2p, wcT = _pack_rail(rail_maker, rail_up_proj, w_cproj)
    nc = _get_prog("rail", build_rail, bool(final))
    in_maps = []
    for c in range(NCORES):
        b, th = c // 2, c % 2
        in_maps.append({
            "xT": np.ascontiguousarray(X_T[b][:, th * TH:(th + 1) * TH]),
            "ident": _IDENT, "bd1": bd1p, "bd2": bd2p, "wcT": wcT,
        })
    res = run_bass_kernel_spmd(nc, in_maps, core_ids=list(range(NCORES)))
    X_T2 = np.empty_like(X_T)
    Xout = np.empty((B, T, D), np.float32) if final else None
    for c in range(NCORES):
        b, th = c // 2, c % 2
        X_T2[b][:, th * TH:(th + 1) * TH] = np.asarray(res.results[c]["xoT"])
        if final:
            Xout[b][th * TH:(th + 1) * TH, :] = np.asarray(res.results[c]["xo"])
    return X_T2, Xout


def kernel(**inputs):
    gi = lambda n: np.asarray(inputs[n], np.float32)
    x = gi("L0_attn_hidden_states")
    X_T = np.ascontiguousarray(x.transpose(0, 2, 1))
    ffn_args = (gi("L0_ffn_wqkv"), gi("L0_ffn_q_scale"), gi("L0_ffn_k_scale"),
                gi("L0_ffn_wo"), gi("L0_ffn_ln_g"), gi("L0_ffn_ln_b"),
                gi("L0_ffn_w1"), gi("L0_ffn_b1"), gi("L0_ffn_w2"),
                gi("L0_ffn_b2"))
    mamba_args = (gi("L0_attn_norm_w"), gi("L0_attn_norm_b"),
                  gi("L0_attn_in_proj_w"), gi("L0_attn_conv_w"),
                  gi("L0_attn_conv_b"), gi("L0_attn_x_proj_w"),
                  gi("L0_attn_dt_proj_w"), gi("L0_attn_dt_proj_b"),
                  gi("L0_attn_A_log"), gi("L0_attn_D_skip"),
                  gi("L0_attn_out_proj_w"), gi("L0_attn_normf_w"),
                  gi("L0_attn_normf_b"))
    rail_args = (gi("L1_ffn_rail_maker"), gi("L1_ffn_rail_up_proj"),
                 gi("L1_ffn_w_cproj"))
    alibi_args = (gi("L1_attn_qkv_w"), gi("L1_attn_qkv_b"),
                  gi("L1_attn_out_w"), gi("L1_attn_out_b"))
    X_T = _ffn_stage2(X_T, *ffn_args)
    X_T = _mamba_stage2(X_T, *mamba_args)
    X_T = _ffn_stage2(X_T, *ffn_args)
    X_T = _rail_stage2(X_T, *rail_args)
    X_T = _alibi_stage2(X_T, *alibi_args)
    X_T = _rail_stage2(X_T, *rail_args)
    return np.ascontiguousarray(X_T.transpose(0, 2, 1))
